# version 13
# baseline (speedup 1.0000x reference)
import numpy as np
from contextlib import ExitStack

import concourse.bass as bass
import concourse.mybir as mybir
from concourse.bass import IndirectOffsetOnAxis
from concourse.bass_utils import run_bass_kernel_spmd
import ml_dtypes

F32 = mybir.dt.float32
BF16 = mybir.dt.bfloat16
U32 = mybir.dt.uint32
AX = mybir.AxisListType.X
ALU = mybir.AluOpType
ACT = mybir.ActivationFunctionType

FULL_CFG = dict(T=2048, D=1024, HF=2048, NH=16, E=8, NC=8)


class Prog:
    def __init__(self, nc):
        self.nc = nc
        self.items = {k: [] for k in ("sync", "tensor", "vector", "scalar", "gpsimd")}
        self.sems = {}
        self.cnt = {}

    def sem(self, name, es):
        self.sems[name] = es.enter_context(self.nc.semaphore(name))
        self.cnt[name] = 0

    def do(self, eng, fn, sem=None, n=1):
        if eng == "vector":
            hv = self.sems["vc"]
            prior = self.cnt["vc"]
            self.items[eng].append(lambda e, h=hv, v=prior: e.wait_ge(h, v))
            if sem is None or sem == "vc":
                sem, n = "vc", 1
            else:
                self.cnt["vc"] += 1
                self.cnt[sem] += n
                h2 = self.sems[sem]

                def wrapped2(e, fn=fn, h2=h2, n=n, hv=hv):
                    fn(e).then_inc(h2, n).then_inc(hv, 1)

                self.items[eng].append(wrapped2)
                return
        if sem is not None:
            self.cnt[sem] += n
            h = self.sems[sem]

            def wrapped(e, fn=fn, h=h, n=n):
                fn(e).then_inc(h, n)

            self.items[eng].append(wrapped)
        else:
            self.items[eng].append(fn)

    QSEM = {"sync": "qs", "scalar": "qa", "gpsimd": "qg"}

    def dma(self, eng, fn, sem=None, handshake=True):
        q = self.QSEM[eng]
        self.cnt[q] += 16
        hq = self.sems[q]
        qv = self.cnt[q]
        self.items[eng].append(lambda e, fn=fn, hq=hq: fn(e).then_inc(hq, 16))
        if handshake:
            self.qsync(eng, sem)

    def qsync(self, eng, sem):
        q = self.QSEM[eng]
        hq = self.sems[q]
        qv = self.cnt[q]
        self.cnt[sem] += 1
        hs = self.sems[sem]
        self.items[eng].append(lambda e, hq=hq, qv=qv, hs=hs: e.wait_ge(hq, qv).then_inc(hs, 1))

    def wait(self, eng, sem, val=None):
        h = self.sems[sem]
        v = self.cnt[sem] if val is None else val
        self.items[eng].append(lambda e, h=h, v=v: e.wait_ge(h, v))

    def v(self, sem):
        return self.cnt[sem]


def build_nc(cfg):
    T, D, HF, NH, E, NC = cfg["T"], cfg["D"], cfg["HF"], cfg["NH"], cfg["E"], cfg["NC"]
    DC = D // 128
    TT = T // 128
    N = NC * T
    CAP = T
    DH = D // NH
    HC = HF // 128
    A_ = N // 128
    NQ = 512 if T >= 512 else T
    QG = T // NQ
    L0SC = 1.0 / float(np.sqrt(D))
    L1SC = 1.0 / float(np.sqrt(DH))

    nc = bass.Bass("TRN2", target_bir_lowering=False)

    def din(name, shape, dt=F32):
        return nc.dram_tensor(name, list(shape), dt, kind="ExternalInput").ap()

    xfull = din("xfull", (N, D))
    xown = din("xown", (T, D))
    coff = din("coff", (128, 1))
    wq_d = din("wq", (D, D)); wk_d = din("wk", (D, D)); wv_d = din("wv", (D, D))
    bq_d = din("bq", (D,)); bk_d = din("bk", (D,)); bv_d = din("bv", (D,))
    rw_d = din("rw", (E, D)); rb_d = din("rb", (E, 1))
    wx_d = din("wx", (HF // 128, 128, D // 128, 128)); vx_d = din("vx", (HF // 128, 128, D // 128, 128))
    wox_d = din("wox", (HF, D))
    lnw_d = din("lnw", (1, D)); lnb_d = din("lnb", (1, D))
    wqkv_d = din("wqkv", (3 * NH // 2, 128, D // 128, 2 * D // NH)); bqkv_d = din("bqkv", (3 * D,))
    wproj_d = din("wproj", (D, D)); bproj_d = din("bproj", (1, D))
    wg_d = din("wg", (E, D)); weo_d = din("weo", (D, D)); be_d = din("be", (E, D))
    bvr_d = din("bvr", (1, D))
    out_d = nc.dram_tensor("out", [T, D], F32, kind="ExternalOutput").ap()

    rt_in = nc.dram_tensor("rt_in", [T, 16], F32)
    rt_all = nc.dram_tensor("rt_all", [N, 16], F32)
    y_in = nc.dram_tensor("y_in", [T, D], F32)
    y_all = nc.dram_tensor("y_all", [N, D], F32)
    wet_in = nc.dram_tensor("wet_in", [D, 2 * D], BF16)
    wet_all = nc.dram_tensor("wet_all", [E * D, 2 * D], BF16)
    Lb = nc.dram_tensor("Lb", [N + 256, 1], U32)
    rtw_d = nc.dram_tensor("rtw", [N, 4], F32)
    rtd_d = nc.dram_tensor("rtd", [N, 2], U32)
    ksp_h = nc.dram_tensor("ksp_h", [D, T], BF16)
    ksp_l = nc.dram_tensor("ksp_l", [D, T], BF16)
    vsp = nc.dram_tensor("vsp", [T, D], F32)
    x1sp = nc.dram_tensor("x1sp", [T, D], F32)
    ssp = nc.dram_tensor("ssp", [T, D], F32)
    msp = nc.dram_tensor("msp", [D, T], F32)
    bcb = nc.dram_tensor("bcb", [1, 4096], F32)

    idf = nc.inline_tensor(np.eye(128, dtype=np.float32), name="idf").ap()
    idb = nc.inline_tensor(np.eye(128).astype(ml_dtypes.bfloat16), name="idb").ap()
    utS = nc.inline_tensor(np.triu(np.ones((128, 128), np.float32), 1), name="utS").ap()
    on1 = nc.inline_tensor(np.ones((128, 1), np.float32), name="on1").ap()
    op2_np = np.zeros((128, 2), np.float32)
    op2_np[0:DH, 0] = 1.0
    op2_np[DH:2 * DH, 1] = 1.0
    op2 = nc.inline_tensor(op2_np, name="op2").ap()
    io8 = nc.inline_tensor(np.arange(E, dtype=np.float32).reshape(1, E), name="io8").ap()

    es = ExitStack()
    sb = lambda name, shape, dt=F32: es.enter_context(nc.sbuf_tensor(name, list(shape), dt))
    psf = lambda name: es.enter_context(nc.psum_tensor(name, [128, 512], F32))

    TCH = 512 if T >= 512 else T
    A = sb("A", (128, TT * D))
    B = sb("B", (128, max(TT * D, 100 * A_)))
    C = sb("C", (128, max(TT * NQ, HC * TCH, DC * DC * 128, DC * D, T + TT * 132 + (NH // 2) * NQ)))
    Dm = sb("Dm", (128, 4096 if D >= 1024 else 4 * D))
    Em = sb("Em", (128, 2048))
    Em2 = sb("Em2", (128, 2048))
    Mm = sb("Mm", (128, 960))
    RTS = sb("RTS", (128, 6 * A_))
    K2 = sb("K2", (128, 1664))
    K2u = K2[:].bitcast(U32)
    DS = K2u[:, 973:973 + 256]
    IDS = K2u[:, 685:685 + A_]
    IDT = K2u[:, 813:813 + 128]
    OWN = K2u[:, 941:941 + TT]
    LBT = K2u[:, 957:957 + TT]
    SMALL = K2[:, 1229:1229 + 256]

    PS = [psf(f"PS{i}") for i in range(8)]
    PTv = PS[7][:].bitcast(BF16)

    p = Prog(nc)
    for s in ("ld", "ld2", "st", "vld", "gdma", "ccw", "ccr", "ccy", "vc", "sc", "mm", "gc",
              "qs", "qa", "qg"):
        p.sem(s, es)

    def bc16(t, c0_f32, n_bf16):
        return t[:].bitcast(BF16)[:, 2 * c0_f32:2 * c0_f32 + n_bf16]

    def u32v(t, c0, n):
        return t[:].bitcast(U32)[:, c0:c0 + n]

    def split_hilo(src_ap, hi_ap, lo_ap):
        p.wait("scalar", "mm")
        p.wait("scalar", "vc")
        p.do("scalar", lambda e, s=src_ap, h=hi_ap: e.activation(h, s, ACT.Copy), sem="sc")
        p.wait("vector", "sc")
        p.do("vector", lambda e, s=src_ap, h=hi_ap, l=lo_ap: e.tensor_sub(l, s, h), sem="vc")

    def ld(dst, src, sem="ld", eng="sync", slow=False):
        p.dma(eng, lambda e, d=dst, s=src, sl=slow: e.dma_start(out=d, in_=s, allow_slow_non_contiguous=sl), sem=sem)

    hp2_ = 2 * (D // NH)
    K2b = K2[:].bitcast(BF16)
    idf_s = K2[:, 0:128]
    utS_s = K2[:, 128:256]
    idb_s = K2b[:, 512:640]
    on1_s = K2[:, 320:321]
    op2_s = K2[:, 321:323]
    bq_s = K2[:, 323:323 + DC]; bk_s = K2[:, 331:331 + DC]
    bqkv_s = K2[0:hp2_, 469:469 + 3 * D // hp2_]
    rb_s = K2[0:E, 339:340]
    coff_s = K2[:, 340:341]
    rws_st = K2[0:E, 341:469]
    rwT_h = K2b[:, 1114:1114 + DC * E]; rwT_l = K2b[:, 1178:1178 + DC * E]
    wgT_h = K2b[:, 1242:1242 + DC * E]; wgT_l = K2b[:, 1306:1306 + DC * E]
    rwT_f = K2[:, 493:493 + DC * E]

    ld(idf_s, idf)
    ld(utS_s, utS)
    ld(on1_s, on1)
    ld(op2_s, op2)
    ld(idb_s, idb, sem="gdma", eng="gpsimd")
    ld(bq_s[:], bq_d.rearrange("(a p) -> p a", p=128), slow=True)
    ld(bk_s[:], bk_d.rearrange("(a p) -> p a", p=128), slow=True)
    ld(bqkv_s[:], bqkv_d.rearrange("(a p) -> p a", p=hp2_), slow=True)
    ld(rb_s[:], rb_d)
    ld(coff_s, coff)
    io8_bc = SMALL[:, 32:32 + E]
    ld(io8_bc, io8.broadcast_to((128, E)))

    p.wait("tensor", "ld")
    for wsrc, dsth, dstl in ((rw_d, rwT_h, rwT_l), (wg_d, wgT_h, wgT_l)):
        for dc in range(DC):
            p.wait("sync", "mm")
            ld(rws_st, wsrc[:, dc * 128:(dc + 1) * 128])
            p.wait("tensor", "ld")
            p.do("tensor", lambda e, dc=dc: e.transpose(
                PS[5][:, dc * E:(dc + 1) * E], rws_st[0:E, :], idf_s[0:E, 0:E]),
                sem="mm" if True else None)
        p.wait("vector", "mm")
        p.do("vector", lambda e: e.tensor_copy(rwT_f[:, 0:DC * E], PS[5][:, 0:DC * E]), sem="vc")
        split_hilo(rwT_f[:, 0:DC * E], dsth[:, 0:DC * E], dstl[:, 0:DC * E])
        p.wait("tensor", "vc")

    p.do("gpsimd", lambda e: e.iota(IDS, pattern=[[1, A_]], base=0, channel_multiplier=A_,
                                    allow_small_or_imprecise_dtypes=True), sem="gc")
    p.do("gpsimd", lambda e: e.iota(IDT[0:A_, :], pattern=[[A_, 128]], base=0, channel_multiplier=1,
                                    allow_small_or_imprecise_dtypes=True), sem="gc")
    p.do("gpsimd", lambda e: e.iota(SMALL[:, 224:225], pattern=[[1, 1]], base=N, channel_multiplier=2,
                                    allow_small_or_imprecise_dtypes=True), sem="gc")
    p.do("gpsimd", lambda e: e.iota(SMALL[:, 225:226], pattern=[[1, 1]], base=N + 1, channel_multiplier=2,
                                    allow_small_or_imprecise_dtypes=True), sem="gc")
    p.do("gpsimd", lambda e: e.iota(SMALL[:, 200:200 + TT], pattern=[[128, TT]], base=0, channel_multiplier=1,
                                    allow_small_or_imprecise_dtypes=True), sem="gc")
    p.wait("vector", "gc")
    p.wait("vector", "ld")
    p.do("vector", lambda e: e.tensor_scalar(SMALL[:, 200:200 + TT], SMALL[:, 200:200 + TT], coff_s[:, 0:1], None, ALU.add), sem="vc")
    p.do("vector", lambda e: e.tensor_copy(OWN, SMALL[:, 200:200 + TT]), sem="vc")
    nlb = (N + 256) // 128
    p.do("vector", lambda e: e.memset(u32v(Mm, 800, nlb), 0), sem="vc")
    p.wait("sync", "vc")
    p.dma("sync", lambda e: e.dma_start(out=Lb.ap().rearrange("(p a) o -> p (a o)", p=128),
                                       in_=u32v(Mm, 800, nlb)), sem="st")

    p.wait("tensor", "gdma")
    for oc in range(DC):
        wof = Dm[:, 0:D]
        p.wait("sync", "vc")
        p.wait("sync", "mm")
        ld(wof, weo_d[oc * 128:(oc + 1) * 128, :])
        woh = bc16(Dm, D, D); wol = bc16(Dm, D + D // 2, D)
        p.wait("scalar", "ld")
        split_hilo(wof, woh, wol)
        for half, src in ((0, woh), (1, wol)):
            p.wait("tensor", "vc")
            for dc in range(DC):
                p.do("tensor", lambda e, s=src, dc=dc: e.transpose(
                    PTv[:, dc * 128:(dc + 1) * 128], s[:, dc * 128:(dc + 1) * 128], idb_s),
                    sem="mm" if dc == DC - 1 else None)
            p.wait("vector", "mm")
            dst = bc16(C, 0, DC * 2 * D).rearrange("p (a c) -> p a c", a=DC)[:, :, half * D + oc * 128:half * D + (oc + 1) * 128]
            p.do("vector", lambda e, d=dst: e.tensor_copy(
                d, PTv[:, 0:DC * 128].rearrange("p (a c) -> p a c", a=DC)), sem="vc")
    p.wait("scalar", "vc")
    p.dma("scalar", lambda e: e.dma_start(out=wet_in.ap().rearrange("(a p) c -> p a c", p=128),
                                          in_=bc16(C, 0, DC * 2 * D).rearrange("p (a c) -> p a c", a=DC)), sem="vld")
    p.wait("gpsimd", "vld")
    p.do("gpsimd", lambda e: e.collective_compute(
        "AllGather", ALU.bypass, replica_groups=[list(range(NC))],
        ins=[wet_in.ap()], outs=[wet_all.ap()]), sem="ccw")

    MU = Mm[:, 0:TT]; RS = Mm[:, TT:2 * TT]
    MU1 = Mm[:, 2 * TT:3 * TT]; RS1 = Mm[:, 3 * TT:4 * TT]
    MU2 = Mm[:, 4 * TT:5 * TT]; RS2 = Mm[:, 5 * TT:6 * TT]
    CW = Mm[:, 6 * TT:6 * TT + TT * E]
    negc0 = SMALL[:, 0:1]
    negcH = SMALL[:, 4:4 + 2 * ((NH + 1) // 2)]

    xnh = lambda dc: bc16(B, 0, 2 * DC * T)[:, dc * T:(dc + 1) * T]
    xnl = lambda dc: bc16(B, 0, 2 * DC * T)[:, DC * T + dc * T:DC * T + (dc + 1) * T]

    NCH = (D + 511) // 512
    def stats_pass(xt, MUc, RSc, j):
        p.wait("vector", "vld")
        bst = Em[:, 0:6 * NCH]; bag = Em[:, 8 * NCH:8 * NCH + 2]
        for ch in range(NCH):
            p.do("vector", lambda e, x=xt, ch=ch: e.bn_stats(
                Em[:, 6 * ch:6 * ch + 6], x[:, 512 * ch:min(512 * (ch + 1), D)]))
        p.do("vector", lambda e, b=bst, a=bag: e.bn_aggr(a, b))
        p.do("vector", lambda e, a=bag, j=j, M=MUc: e.tensor_copy(M[:, j:j + 1], a[:, 0:1]))
        p.do("vector", lambda e, a=bag: e.tensor_single_scalar(Em[:, 8 * NCH + 4:8 * NCH + 5], a[:, 1:2], 1e-5, ALU.add), sem="vc")
        p.wait("scalar", "vc")
        p.do("scalar", lambda e: e.activation(Em[:, 8 * NCH + 5:8 * NCH + 6], Em[:, 8 * NCH + 4:8 * NCH + 5], ACT.Sqrt), sem="sc")
        p.wait("vector", "sc")
        p.do("vector", lambda e, j=j, R=RSc: e.reciprocal(R[:, j:j + 1], Em[:, 8 * NCH + 5:8 * NCH + 6]), sem="vc")

    def split_transpose_to(dst_arena, j, src_f32):
        hi_t = bc16(Em2, 0, D); lo_t = bc16(Em2, D // 2, D)
        split_hilo(src_f32, hi_t, lo_t)
        for src, base in ((hi_t, 0), (lo_t, DC * T)):
            p.wait("tensor", "vc")
            for dc in range(DC):
                p.do("tensor", lambda e, s=src, dc=dc: e.transpose(
                    PTv[:, dc * 128:(dc + 1) * 128], s[:, dc * 128:(dc + 1) * 128], idb_s),
                    sem="mm" if dc == DC - 1 else None)
            p.wait("vector", "mm")
            dstap = bc16(dst_arena, 0, 2 * DC * T).rearrange("p (c t) -> p c t", c=2 * DC)[:, base // T:base // T + DC, j * 128:(j + 1) * 128]
            p.do("vector", lambda e, d=dstap: e.tensor_copy(d, PTv[:, 0:DC * 128].rearrange("p (c t) -> p c t", c=DC)), sem="vc")
        p.wait("tensor", "vc")

    p.wait("vector", "vld")
    for j in range(TT):
        xt = A[:, j * D:(j + 1) * D]
        ld(xt, xown[j * 128:(j + 1) * 128, :])
        p.wait("vector", "ld")
        stats_pass(xt, MU, RS, j)
        xnt = Em[:, 1024:1024 + D] if D <= 1024 else Em[:, 0:D]
        p.do("vector", lambda e, x=xt, o=xnt, j=j: e.tensor_scalar(o, x, MU[:, j:j + 1], RS[:, j:j + 1], ALU.subtract, ALU.mult), sem="vc")
        split_transpose_to(B, j, xnt)

    LgF = Em2[0:E, 0:T]
    for ts in range(T // NQ):
        p.wait("tensor", "vc")
        for pp in range(3):
            for dc in range(DC):
                lh = (rwT_h if pp < 2 else rwT_l)[:, dc * E:(dc + 1) * E]
                rh = (xnh(dc) if pp != 1 else xnl(dc))[:, ts * NQ:(ts + 1) * NQ]
                first = (pp == 0 and dc == 0); last = (pp == 2 and dc == DC - 1)
                p.do("tensor", lambda e, l=lh, r=rh, f=first, s=last: e.matmul(
                    PS[6][0:E, 0:NQ], l, r, start=f, stop=s), sem="mm" if last else None)
        p.wait("vector", "mm")
        p.do("vector", lambda e, ts=ts: e.tensor_scalar(
            LgF[:, ts * NQ:(ts + 1) * NQ], PS[6][0:E, 0:NQ], rb_s[0:E, 0:1], None, ALU.add), sem="vc")
    p.wait("tensor", "vc")
    for j in range(TT):
        p.wait("tensor", "vc")
        p.do("tensor", lambda e, j=j: e.transpose(PS[5][:, 0:E], LgF[:, j * 128:(j + 1) * 128], idf_s[0:E, 0:E]), sem="mm")
        p.wait("vector", "mm")
        p.wait("vector", "vld", p.v("vld"))
        stg0 = Em[:, 0:16]
        p.do("vector", lambda e, s=stg0: e.tensor_copy(s[:, 0:E], PS[5][:, 0:E]))
        p.do("vector", lambda e, j=j, s=stg0: e.tensor_copy(s[:, E:E + 1], MU[:, j:j + 1]))
        p.do("vector", lambda e, j=j, s=stg0: e.tensor_copy(s[:, E + 1:E + 2], RS[:, j:j + 1]))
        p.do("vector", lambda e, s=stg0: e.memset(s[:, E + 2:16], 0.0), sem="vc")
        p.wait("scalar", "vc")
        p.dma("scalar", lambda e, j=j, s=stg0: e.dma_start(out=rt_in.ap()[j * 128:(j + 1) * 128, :], in_=s), sem="vld")
    p.wait("gpsimd", "vld")
    p.do("gpsimd", lambda e: e.collective_compute(
        "AllGather", ALU.bypass, replica_groups=[list(range(NC))],
        ins=[rt_in.ap()], outs=[rt_all.ap()]), sem="ccr")

    def load_w_full(wd):
        for dc in range(DC):
            blk = Dm[:, 0:D]
            p.wait("sync", "vc")
            ld(blk, wd[dc * 128:(dc + 1) * 128, :])
            p.wait("scalar", "ld")
            hi = C[:].bitcast(BF16)[:, dc * D:(dc + 1) * D]
            lo = C[:].bitcast(BF16)[:, DC * D + dc * D:DC * D + (dc + 1) * D]
            split_hilo(blk, hi, lo)
        p.wait("tensor", "vc")

    def wch(hi_side, oc, dc):
        base = (0 if hi_side else DC * D) + dc * D + oc * 128
        return C[:].bitcast(BF16)[:, base:base + 128]


    for wi, wd in enumerate((wq_d, wk_d, wv_d)):
        load_w_full(wd)
        if wi < 2:
            for ts in range(T // NQ):
                for oc in range(DC):
                    p.wait("tensor", "vc")
                    p.wait("vector", "vld", p.v("vld"))
                    pso = PS[0] if oc % 2 == 0 else PS[1]
                    for pp in range(3):
                        for dc in range(DC):
                            lh = wch(pp < 2, oc, dc)
                            rh = (xnh(dc) if pp != 1 else xnl(dc))[:, ts * NQ:(ts + 1) * NQ]
                            first = (pp == 0 and dc == 0); last = (pp == 2 and dc == DC - 1)
                            p.do("tensor", lambda e, o=pso, l=lh, r=rh, f=first, s=last: e.matmul(
                                o[:, 0:NQ], l, r, start=f, stop=s), sem="mm" if last else None)
                    p.wait("vector", "mm")
                    q32 = Em[:, 0:NQ]
                    bias = bq_s if wi == 0 else bk_s
                    p.do("vector", lambda e, o=q32, i=pso, b=bias, oc=oc: e.tensor_scalar(
                        o, i[:, 0:NQ], b[:, oc:oc + 1], None, ALU.add), sem="vc")
                    hi_t = bc16(Em, NQ, NQ); lo_t = bc16(Em, NQ + NQ // 2, NQ)
                    split_hilo(q32, hi_t, lo_t)
                    if wi == 0:
                        dh_ = bc16(A, 0, 2 * DC * T)[:, oc * T + ts * NQ:oc * T + (ts + 1) * NQ]
                        dl_ = bc16(A, 0, 2 * DC * T)[:, DC * T + oc * T + ts * NQ:DC * T + oc * T + (ts + 1) * NQ]
                        p.do("vector", lambda e, d=dh_, s=hi_t: e.tensor_copy(d, s))
                        p.do("vector", lambda e, d=dl_, s=lo_t: e.tensor_copy(d, s), sem="vc")
                    else:
                        p.wait("scalar", "vc")
                        p.dma("scalar", lambda e, s=hi_t, oc=oc, ts=ts: e.dma_start(
                            out=ksp_h.ap()[oc * 128:(oc + 1) * 128, ts * NQ:(ts + 1) * NQ], in_=s), sem="vld")
                        p.wait("scalar", "vc")
                        p.dma("scalar", lambda e, s=lo_t, oc=oc, ts=ts: e.dma_start(
                            out=ksp_l.ap()[oc * 128:(oc + 1) * 128, ts * NQ:(ts + 1) * NQ], in_=s), sem="vld")
                    sq = Em[:, 1536:1536 + NQ] if NQ <= 512 else Em[:, NQ:2 * NQ]
                    p.do("scalar", lambda e, s=sq, i=q32: e.activation(s, i, ACT.Square), sem="sc")
                    p.wait("tensor", "sc")
                    p.do("tensor", lambda e, s=sq, f=(oc == 0), l=(oc == DC - 1): e.matmul(
                        PS[6][0:1, 0:NQ], on1_s, s, start=f, stop=l), sem="mm" if oc == DC - 1 else None)
                p.wait("vector", "mm")
                ncol = 16 if wi == 0 else 17
                p.do("vector", lambda e, c=ncol: e.reduce_max(SMALL[0:1, 64 + c:65 + c], PS[6][0:1, 0:NQ], axis=AX))
                if ts == 0:
                    p.do("vector", lambda e, c=ncol: e.tensor_copy(SMALL[0:1, c:c + 1], SMALL[0:1, 64 + c:65 + c]), sem="vc")
                else:
                    p.do("vector", lambda e, c=ncol: e.tensor_tensor(SMALL[0:1, c:c + 1], SMALL[0:1, c:c + 1], SMALL[0:1, 64 + c:65 + c], ALU.max), sem="vc")
        else:
            bv_bc = Em2[:, D:2 * D] if D <= 1024 else Em2[:, 0:D]
            ld(bv_bc, bvr_d.broadcast_to((128, D)), sem="ld2")
            p.wait("vector", "ld2")
            for j in range(TT):
                for half in range(D // NQ):
                    p.wait("tensor", "vc")
                    p.wait("vector", "vld", p.v("vld"))
                    pso = PS[2] if half % 2 == 0 else PS[3]
                    for ri in range(NQ // 128):
                        oc2 = half * (NQ // 128) + ri
                        for pp in range(3):
                            for dc in range(DC):
                                lh = (xnh(dc) if pp != 2 else xnl(dc))[:, j * 128:(j + 1) * 128]
                                rh = wch(pp != 1, oc2, dc)
                                first = (pp == 0 and dc == 0); last = (pp == 2 and dc == DC - 1)
                                p.do("tensor", lambda e, o=pso, l=lh, r=rh, f=first, s=last, ri=ri: e.matmul(
                                    o[:, ri * 128:(ri + 1) * 128], l, r, start=f, stop=s),
                                    sem="mm" if last and ri == NQ // 128 - 1 else None)
                    p.wait("vector", "mm")
                    vv = Em[:, 0:NQ]
                    p.do("vector", lambda e, o=vv, i=pso, h=half, b=bv_bc: e.tensor_add(
                        o, i[:, 0:NQ], b[:, h * NQ:(h + 1) * NQ]), sem="vc")
                    p.wait("scalar", "vc")
                    p.dma("scalar", lambda e, s=vv, j=j, h=half: e.dma_start(
                        out=vsp.ap()[j * 128:(j + 1) * 128, h * NQ:(h + 1) * NQ], in_=s), sem="vld")

    p.wait("scalar", "vc")
    p.do("scalar", lambda e: e.activation(SMALL[0:1, 18:19], SMALL[0:1, 16:17], ACT.Sqrt))
    p.do("scalar", lambda e: e.activation(SMALL[0:1, 19:20], SMALL[0:1, 17:18], ACT.Sqrt), sem="sc")
    p.wait("vector", "sc")
    p.do("vector", lambda e: e.tensor_tensor(SMALL[0:1, 20:21], SMALL[0:1, 18:19], SMALL[0:1, 19:20], ALU.mult))
    p.do("vector", lambda e: e.tensor_single_scalar(SMALL[0:1, 21:22], SMALL[0:1, 20:21], -L0SC, ALU.mult), sem="vc")
    p.wait("scalar", "vc")
    p.dma("scalar", lambda e: e.dma_start(out=bcb.ap()[0:1, 0:1], in_=SMALL[0:1, 21:22]), sem="vld")
    p.wait("sync", "vld")
    ld(negc0, bcb.ap()[0:1, 0:1].broadcast_to((128, 1)), sem="ld2")

    p.wait("sync", "ccr")
    p.wait("sync", "mm")
    RAW = B[:, 0:16 * A_]
    ld(RAW, rt_all.ap().rearrange("(p a) c -> p (a c)", p=128), sem="ld2")
    RAW3 = RAW.rearrange("p (a c) -> p a c", c=16)
    LG = RAW3[:, :, 0:E]
    co = [16 * A_]

    def arr(n):
        r = B[:, co[0]:co[0] + n]
        co[0] += n
        return r

    V0 = arr(A_); V1 = arr(A_)
    OH0 = arr(A_ * E); OH1 = arr(A_ * E); LM = arr(A_ * E)
    HTEA = arr(A_ * E); PRF = arr(A_ * E)
    BS = arr(E)
    P0s = arr(A_); P1s = arr(A_); E0v = arr(A_); E1v = arr(A_)
    W0 = arr(A_); W1 = arr(A_); VA0 = arr(A_); VA1 = arr(A_)
    DC0 = arr(A_); DC1 = arr(A_); DSS0 = arr(A_); DSS1 = arr(A_)
    T1 = arr(A_); T2 = arr(A_ * E)

    d3 = lambda t: t.rearrange("p (a c) -> p a c", c=E)
    dea = lambda t: t.rearrange("p (c a) -> p c a", c=E)
    c1 = lambda t: t.rearrange("p (a o) -> p a o", o=1)
    V = "vector"

    p.wait(V, "ld2")
    p.do(V, lambda e: e.reduce_max(c1(V0), LG, axis=AX))
    p.do(V, lambda e: e.tensor_tensor(d3(OH0), LG, c1(V0).broadcast_to((128, A_, E)), ALU.is_equal))
    p.do(V, lambda e: e.scalar_tensor_tensor(d3(LM), d3(OH0), -1e30, LG, ALU.mult, ALU.add))
    p.do(V, lambda e: e.reduce_max(c1(V1), d3(LM), axis=AX))
    p.do(V, lambda e: e.tensor_tensor(d3(OH1), d3(LM), c1(V1).broadcast_to((128, A_, E)), ALU.is_equal))
    p.do(V, lambda e: e.tensor_tensor(dea(HTEA).rearrange("p c a -> p a c"), d3(OH0), d3(OH1), ALU.add))
    for c_ in range(E):
        p.do(V, lambda e, c_=c_: e.tensor_tensor_scan(
            dea(PRF)[:, c_, :], dea(HTEA)[:, c_, :], dea(HTEA)[:, c_, :], 0.0, ALU.add, ALU.bypass),
            sem="vc" if c_ == E - 1 else None)
    p.wait("tensor", "vc")
    p.do("tensor", lambda e: e.matmul(PS[6][:, 0:E], utS_s, dea(PRF)[:, :, A_ - 1], start=True, stop=True), sem="mm")
    p.wait(V, "mm")
    p.do(V, lambda e: e.tensor_copy(BS, PS[6][:, 0:E]))
    p.do(V, lambda e: e.tensor_tensor(dea(PRF), dea(PRF),
                                      BS.rearrange("p (c o) -> p c o", o=1).broadcast_to((128, E, A_)), ALU.add))
    for OH, Ps_, Ev_ in ((OH0, P0s, E0v), (OH1, P1s, E1v)):
        p.do(V, lambda e, OH=OH: e.tensor_tensor(d3(T2), d3(OH), dea(PRF).rearrange("p c a -> p a c"), ALU.mult))
        p.do(V, lambda e, Ps_=Ps_: e.reduce_sum(c1(Ps_), d3(T2), axis=AX))
        p.do(V, lambda e, OH=OH: e.tensor_tensor(d3(T2), d3(OH),
             io8_bc.rearrange("p (o c) -> p o c", o=1).broadcast_to((128, A_, E)), ALU.mult))
        p.do(V, lambda e, Ev_=Ev_: e.reduce_sum(c1(Ev_), d3(T2), axis=AX))
    p.do(V, lambda e: e.tensor_tensor(T1, V0, V1, ALU.subtract), sem="vc")
    p.wait("scalar", "vc")
    p.do("scalar", lambda e: e.activation(W0, T1, ACT.Sigmoid), sem="sc")
    p.wait(V, "sc")
    p.do(V, lambda e: e.tensor_single_scalar(W1, W0, 1.0, ALU.subtract))
    p.do(V, lambda e: e.tensor_single_scalar(W1, W1, -1.0, ALU.mult))
    for si_, (Ps_, Ev_, Va, Wv_, Dc, Ds) in enumerate(((P0s, E0v, VA0, W0, DC0, DSS0), (P1s, E1v, VA1, W1, DC1, DSS1))):
        p.do(V, lambda e, Ps_=Ps_, Va=Va: e.tensor_single_scalar(Va, Ps_, float(CAP), ALU.is_le))
        p.do(V, lambda e, Wv_=Wv_, Va=Va: e.tensor_tensor(Wv_, Wv_, Va, ALU.mult))
        p.do(V, lambda e, Ev_=Ev_, Ps_=Ps_, Dc=Dc: e.scalar_tensor_tensor(Dc, Ev_, float(CAP), Ps_, ALU.mult, ALU.add))
        p.do(V, lambda e, Dc=Dc: e.tensor_single_scalar(Dc, Dc, 1.0, ALU.subtract))
        trc = SMALL[:, 224 + si_:225 + si_]
        p.do(V, lambda e, Ds=Ds, Dc=Dc, Va=Va: e.tensor_tensor(Ds, Dc, Va, ALU.mult))
        p.do(V, lambda e, Va=Va, T1=T1: e.tensor_scalar(T1, Va, -1.0, -1.0, ALU.mult, ALU.subtract))
        p.do(V, lambda e, T1=T1, trc=trc: e.tensor_scalar(T1, T1, trc[:, 0:1], None, ALU.mult))
        p.do(V, lambda e, Ds=Ds, T1=T1: e.tensor_tensor(Ds, Ds, T1, ALU.add))
        p.do(V, lambda e, Dc=Dc: e.tensor_single_scalar(Dc, Dc, float(N - 1), ALU.min), sem="vc")
    RTW3 = RTS[:, 0:4 * A_].rearrange("p (a c) -> p a c", c=4)
    p.do(V, lambda e: e.tensor_copy(RTW3[:, :, 0:1], c1(W0)))
    p.do(V, lambda e: e.tensor_copy(RTW3[:, :, 1:2], c1(W1)))
    p.do(V, lambda e: e.memset(RTW3[:, :, 2:4], 0.0), sem="vc")
    RTD3 = u32v(RTS, 4 * A_, 2 * A_).rearrange("p (a c) -> p a c", c=2)
    p.do(V, lambda e: e.tensor_copy(RTD3[:, :, 0:1], c1(DC0)))
    p.do(V, lambda e: e.tensor_copy(RTD3[:, :, 1:2], c1(DC1)), sem="vc")
    p.wait("scalar", "vc")
    p.dma("scalar", lambda e: e.dma_start(out=rtw_d.ap().rearrange("(p a) c -> p a c", p=128), in_=RTW3), sem="vld")
    p.wait("scalar", "vc")
    p.dma("scalar", lambda e: e.dma_start(out=rtd_d.ap().rearrange("(p a) c -> p a c", p=128), in_=RTD3), sem="vld")
    p.wait("tensor", "vc")
    for si, Ds in enumerate((DSS0, DSS1)):
        p.wait("tensor", "vc")
        p.do("tensor", lambda e, Ds=Ds: e.transpose(PS[4][0:A_, 0:128], Ds, idf_s), sem="mm")
        p.wait(V, "mm")
        p.do(V, lambda e: e.tensor_copy(Em[0:A_, 0:128], PS[4][0:A_, 0:128]), sem="vc")
        p.do(V, lambda e, si=si: e.tensor_copy(DS[0:A_, si * 128:(si + 1) * 128], Em[0:A_, 0:128]), sem="vc")
    p.wait("sync", "vld")
    p.wait("sync", "mm")
    p.wait("sync", "vc")
    ld(bc16(B, 0, DC * T).rearrange("p (a c) -> p a c", a=DC),
       ksp_h.ap().rearrange("(a p) c -> p a c", p=128), sem="ld2")
    ld(bc16(B, DC * T // 2, DC * T).rearrange("p (a c) -> p a c", a=DC),
       ksp_l.ap().rearrange("(a p) c -> p a c", p=128), sem="ld2")
    kpre = p.v("ld2")
    p.wait("gpsimd", "vc")
    p.wait("gpsimd", "st")
    p.wait("gpsimd", "gc")
    p.wait("gpsimd", "ld2", kpre)
    for si in range(2):
        for g in range(128):
            idxap = DS[0:A_, si * 128 + g:si * 128 + g + 1]
            p.dma("gpsimd", lambda e, i=idxap, g=g: e.indirect_dma_start(
                out=Lb.ap(), out_offset=IndirectOffsetOnAxis(ap=i, axis=0),
                in_=IDT[0:A_, g:g + 1], in_offset=None), sem="gdma", handshake=False)
            if (si * 128 + g) % 16 == 15:
                p.qsync("gpsimd", "gdma")

    KTQ = TT // QG
    nqt = NQ // 128
    p.wait("tensor", "ld2", kpre)
    svld = []
    for qg in range(QG):
        ktmax = (qg + 1) * KTQ
        ksc = {}
        for kt in range(ktmax):
            kh = bc16(B, 0, DC * T); kl = bc16(B, DC * T // 2, DC * T)
            pso = PS[0] if kt % 2 == 0 else PS[1]
            if kt >= 2:
                p.wait("tensor", "sc", ksc[kt - 2])
            for pp in range(3):
                for dc in range(DC):
                    lh = (kh if pp < 2 else kl)[:, dc * T + kt * 128:dc * T + kt * 128 + 128]
                    qh_ = bc16(A, 0, 2 * DC * T)[:, dc * T + qg * NQ:dc * T + (qg + 1) * NQ]
                    ql_ = bc16(A, 0, 2 * DC * T)[:, DC * T + dc * T + qg * NQ:DC * T + dc * T + (qg + 1) * NQ]
                    rh = qh_ if pp != 1 else ql_
                    first = (pp == 0 and dc == 0); last = (pp == 2 and dc == DC - 1)
                    p.do("tensor", lambda e, o=pso, l=lh, r=rh, f=first, s=last: e.matmul(
                        o[:, 0:NQ], l, r, start=f, stop=s), sem="mm" if last else None)
            p.wait("scalar", "mm")
            Edst = C[:, kt * NQ:(kt + 1) * NQ]
            p.do("scalar", lambda e, d=Edst, i=pso: e.activation(
                d, i[:, 0:NQ], ACT.Exp, bias=negc0[:, 0:1], scale=L0SC), sem="sc")
            ksc[kt] = p.v("sc")
            if (kt + 1) * 128 > qg * NQ:
                p.wait("gpsimd", "sc")
                base = qg * NQ - kt * 128
                p.do("gpsimd", lambda e, d=Edst, b=base: e.affine_select(
                    d, d, pattern=[[1, NQ]], compare_op=ALU.is_ge, fill=0.0, base=b, channel_multiplier=-1), sem="gc")
        p.wait("tensor", "sc")
        p.wait("tensor", "gc")
        for dh2 in range(D // NQ):
            vmm = {}
            vld2 = {}

            def v_load(kt, dh2=dh2, qg=qg, vmm=vmm, vld2=vld2):
                s = kt % 2
                if kt >= 2:
                    p.wait("sync", "mm", vmm[kt - 2])
                else:
                    p.wait("sync", "mm")
                ld(Em2[:, s * NQ:s * NQ + NQ], vsp.ap()[kt * 128:(kt + 1) * 128, dh2 * NQ:(dh2 + 1) * NQ], sem="ld2")
                vld2[kt] = p.v("ld2")

            v_load(0)
            if ktmax > 1:
                v_load(1)
            for kt in range(ktmax):
                vv = Em2[:, (kt % 2) * NQ:(kt % 2) * NQ + NQ]
                p.wait("tensor", "ld2", vld2[kt])
                for qt in range(nqt):
                    ktlim = qg * KTQ + ((qt * 128 + 127) // 128) + 1
                    ktlim = min(ktmax, qg * KTQ + qt + 1) if NQ == 512 else ktmax
                    if kt >= ktlim:
                        continue
                    p.do("tensor", lambda e, o=PS[2 + qt], l=C[:, kt * NQ + qt * 128:kt * NQ + qt * 128 + 128],
                         r=vv, f=(kt == 0), s=(kt == ktlim - 1): e.matmul(
                        o[:, 0:NQ], l, r, start=f, stop=s), sem="mm" if qt == nqt - 1 else None)
                vmm[kt] = p.v("mm")
                if kt + 2 < ktmax:
                    v_load(kt + 2)
            if dh2 == 0:
                for qt in range(nqt):
                    ktlim = min(ktmax, qg * KTQ + qt + 1) if NQ == 512 else ktmax
                    for kt in range(ktlim):
                        p.do("tensor", lambda e, l=C[:, kt * NQ + qt * 128:kt * NQ + qt * 128 + 128],
                             qt=qt, f=(kt == 0), s=(kt == ktlim - 1): e.matmul(
                            PS[6][:, qt:qt + 1], l, on1_s, start=f, stop=s, skip_group_check=True),
                            sem="mm" if kt == ktlim - 1 else None)
                p.wait("vector", "mm")
                p.do("vector", lambda e: e.reciprocal(SMALL[:, 64:64 + nqt], PS[6][:, 0:nqt]), sem="vc")
            p.wait("vector", "mm")
            for qt in range(nqt):
                tq = qg * nqt + qt
                sst = Em[:, 1024 + (qt % 2) * NQ:1024 + (qt % 2) * NQ + NQ]
                if len(svld) >= 2:
                    p.wait("vector", "vld", svld[-2])
                p.do("vector", lambda e, o=sst, i=PS[2 + qt], qt=qt:
                     e.tensor_scalar(o, i[:, 0:NQ], SMALL[:, 64 + qt:64 + qt + 1], None, ALU.mult), sem="vc")
                p.wait("scalar", "vc")
                p.dma("scalar", lambda e, s2=sst, tq=tq, dh2=dh2: e.dma_start(
                    out=ssp.ap()[tq * 128:(tq + 1) * 128, dh2 * NQ:(dh2 + 1) * NQ], in_=s2), sem="vld")
                svld.append(p.v("vld"))
            p.wait("tensor", "vc")

    p.qsync("gpsimd", "gdma")
    for j in range(TT):
        p.dma("gpsimd", lambda e, j=j: e.indirect_dma_start(
            out=LBT[:, j:j + 1], out_offset=None, in_=Lb.ap(),
            in_offset=IndirectOffsetOnAxis(ap=OWN[:, j:j + 1], axis=0)), sem="gdma")
    p.wait("gpsimd", "gdma")
    for j in range(TT):
        xg = Em[:, 0:D] if D <= 1024 else Em[:, 0:D]
        stg = Em[:, 1536:1552]
        p.wait("gpsimd", "vc", p.v("vc"))
        p.dma("gpsimd", lambda e, x=xg, j=j: e.indirect_dma_start(
            out=x, out_offset=None, in_=xfull,
            in_offset=IndirectOffsetOnAxis(ap=LBT[:, j:j + 1], axis=0)), sem="gdma")
        p.dma("gpsimd", lambda e, s=stg, j=j: e.indirect_dma_start(
            out=s, out_offset=None, in_=rt_all.ap(),
            in_offset=IndirectOffsetOnAxis(ap=LBT[:, j:j + 1], axis=0)), sem="gdma")
        p.wait("vector", "gdma")
        xe = Em[:, 512:512 + D] if D <= 1024 else Em[:, D:2 * D]
        p.do("vector", lambda e, x=xg, o=xe, s=stg: e.tensor_scalar(
            o, x, s[:, E:E + 1], s[:, E + 1:E + 2], ALU.subtract, ALU.mult), sem="vc")
        split_transpose_to(A, j, xe)
    xeh = lambda dc: bc16(A, 0, 2 * DC * T)[:, dc * T:(dc + 1) * T]
    xel = lambda dc: bc16(A, 0, 2 * DC * T)[:, DC * T + dc * T:DC * T + (dc + 1) * T]

    UH = HC * TCH
    for tc in range(T // TCH):
        units = [(hc, mi) for hc in range(HC) for mi in range(2)]
        mm_hist = {}
        vc_hist = {}

        uld = {}

        def u_load(u, units=units, mm_hist=mm_hist, vc_hist=vc_hist, tc=tc, uld=uld):
            hc, mi = units[u]
            s = u % 2
            wd = wx_d if mi == 0 else vx_d
            if u >= 2:
                p.wait("sync", "mm", mm_hist[u - 2])
                p.wait("sync", "vc", vc_hist[u - 2])
            else:
                p.wait("sync", "mm")
                p.wait("sync", "vc")
            blk = Dm[:, s * 2 * D:s * 2 * D + D]
            ld(blk, wd[hc, :, :, :].rearrange("p a c -> p (a c)"))
            uld[u] = p.v("ld")

        u_load(0)
        u_load(1)
        for u, (hc, mi) in enumerate(units):
            s = u % 2
            blk = Dm[:, s * 2 * D:s * 2 * D + D]
            hi = bc16(Dm, s * 2 * D + D, D); lo = bc16(Dm, s * 2 * D + D + D // 2, D)
            p.wait("scalar", "ld", uld[u])
            if u >= 2:
                p.wait("scalar", "mm", mm_hist[u - 2])
            split_hilo(blk, hi, lo)
            vc_hist[u] = p.v("vc")
            pso = PS[4 + (u % 4)]
            p.wait("tensor", "vc")
            for pp in range(3):
                for dc in range(DC):
                    lh = (hi if pp < 2 else lo).rearrange("p (a c) -> p a c", a=DC)[:, dc, :]
                    rh = (xeh(dc) if pp != 1 else xel(dc))[:, tc * TCH:(tc + 1) * TCH]
                    first = (pp == 0 and dc == 0); last = (pp == 2 and dc == DC - 1)
                    p.do("tensor", lambda e, o=pso, l=lh, r=rh, f=first, s=last: e.matmul(
                        o[:, 0:TCH], l, r, start=f, stop=s), sem="mm" if last else None)
            mm_hist[u] = p.v("mm")
            if u + 2 < len(units):
                u_load(u + 2)
            if mi == 1:
                pa = PS[4 + ((u - 1) % 4)]; pb = PS[4 + (u % 4)]
                p.wait("scalar", "mm")
                if u >= 4:
                    p.wait("scalar", "vc", vc_hist[u - 4])
                sl = Em[:, (hc % 2) * TCH:(hc % 2) * TCH + TCH]
                p.do("scalar", lambda e, o=sl, i=pb: e.activation(o, i[:, 0:TCH], ACT.Silu), sem="sc")
                p.wait("vector", "sc")
                uu = Em[:, 1024 + (hc % 2) * TCH:1024 + (hc % 2) * TCH + TCH] if TCH <= 512 else Em[:, TCH:2 * TCH]
                p.do("vector", lambda e, o=uu, s2=sl, i=pa: e.tensor_tensor(o, i[:, 0:TCH], s2, ALU.mult), sem="vc")
                split_hilo(uu, bc16(C, 0, 2 * UH)[:, hc * TCH:(hc + 1) * TCH],
                           bc16(C, 0, 2 * UH)[:, UH + hc * TCH:UH + (hc + 1) * TCH])
                vc_hist[u] = p.v("vc")
        p.wait("tensor", "vc")
        ymm = {}
        yld = {}

        def y_load(hc, tc=tc, ymm=ymm, yld=yld):
            s = hc % 2
            if hc >= 2:
                p.wait("sync", "mm", ymm[hc - 2])
            else:
                p.wait("sync", "mm")
                p.wait("sync", "vc")
            blk = Dm[:, s * 2 * D:s * 2 * D + D]
            ld(blk, wox_d[hc * 128:(hc + 1) * 128, :])
            yld[hc] = p.v("ld")

        y_load(0)
        y_load(1)
        for hc in range(HC):
            s = hc % 2
            blk = Dm[:, s * 2 * D:s * 2 * D + D]
            hi = bc16(Dm, s * 2 * D + D, D); lo = bc16(Dm, s * 2 * D + D + D // 2, D)
            p.wait("scalar", "ld", yld[hc])
            if hc >= 2:
                p.wait("scalar", "mm", ymm[hc - 2])
            split_hilo(blk, hi, lo)
            p.wait("tensor", "vc")
            for ts4 in range(TCH // 128):
                uh_ = bc16(C, 0, 2 * UH)[:, hc * TCH + ts4 * 128:hc * TCH + ts4 * 128 + 128]
                ul_ = bc16(C, 0, 2 * UH)[:, UH + hc * TCH + ts4 * 128:UH + hc * TCH + ts4 * 128 + 128]
                for dh2 in range(D // NQ):
                    pso = PS[ts4 * (D // NQ) + dh2]
                    for pp in range(3):
                        lh = uh_ if pp < 2 else ul_
                        rh = (hi if pp != 1 else lo)[:, dh2 * NQ:(dh2 + 1) * NQ]
                        first = (pp == 0 and hc == 0); last = (pp == 2 and hc == HC - 1)
                        p.do("tensor", lambda e, o=pso, l=lh, r=rh, f=first, s=last: e.matmul(
                            o[:, 0:NQ], l, r, start=f, stop=s),
                            sem="mm" if pp == 2 else None)
            ymm[hc] = p.v("mm")
            if hc + 2 < HC:
                y_load(hc + 2)
        p.wait("vector", "mm")
        for ts4 in range(TCH // 128):
            for dh2 in range(D // NQ):
                yv = Em[:, 0:NQ]
                p.do("vector", lambda e, o=yv, i=PS[ts4 * (D // NQ) + dh2]: e.tensor_copy(o, i[:, 0:NQ]), sem="vc")
                p.wait("scalar", "vc")
                p.dma("scalar", lambda e, s=yv, tc=tc, ts4=ts4, dh2=dh2: e.dma_start(
                    out=y_in.ap()[tc * TCH + ts4 * 128:tc * TCH + (ts4 + 1) * 128, dh2 * NQ:(dh2 + 1) * NQ],
                    in_=s), sem="vld")
                p.wait("vector", "vld")
    p.wait("gpsimd", "vld")
    p.do("gpsimd", lambda e: e.collective_compute(
        "AllGather", ALU.bypass, replica_groups=[list(range(NC))],
        ins=[y_in.ap()], outs=[y_all.ap()]), sem="ccy")

    p.wait("gpsimd", "ccy")
    p.wait("vector", "mm")
    for j in range(TT):
        p.wait("sync", "vc")
        ld(Dm[:, 0:D], ssp.ap()[j * 128:(j + 1) * 128, :], sem="ld2")
        rdv = u32v(SMALL, 128, 2)
        rwv = SMALL[:, 132:136]
        p.wait("gpsimd", "vc", p.v("vc"))
        p.dma("gpsimd", lambda e, d=rdv, j=j: e.indirect_dma_start(
            out=d, out_offset=None, in_=rtd_d.ap(),
            in_offset=IndirectOffsetOnAxis(ap=OWN[:, j:j + 1], axis=0)), sem="gdma")
        p.dma("gpsimd", lambda e, d=rwv, j=j: e.indirect_dma_start(
            out=d, out_offset=None, in_=rtw_d.ap(),
            in_offset=IndirectOffsetOnAxis(ap=OWN[:, j:j + 1], axis=0)), sem="gdma")
        p.wait("gpsimd", "gdma")
        g0 = Em[:, 0:D]; g1v = Em2[:, 0:D]
        p.dma("gpsimd", lambda e, d=g0, i=rdv: e.indirect_dma_start(
            out=d, out_offset=None, in_=y_all.ap(),
            in_offset=IndirectOffsetOnAxis(ap=i[:, 0:1], axis=0)), sem="gdma")
        p.dma("gpsimd", lambda e, d=g1v, i=rdv: e.indirect_dma_start(
            out=d, out_offset=None, in_=y_all.ap(),
            in_offset=IndirectOffsetOnAxis(ap=i[:, 1:2], axis=0)), sem="gdma")
        xr = Em2[:, D:2 * D] if D <= 1024 else Em2[:, 0:D]
        p.wait("sync", "vc")
        ld(xr, xown[j * 128:(j + 1) * 128, :], sem="ld")
        p.wait("vector", "gdma")
        p.wait("vector", "ld")
        p.wait("vector", "ld2")
        mo = Em[:, D:2 * D] if D <= 1024 else Em[:, 0:D]
        p.do("vector", lambda e, o=mo, a=g0, w=rwv: e.tensor_scalar(o, a, w[:, 0:1], None, ALU.mult))
        p.do("vector", lambda e, o=mo, a=g1v, w=rwv: e.scalar_tensor_tensor(o, a, w[:, 1:2], o, ALU.mult, ALU.add), sem="vc")
        p.wait("scalar", "vc")
        gs = Em[:, 0:D]
        p.do("scalar", lambda e, o=gs, i=mo: e.activation(o, i, ACT.Silu), sem="sc")
        p.wait("vector", "sc")
        x1t = A[:, j * D:(j + 1) * D]
        p.do("vector", lambda e, o=x1t, s=Dm[:, 0:D], g=gs: e.tensor_tensor(o, s, g, ALU.mult))
        p.do("vector", lambda e, o=x1t, x=xr: e.tensor_tensor(o, o, x, ALU.add), sem="vc")
        p.wait("scalar", "vc")
        p.dma("scalar", lambda e, s=x1t, j=j: e.dma_start(out=x1sp.ap()[j * 128:(j + 1) * 128, :], in_=s), sem="vld")

    for j in range(TT):
        x1t = A[:, j * D:(j + 1) * D]
        stats_pass(x1t, MU1, RS1, j)
        xnt = Em[:, 1024:1024 + D] if D <= 1024 else Em[:, 0:D]
        p.do("vector", lambda e, x=x1t, o=xnt, j=j: e.tensor_scalar(
            o, x, MU1[:, j:j + 1], RS1[:, j:j + 1], ALU.subtract, ALU.mult), sem="vc")
        split_transpose_to(B, j, xnt)
        for pp in range(3):
            for dc in range(DC):
                lh = (xnh(dc) if pp != 2 else xnl(dc))[:, j * 128:(j + 1) * 128]
                rh = (wgT_h if pp != 1 else wgT_l)[:, dc * E:(dc + 1) * E]
                first = (pp == 0 and dc == 0); last = (pp == 2 and dc == DC - 1)
                p.do("tensor", lambda e, l=lh, r=rh, f=first, s=last: e.matmul(
                    PS[6][:, 0:E], l, r, start=f, stop=s), sem="mm" if last else None)
        p.wait("vector", "mm")
        lgt = Em[:, 0:E]
        p.do("vector", lambda e, o=lgt: e.tensor_copy(o, PS[6][:, 0:E]))
        v0 = Em[:, 16:17]; v1 = Em[:, 17:18]; oh0 = Em[:, 32:32 + E]; oh1 = Em[:, 48:48 + E]; lm = Em[:, 64:64 + E]
        p.do(V, lambda e, l=lgt, o=v0: e.reduce_max(o, l, axis=AX))
        p.do(V, lambda e, l=lgt, m=v0, o=oh0: e.tensor_scalar(o, l, m[:, 0:1], None, ALU.is_equal))
        p.do(V, lambda e, l=lgt, o=lm, h=oh0: e.scalar_tensor_tensor(o, h, -1e30, l, ALU.mult, ALU.add))
        p.do(V, lambda e, l=lm, o=v1: e.reduce_max(o, l, axis=AX))
        p.do(V, lambda e, l=lm, m=v1, o=oh1: e.tensor_scalar(o, l, m[:, 0:1], None, ALU.is_equal))
        p.do(V, lambda e, a=v0, b=v1, o=Em[:, 18:19]: e.tensor_tensor(o, a, b, ALU.subtract), sem="vc")
        p.wait("scalar", "vc")
        p.do("scalar", lambda e, i=Em[:, 18:19], o=Em[:, 19:20]: e.activation(o, i, ACT.Sigmoid), sem="sc")
        p.wait(V, "sc")
        p.do(V, lambda e, w=Em[:, 19:20], o=Em[:, 20:21]: e.tensor_single_scalar(o, w, 1.0, ALU.subtract))
        p.do(V, lambda e, o=Em[:, 20:21]: e.tensor_single_scalar(o, o, -1.0, ALU.mult))
        p.do(V, lambda e, h=oh0, w=Em[:, 19:20], o=CW[:, j * E:(j + 1) * E]: e.tensor_scalar(o, h, w[:, 0:1], None, ALU.mult))
        p.do(V, lambda e, h=oh1, w=Em[:, 20:21], o=CW[:, j * E:(j + 1) * E]: e.scalar_tensor_tensor(
            o, h, w[:, 0:1], o, ALU.mult, ALU.add), sem="vc")

    p.wait("sync", "ccw")
    p.wait("vector", "vld")
    wmm = {}

    wld = {}

    def wet_load(e_, wmm=wmm, wld=wld):
        if e_ >= 1:
            p.wait("sync", "mm", wmm[e_ - 1])
        ld(bc16(C, 0, DC * 2 * D).rearrange("p (a c) -> p a c", a=DC),
           wet_all.ap()[e_ * D:(e_ + 1) * D, :].rearrange("(a p) c -> p a c", p=128), sem="ld2")
        wld[e_] = p.v("ld2")

    p.wait("sync", "mm")
    wet_load(0)
    zvch = []
    for e_ in range(E):
        wet_sb = bc16(C, 0, DC * 2 * D)
        beb = Em2[:, 1024:1024 + D] if D <= 1024 else Em2[:, D:2 * D]
        p.wait("sync", "vc")
        ld(beb, be_d[e_:e_ + 1, :].broadcast_to((128, D)), sem="ld")
        p.wait("tensor", "ld2", wld[e_])
        for j in range(TT):
            for dh2 in range(D // NQ):
                z = (e_ * TT + j) * (D // NQ) + dh2
                if z >= 2:
                    p.wait("tensor", "vc", zvch[z - 2])
                pso = PS[0] if z % 2 == 0 else PS[1]
                for pp in range(3):
                    for dc in range(DC):
                        lh = (xnh(dc) if pp != 2 else xnl(dc))[:, j * 128:(j + 1) * 128]
                        rh = wet_sb.rearrange("p (a c) -> p a c", a=DC)[:, dc, :][:, (0 if pp != 1 else D) + dh2 * NQ:(0 if pp != 1 else D) + (dh2 + 1) * NQ]
                        first = (pp == 0 and dc == 0); last = (pp == 2 and dc == DC - 1)
                        p.do("tensor", lambda e, o=pso, l=lh, r=rh, f=first, s=last: e.matmul(
                            o[:, 0:NQ], l, r, start=f, stop=s), sem="mm" if last else None)
                p.wait("vector", "mm")
                p.wait("vector", "ld")
                ta = Em[:, (z % 2) * NQ:(z % 2) * NQ + NQ]
                p.do("vector", lambda e, o=ta, i=pso, b=beb, dh2=dh2: e.tensor_tensor(
                    o, i[:, 0:NQ], b[:, dh2 * NQ:(dh2 + 1) * NQ], ALU.add))
                accs = A[:, j * D + dh2 * NQ:j * D + (dh2 + 1) * NQ]
                if e_ == 0:
                    p.do("vector", lambda e, o=accs, i=ta, j=j: e.tensor_scalar(
                        o, i, CW[:, j * E:j * E + 1], None, ALU.mult), sem="vc")
                else:
                    p.do("vector", lambda e, o=accs, i=ta, j=j, e_=e_: e.scalar_tensor_tensor(
                        o, i, CW[:, j * E + e_:j * E + e_ + 1], o, ALU.mult, ALU.add), sem="vc")
                zvch.append(p.v("vc"))
        wmm[e_] = p.v("mm")
        if e_ + 1 < E:
            wet_load(e_ + 1)
    p.wait("scalar", "vc")
    for j in range(TT):
        p.do("scalar", lambda e, j=j: e.activation(A[:, j * D:(j + 1) * D], A[:, j * D:(j + 1) * D], ACT.Silu), sem="sc")

    lnw_bc = Dm[:, 0:D]; lnb_bc = Dm[:, D:2 * D]
    p.wait("sync", "mm")
    p.wait("sync", "vc")
    ld(lnw_bc, lnw_d.broadcast_to((128, D)), sem="ld2")
    ld(lnb_bc, lnb_d.broadcast_to((128, D)), sem="ld2")
    p.wait("vector", "ld2")
    for j in range(TT):
        x1t = Em[:, 0:D] if D <= 1024 else Em[:, 0:D]
        p.wait("sync", "vc")
        p.wait("sync", "sc")
        ld(x1t, x1sp.ap()[j * 128:(j + 1) * 128, :], sem="ld")
        p.wait("vector", "ld")
        xn1t = Em[:, 1024:1024 + D] if D <= 1024 else Em[:, D:2 * D]
        p.do("vector", lambda e, x=x1t, o=xn1t, j=j: e.tensor_scalar(
            o, x, MU1[:, j:j + 1], RS1[:, j:j + 1], ALU.subtract, ALU.mult), sem="vc")
        bst = Em2[:, 0:6 * NCH]; bag = Em2[:, 8 * NCH:8 * NCH + 2]
        for ch in range(NCH):
            p.do("vector", lambda e, x=xn1t, ch=ch: e.bn_stats(
                Em2[:, 6 * ch:6 * ch + 6], x[:, 512 * ch:min(512 * (ch + 1), D)]))
        p.do("vector", lambda e, b=bst, a=bag: e.bn_aggr(a, b))
        p.do("vector", lambda e, a=bag, j=j: e.tensor_copy(MU2[:, j:j + 1], a[:, 0:1]))
        p.do("vector", lambda e, a=bag: e.tensor_single_scalar(Em2[:, 8 * NCH + 4:8 * NCH + 5], a[:, 1:2], 1e-5, ALU.add), sem="vc")
        p.wait("scalar", "vc")
        p.do("scalar", lambda e: e.activation(Em2[:, 8 * NCH + 5:8 * NCH + 6], Em2[:, 8 * NCH + 4:8 * NCH + 5], ACT.Sqrt), sem="sc")
        p.wait("vector", "sc")
        p.do("vector", lambda e, j=j: e.reciprocal(RS2[:, j:j + 1], Em2[:, 8 * NCH + 5:8 * NCH + 6]), sem="vc")
        znt = Em2[:, 1024:1024 + D] if D <= 1024 else Em2[:, D:2 * D]
        p.do("vector", lambda e, x=xn1t, o=znt, j=j: e.tensor_scalar(
            o, x, MU2[:, j:j + 1], RS2[:, j:j + 1], ALU.subtract, ALU.mult))
        p.do("vector", lambda e, o=znt: e.tensor_tensor(o, o, lnw_bc, ALU.mult))
        p.do("vector", lambda e, o=znt: e.tensor_tensor(o, o, lnb_bc, ALU.add), sem="vc")
        split_transpose_to(B, j, znt)
    znh = xnh; znl = xnl

    hp2 = 2 * DH
    NPH = NH // 2
    Qh = bc16(Dm, 0, T); Ql = bc16(Dm, T // 2, T)
    Kh = bc16(Dm, T, T); Kl = bc16(Dm, T + T // 2, T)
    VA0c = 0
    VAb = T

    for hp in range(NPH):
        for part in range(3):
            colbase = part * D + hp * hp2
            wful = Em2[:, 0:DC * hp2]
            p.wait("sync", "mm")
            p.wait("sync", "vc")
            blkid = part * (D // hp2) * 0 + part * NPH + hp if False else part * (NH // 2) + hp
            ld(wful, wqkv_d[blkid, :, :, :].rearrange("p a c -> p (a c)"))
            p.wait("scalar", "ld")
            wh = bc16(Em2, DC * hp2 // 2 * 0 + 512, DC * hp2) if False else bc16(Em2, 512, DC * hp2)
            wl = bc16(Em2, 512 + DC * hp2 // 2, DC * hp2)
            split_hilo(wful, wh, wl)
            p.wait("tensor", "vc")
            for ts in range(T // NQ):
                p.wait("tensor", "vc")
                pso = PS[0] if ts % 2 == 0 else PS[1]
                for pp in range(3):
                    for dc in range(DC):
                        lh = (wh if pp < 2 else wl).rearrange("p (a c) -> p a c", a=DC)[:, dc, :]
                        rh = (znh(dc) if pp != 1 else znl(dc))[:, ts * NQ:(ts + 1) * NQ]
                        first = (pp == 0 and dc == 0); last = (pp == 2 and dc == DC - 1)
                        p.do("tensor", lambda e, o=pso, l=lh, r=rh, f=first, s=last: e.matmul(
                            o[0:hp2, 0:NQ], l, r, start=f, stop=s), sem="mm" if last else None)
                p.wait("vector", "mm")
                ev = Em[:, 0:NQ]
                bcol = part * (D // hp2) + hp
                p.do("vector", lambda e, o=ev, i=pso, bcol=bcol: e.tensor_scalar(
                    o[0:hp2, :], i[0:hp2, 0:NQ], bqkv_s[0:hp2, bcol:bcol + 1], None, ALU.add), sem="vc")
                if part < 2:
                    hi_t = bc16(Em, 512, NQ); lo_t = bc16(Em, 512 + NQ // 2, NQ)
                    split_hilo(ev, hi_t, lo_t)
                    dsth = Qh if part == 0 else Kh
                    dstl = Ql if part == 0 else Kl
                    p.do("vector", lambda e, d=dsth, s=hi_t, ts=ts: e.tensor_copy(
                        d[0:hp2, ts * NQ:(ts + 1) * NQ], s[0:hp2, :]))
                    p.do("vector", lambda e, d=dstl, s=lo_t, ts=ts: e.tensor_copy(
                        d[0:hp2, ts * NQ:(ts + 1) * NQ], s[0:hp2, :]), sem="vc")
                    sqb = Em[:, 1024:1024 + NQ]
                    p.do("scalar", lambda e, o=sqb, i=ev: e.activation(o[0:hp2, :], i[0:hp2, :], ACT.Square), sem="sc")
                    p.wait("tensor", "sc")
                    p.do("tensor", lambda e, s=sqb: e.matmul(PS[6][0:2, 0:NQ], op2_s[0:hp2, :], s[0:hp2, :],
                                                             start=True, stop=True), sem="mm")
                    p.wait("vector", "mm")
                    ncol = 70 if part == 0 else 71
                    p.do("vector", lambda e, c=ncol: e.reduce_max(SMALL[0:2, 76 + c - 70:77 + c - 70], PS[6][0:2, 0:NQ], axis=AX))
                    if ts == 0:
                        p.do("vector", lambda e, c=ncol: e.tensor_copy(SMALL[0:2, c:c + 1], SMALL[0:2, 76 + c - 70:77 + c - 70]), sem="vc")
                    else:
                        p.do("vector", lambda e, c=ncol: e.tensor_tensor(SMALL[0:2, c:c + 1], SMALL[0:2, c:c + 1], SMALL[0:2, 76 + c - 70:77 + c - 70], ALU.max), sem="vc")
                else:
                    p.do("vector", lambda e, s=ev, ts=ts: e.tensor_copy(
                        C[0:hp2, ts * NQ:(ts + 1) * NQ], s[0:hp2, :]), sem="vc")
        p.wait("tensor", "vc")
        for tt2 in range(TT):
            p.wait("tensor", "vc")
            p.do("tensor", lambda e, tt2=tt2: e.transpose(PS[4][:, 0:2 * DH], C[0:2 * DH, tt2 * 128:(tt2 + 1) * 128], idf_s[0:hp2, 0:hp2]), sem="mm")
            p.wait("vector", "mm")
            p.do("vector", lambda e, tt2=tt2: e.tensor_copy(C[:, VAb + tt2 * 132:VAb + tt2 * 132 + DH], PS[4][:, 0:DH]))
            p.do("vector", lambda e, tt2=tt2: e.tensor_copy(C[:, VAb + tt2 * 132 + 66:VAb + tt2 * 132 + 66 + DH], PS[4][:, DH:2 * DH]))
            p.do("vector", lambda e, tt2=tt2: e.memset(C[:, VAb + tt2 * 132 + DH:VAb + tt2 * 132 + DH + 1], 1.0))
            p.do("vector", lambda e, tt2=tt2: e.memset(C[:, VAb + tt2 * 132 + 66 + DH:VAb + tt2 * 132 + 66 + DH + 1], 1.0), sem="vc")
        p.wait("vector", "vld")
        p.wait("scalar", "vc")
        p.do("scalar", lambda e: e.activation(SMALL[0:2, 72:73], SMALL[0:2, 70:71], ACT.Sqrt))
        p.do("scalar", lambda e: e.activation(SMALL[0:2, 73:74], SMALL[0:2, 71:72], ACT.Sqrt), sem="sc")
        p.wait("vector", "sc")
        p.do("vector", lambda e: e.tensor_tensor(SMALL[0:2, 74:75], SMALL[0:2, 72:73], SMALL[0:2, 73:74], ALU.mult))
        p.do("vector", lambda e: e.tensor_single_scalar(SMALL[0:2, 75:76], SMALL[0:2, 74:75], -L1SC, ALU.mult), sem="vc")
        p.wait("scalar", "vc")
        p.dma("scalar", lambda e, hp=hp: e.dma_start(out=bcb.ap()[0:1, 8 + 2 * hp:8 + 2 * hp + 2].transpose([1, 0]),
                                                    in_=SMALL[0:2, 75:76]), sem="vld")
        p.wait("sync", "vld")
        ld(negcH[:, 2 * hp:2 * hp + 2], bcb.ap()[0:1, 8 + 2 * hp:8 + 2 * hp + 2].broadcast_to((128, 2)), sem="ld2")

        for qq in range(T // NQ):
            for h2 in range(2):
                p.wait("tensor", "vc")
                p.wait("scalar", "ld2", p.v("ld2"))
                po = PS[2] if h2 == 0 else PS[3]
                sc_hist = []

                def emit_pv(kt, hp=hp, h2=h2, po=po):
                    Ehb = Em[:, (kt % 2) * NQ:(kt % 2) * NQ + NQ]
                    p.wait("tensor", "sc", sc_hist[kt])
                    p.do("tensor", lambda e, o=po, kt=kt, r=Ehb: e.matmul(
                        o[0:DH + 1, 0:NQ], C[:, VAb + kt * 132 + 66 * h2:VAb + kt * 132 + 66 * h2 + DH + 1], r,
                        start=(kt == 0), stop=(kt == TT - 1)), sem="mm")

                for kt in range(TT):
                    pss = PS[0] if kt % 2 == 0 else PS[1]
                    if kt >= 2:
                        p.wait("tensor", "sc", sc_hist[kt - 2])
                    for pp in range(3):
                        lh = (Kh if pp < 2 else Kl)[h2 * DH:(h2 + 1) * DH, kt * 128:(kt + 1) * 128]
                        rh = (Qh if pp != 1 else Ql)[h2 * DH:(h2 + 1) * DH, qq * NQ:(qq + 1) * NQ]
                        first = (pp == 0); last = (pp == 2)
                        p.do("tensor", lambda e, o=pss, l=lh, r=rh, f=first, s=last: e.matmul(
                            o[:, 0:NQ], l, r, start=f, stop=s), sem="mm" if last else None)
                    p.wait("scalar", "mm")
                    Ehb = Em[:, (kt % 2) * NQ:(kt % 2) * NQ + NQ]
                    p.do("scalar", lambda e, d=Ehb, i=pss, hp=hp, h2=h2: e.activation(
                        d, i[:, 0:NQ], ACT.Exp, bias=negcH[:, 2 * hp + h2:2 * hp + h2 + 1], scale=L1SC), sem="sc")
                    sc_hist.append(p.v("sc"))
                    if kt >= 1:
                        emit_pv(kt - 1)
                emit_pv(TT - 1)
                p.wait("vector", "mm")
                p.wait("vector", "vld")
                p.do("vector", lambda e, i=po: e.reciprocal(Em2[0:1, 0:NQ], i[DH:DH + 1, 0:NQ]), sem="vc")
                p.wait("scalar", "vc")
                p.dma("scalar", lambda e: e.dma_start(out=bcb.ap()[0:1, 1024:1024 + NQ], in_=Em2[0:1, 0:NQ]), sem="vld")
                p.wait("sync", "vld")
                p.wait("sync", "vc")
                ld(Em2[0:DH, 0:NQ], bcb.ap()[0:1, 1024:1024 + NQ].broadcast_to((DH, NQ)), sem="ld2")
                p.wait("vector", "ld2")
                mslice = Em[:, 1024:1024 + NQ]
                p.do("vector", lambda e, i=po: e.tensor_tensor(
                    mslice[0:DH, :] if DH <= 128 else None, i[0:DH, 0:NQ], Em2[0:DH, 0:NQ], ALU.mult), sem="vc")
                p.wait("scalar", "vc")
                p.dma("scalar", lambda e, hp=hp, h2=h2, qq=qq: e.dma_start(
                    out=msp.ap()[hp * hp2 + h2 * DH:hp * hp2 + (h2 + 1) * DH, qq * NQ:(qq + 1) * NQ],
                    in_=mslice[0:DH, :]), sem="vld")
                p.wait("vector", "vld")

    wph = bc16(C, 0, D * DC)
    wpl = bc16(C, (D * DC) // 2, D * DC)
    for dc in range(DC):
        blk = Em2[:, 0:D]
        p.wait("sync", "vc")
        p.wait("sync", "sc")
        ld(blk, wproj_d[dc * 128:(dc + 1) * 128, :])
        p.wait("scalar", "ld")
        split_hilo(blk, wph[:, dc * D:(dc + 1) * D], wpl[:, dc * D:(dc + 1) * D])
    bpb = Dm[:, 0:D]
    p.wait("sync", "sc")
    ld(bpb, bproj_d.broadcast_to((128, D)), sem="ld")
    p.wait("tensor", "vc")
    p.wait("vector", "vld")
    for qq2 in range(T // NQ):
        mchq = B[:, 0:DC * NQ]
        p.wait("sync", "mm")
        p.wait("sync", "vc")
        if qq2 == 0:
            p.wait("sync", "vld")
        ld(mchq.rearrange("p (a c) -> p a c", a=DC),
           msp.ap()[:, qq2 * NQ:(qq2 + 1) * NQ].rearrange("(a p) c -> p a c", p=128), sem="ld2")
        p.wait("scalar", "ld2")
        mh = bc16(B, DC * NQ, DC * NQ); ml_ = bc16(B, DC * NQ + DC * NQ // 2, DC * NQ)
        split_hilo(mchq, mh, ml_)
        p.wait("tensor", "vc")
        for qt in [qq2 * (NQ // 128) + i for i in range(NQ // 128)]:
            qo = (qt - qq2 * (NQ // 128)) * 128
            for dh2 in range(D // NQ):
                p.wait("tensor", "vc")
                pso = PS[4] if dh2 % 2 == 0 else PS[5]
                for pp in range(3):
                    for dc in range(DC):
                        lh = (mh if pp < 2 else ml_).rearrange("p (a c) -> p a c", a=DC)[:, dc, :][:, qo:qo + 128]
                        rh = (wph if pp != 1 else wpl)[:, dc * D + dh2 * NQ:dc * D + (dh2 + 1) * NQ]
                        first = (pp == 0 and dc == 0); last = (pp == 2 and dc == DC - 1)
                        p.do("tensor", lambda e, o=pso, l=lh, r=rh, f=first, s=last: e.matmul(
                            o[:, 0:NQ], l, r, start=f, stop=s), sem="mm" if last else None)
            x1t = Em[:, 0:D]
            p.wait("sync", "vc")
            ld(x1t, x1sp.ap()[qt * 128:(qt + 1) * 128, :], sem="ld")
            p.wait("vector", "mm")
            p.wait("vector", "ld")
            xn1t = Em[:, 1024:1024 + D] if D <= 1024 else Em[:, D:2 * D]
            p.do("vector", lambda e, x=x1t, o=xn1t, qt=qt: e.tensor_scalar(
                o, x, MU1[:, qt:qt + 1], RS1[:, qt:qt + 1], ALU.subtract, ALU.mult), sem="vc")
            for dh2 in range(D // NQ):
                pso = PS[4] if dh2 % 2 == 0 else PS[5]
                st_ = Em2[:, 0:NQ]
                p.do("vector", lambda e, o=st_, i=pso, dh2=dh2: e.tensor_tensor(
                    o, i[:, 0:NQ], bpb[:, dh2 * NQ:(dh2 + 1) * NQ], ALU.add))
                p.do("vector", lambda e, o=st_, z=xn1t, dh2=dh2: e.tensor_tensor(
                    o, o, z[:, dh2 * NQ:(dh2 + 1) * NQ], ALU.add))
                p.do("vector", lambda e, o=st_, g=A, qt=qt, dh2=dh2: e.tensor_tensor(
                    o, o, g[:, qt * D + dh2 * NQ:qt * D + (dh2 + 1) * NQ], ALU.mult))
                p.do("vector", lambda e, o=st_, x=x1t, dh2=dh2: e.tensor_tensor(
                    o, o, x[:, dh2 * NQ:(dh2 + 1) * NQ], ALU.add), sem="vc")
                p.wait("scalar", "vc")
                p.dma("scalar", lambda e, s=st_, qt=qt, dh2=dh2: e.dma_start(
                    out=out_d[qt * 128:(qt + 1) * 128, dh2 * NQ:(dh2 + 1) * NQ], in_=s), sem="vld")
                p.wait("vector", "vld")

    p.wait("gpsimd", "vld")
    p.wait("gpsimd", "st")
    p.wait("gpsimd", "gdma")
    p.wait("sync", "vld")

    with nc.Block() as block:
        for name in ("sync", "tensor", "vector", "scalar", "gpsimd"):
            def make_body(nm):
                def body(eng):
                    for fn in p.items[nm]:
                        fn(eng)
                return body
            getattr(block, name)(make_body(name))

    es.close()
    return nc


_NC_CACHE = {}


def _get_nc(key, cfg):
    if key not in _NC_CACHE:
        _NC_CACHE[key] = build_nc(cfg)
    return _NC_CACHE[key]


def make_in_maps(I, cfg):
    T, D, HF, NH, E, NC = cfg["T"], cfg["D"], cfg["HF"], cfg["NH"], cfg["E"], cfg["NC"]
    N = NC * T
    x = np.ascontiguousarray(I["L0_attn_x"], dtype=np.float32)
    xf = np.ascontiguousarray(x.reshape(N, D))
    c = np.ascontiguousarray
    maps = []
    for ci in range(NC):
        m = dict(
            xfull=xf,
            xown=c(x[ci]),
            coff=np.full((128, 1), float(ci * T), dtype=np.float32),
            wq=c(I["L0_attn_Wq"]), wk=c(I["L0_attn_Wk"]), wv=c(I["L0_attn_Wv"]),
            bq=c(I["L0_attn_bq"]), bk=c(I["L0_attn_bk"]), bv=c(I["L0_attn_bv"]),
            bvr=c(I["L0_attn_bv"].reshape(1, D)),
            rw=c(I["L0_ffn_router_w"]), rb=c(I["L0_ffn_router_b"].reshape(E, 1)),
            wx=c(I["L0_ffn_W"][ci].reshape(D // 128, 128, HF // 128, 128).transpose(2, 1, 0, 3)),
            vx=c(I["L0_ffn_V"][ci].reshape(D // 128, 128, HF // 128, 128).transpose(2, 1, 0, 3)),
            wox=c(I["L0_ffn_W_out"][ci]),
            lnw=c(I["L1_attn_ln_w"].reshape(1, D)), lnb=c(I["L1_attn_ln_b"].reshape(1, D)),
            wqkv=c(I["L1_attn_W_qkv"].reshape(D // 128, 128, 3, NH // 2, 2 * D // NH).transpose(2, 3, 1, 0, 4).reshape(3 * NH // 2, 128, D // 128, 2 * D // NH)),
            bqkv=c(I["L1_attn_b_qkv"]),
            wproj=c(I["L1_attn_W_proj"]), bproj=c(I["L1_attn_b_proj"].reshape(1, D)),
            wg=c(I["L1_ffn_Wg"]), weo=c(I["L1_ffn_We"][ci]), be=c(I["L1_ffn_be"]),
        )
        maps.append(m)
    return maps


def kernel(**inputs):
    cfg = FULL_CFG
    nc = _get_nc("full", cfg)
    in_maps = make_in_maps(inputs, cfg)
    res = run_bass_kernel_spmd(nc, in_maps, core_ids=list(range(cfg["NC"])))
    out = np.stack([np.asarray(res.results[ci]["out"]) for ci in range(cfg["NC"])], axis=0)
    return out.astype(np.float32)


# version 18
# speedup vs baseline: 1.3011x; 1.3011x over previous
import os
import numpy as np
from contextlib import ExitStack

import concourse.bass as bass
import concourse.mybir as mybir
from concourse.bass import IndirectOffsetOnAxis
from concourse.bass_utils import run_bass_kernel_spmd
import ml_dtypes

F32 = mybir.dt.float32
BF16 = mybir.dt.bfloat16
U32 = mybir.dt.uint32
AX = mybir.AxisListType.X
ALU = mybir.AluOpType
ACT = mybir.ActivationFunctionType

FULL_CFG = dict(T=2048, D=1024, HF=2048, NH=16, E=8, NC=8)


class Prog:
    def __init__(self, nc):
        self.nc = nc
        self.items = {k: [] for k in ("sync", "tensor", "vector", "scalar", "gpsimd")}
        self.sems = {}
        self.cnt = {}

    def sem(self, name, es):
        self.sems[name] = es.enter_context(self.nc.semaphore(name))
        self.cnt[name] = 0

    def do(self, eng, fn, sem=None, n=1):
        if eng == "vector":
            hv = self.sems["vc"]
            prior = self.cnt["vc"]
            self.items[eng].append(lambda e, h=hv, v=prior: e.wait_ge(h, v))
            if sem is None or sem == "vc":
                sem, n = "vc", 1
            else:
                self.cnt["vc"] += 1
                self.cnt[sem] += n
                h2 = self.sems[sem]

                def wrapped2(e, fn=fn, h2=h2, n=n, hv=hv):
                    fn(e).then_inc(h2, n).then_inc(hv, 1)

                self.items[eng].append(wrapped2)
                return
        if sem is not None:
            self.cnt[sem] += n
            h = self.sems[sem]

            def wrapped(e, fn=fn, h=h, n=n):
                fn(e).then_inc(h, n)

            self.items[eng].append(wrapped)
        else:
            self.items[eng].append(fn)

    QSEM = {"sync": "qs", "scalar": "qa", "gpsimd": "qg"}

    def dma(self, eng, fn, sem=None, handshake=True):
        q = self.QSEM[eng]
        self.cnt[q] += 16
        hq = self.sems[q]
        qv = self.cnt[q]
        self.items[eng].append(lambda e, fn=fn, hq=hq: fn(e).then_inc(hq, 16))
        if handshake:
            self.qsync(eng, sem)

    def qsync(self, eng, sem):
        q = self.QSEM[eng]
        hq = self.sems[q]
        qv = self.cnt[q]
        self.cnt[sem] += 1
        hs = self.sems[sem]
        self.items[eng].append(lambda e, hq=hq, qv=qv, hs=hs: e.wait_ge(hq, qv).then_inc(hs, 1))

    def wait(self, eng, sem, val=None):
        h = self.sems[sem]
        v = self.cnt[sem] if val is None else val
        self.items[eng].append(lambda e, h=h, v=v: e.wait_ge(h, v))

    def v(self, sem):
        return self.cnt[sem]


def build_nc(cfg):
    T, D, HF, NH, E, NC = cfg["T"], cfg["D"], cfg["HF"], cfg["NH"], cfg["E"], cfg["NC"]
    DC = D // 128
    TT = T // 128
    N = NC * T
    CAP = T
    DH = D // NH
    HC = HF // 128
    A_ = N // 128
    NQ = 512 if T >= 512 else T
    QG = T // NQ
    L0SC = 1.0 / float(np.sqrt(D))
    L1SC = 1.0 / float(np.sqrt(DH))

    nc = bass.Bass("TRN2", target_bir_lowering=False)

    def din(name, shape, dt=F32):
        return nc.dram_tensor(name, list(shape), dt, kind="ExternalInput").ap()

    xfull = din("xfull", (N, D))
    xown = din("xown", (T, D))
    coff = din("coff", (128, 1))
    wq_d = din("wq", (D, D)); wk_d = din("wk", (D, D)); wv_d = din("wv", (D, D))
    bq_d = din("bq", (D,)); bk_d = din("bk", (D,)); bv_d = din("bv", (D,))
    rw_d = din("rw", (E, D)); rb_d = din("rb", (E, 1))
    wx_d = din("wx", (HF // 128, 128, D // 128, 128)); vx_d = din("vx", (HF // 128, 128, D // 128, 128))
    wox_d = din("wox", (HF, D))
    lnw_d = din("lnw", (1, D)); lnb_d = din("lnb", (1, D))
    wqkv_d = din("wqkv", (3 * NH // 2, 128, D // 128, 2 * D // NH)); bqkv_d = din("bqkv", (3 * D,))
    wproj_d = din("wproj", (D, D)); bproj_d = din("bproj", (1, D))
    wg_d = din("wg", (E, D)); weo_d = din("weo", (D, D)); be_d = din("be", (E, D))
    bvr_d = din("bvr", (1, D))
    out_d = nc.dram_tensor("out", [T, D], F32, kind="ExternalOutput").ap()

    rt_in = nc.dram_tensor("rt_in", [T, 16], F32)
    rt_all = nc.dram_tensor("rt_all", [N, 16], F32)
    y_in = nc.dram_tensor("y_in", [T, D], F32)
    y_all = nc.dram_tensor("y_all", [N, D], F32)
    wet_in = nc.dram_tensor("wet_in", [D, 2 * D], BF16)
    wet_all = nc.dram_tensor("wet_all", [E * D, 2 * D], BF16)
    DBGV = os.environ.get("DBG_LB") == "1"
    Lb = nc.dram_tensor("Lb", [N + 2 * T, 1], U32)
    Lpart = nc.dram_tensor("Lpart", [N + 2 * T, 1], U32)
    rtw_d = nc.dram_tensor("rtw", [N, 4], F32)
    if DBGV:
        dbgLb = nc.dram_tensor("dbgLb", [N + 2 * T, 1], U32, kind="ExternalOutput")
        dbgRtw = nc.dram_tensor("dbgRtw", [N, 4], F32, kind="ExternalOutput")
        dbgY = nc.dram_tensor("dbgY", [N, D], F32, kind="ExternalOutput")
        dbgS = nc.dram_tensor("dbgS", [T, D], F32, kind="ExternalOutput")
        dbgX1 = nc.dram_tensor("dbgX1", [T, D], F32, kind="ExternalOutput")
        dbgB = nc.dram_tensor("dbgB", [1, 4096], F32, kind="ExternalOutput")
        dbgVs = nc.dram_tensor("dbgVs", [T, D], F32, kind="ExternalOutput")
        dbgKh = nc.dram_tensor("dbgKh", [D, T], BF16, kind="ExternalOutput")
        dbgKl = nc.dram_tensor("dbgKl", [D, T], BF16, kind="ExternalOutput")
    rtd_d = nc.dram_tensor("rtd", [N, 2], U32)
    ksp_h = nc.dram_tensor("ksp_h", [D, T], BF16)
    ksp_l = nc.dram_tensor("ksp_l", [D, T], BF16)
    vsp = nc.dram_tensor("vsp", [T, D], F32)
    x1sp = nc.dram_tensor("x1sp", [T, D], F32)
    ssp = nc.dram_tensor("ssp", [T, D], F32)
    msp = nc.dram_tensor("msp", [D, T], F32)
    bcb = nc.dram_tensor("bcb", [1, 4096], F32)

    idf = nc.inline_tensor(np.eye(128, dtype=np.float32), name="idf").ap()
    idb = nc.inline_tensor(np.eye(128).astype(ml_dtypes.bfloat16), name="idb").ap()
    utS = nc.inline_tensor(np.triu(np.ones((128, 128), np.float32), 1), name="utS").ap()
    on1 = nc.inline_tensor(np.ones((128, 1), np.float32), name="on1").ap()
    op2_np = np.zeros((128, 2), np.float32)
    op2_np[0:DH, 0] = 1.0
    op2_np[DH:2 * DH, 1] = 1.0
    op2 = nc.inline_tensor(op2_np, name="op2").ap()
    io8 = nc.inline_tensor(np.arange(E, dtype=np.float32).reshape(1, E), name="io8").ap()
    trm = nc.inline_tensor((np.arange(128) % (128 // NC) * 2.0 * A_).astype(np.float32).reshape(128, 1),
                           name="trm").ap()

    es = ExitStack()
    sb = lambda name, shape, dt=F32: es.enter_context(nc.sbuf_tensor(name, list(shape), dt))
    psf = lambda name: es.enter_context(nc.psum_tensor(name, [128, 512], F32))

    TCH = 512 if T >= 512 else T
    A = sb("A", (128, TT * D))
    B = sb("B", (128, max(TT * D, 100 * A_)))
    C = sb("C", (128, max(TT * NQ, HC * TCH, DC * DC * 128, DC * D, T + TT * 132 + (NH // 2) * NQ)))
    Dm = sb("Dm", (128, 4096 if D >= 1024 else 4 * D))
    Em = sb("Em", (128, 2048))
    Em2 = sb("Em2", (128, 2048))
    Mm = sb("Mm", (128, 960))
    RTS = sb("RTS", (128, 6 * A_))
    K2 = sb("K2", (128, 1664))
    K2u = K2[:].bitcast(U32)
    DS = K2u[:, 973:973 + 256]
    IDS = K2u[:, 685:685 + A_]
    IDT = K2u[:, 813:813 + 128]
    OWN = K2u[:, 941:941 + TT]
    LBT = K2u[:, 957:957 + TT]
    SMALL = K2[:, 1229:1229 + 256]

    PS = [psf(f"PS{i}") for i in range(8)]
    PTv = PS[7][:].bitcast(BF16)

    p = Prog(nc)
    for s in ("ld", "ld2", "st", "vld", "gdma", "ccw", "ccr", "ccy", "ccl", "vc", "sc", "mm", "gc",
              "qs", "qa", "qg"):
        p.sem(s, es)

    def bc16(t, c0_f32, n_bf16):
        return t[:].bitcast(BF16)[:, 2 * c0_f32:2 * c0_f32 + n_bf16]

    def u32v(t, c0, n):
        return t[:].bitcast(U32)[:, c0:c0 + n]

    def split_hilo(src_ap, hi_ap, lo_ap):
        p.wait("scalar", "mm")
        p.wait("scalar", "vc")
        p.do("scalar", lambda e, s=src_ap, h=hi_ap: e.activation(h, s, ACT.Copy), sem="sc")
        p.wait("vector", "sc")
        p.do("vector", lambda e, s=src_ap, h=hi_ap, l=lo_ap: e.tensor_sub(l, s, h), sem="vc")

    def ld(dst, src, sem="ld", eng="sync", slow=False):
        p.dma(eng, lambda e, d=dst, s=src, sl=slow: e.dma_start(out=d, in_=s, allow_slow_non_contiguous=sl), sem=sem)

    hp2_ = 2 * (D // NH)
    K2b = K2[:].bitcast(BF16)
    idf_s = K2[:, 0:128]
    utS_s = K2[:, 128:256]
    idb_s = K2b[:, 512:640]
    on1_s = K2[:, 320:321]
    op2_s = K2[:, 321:323]
    bq_s = K2[:, 323:323 + DC]; bk_s = K2[:, 331:331 + DC]
    bqkv_s = K2[0:hp2_, 469:469 + 3 * D // hp2_]
    rb_s = K2[0:E, 339:340]
    coff_s = K2[:, 340:341]
    rws_st = K2[0:E, 341:469]
    rwT_h = K2b[:, 1114:1114 + DC * E]; rwT_l = K2b[:, 1178:1178 + DC * E]
    wgT_h = K2b[:, 1242:1242 + DC * E]; wgT_l = K2b[:, 1306:1306 + DC * E]
    rwT_f = K2[:, 493:493 + DC * E]

    ld(idf_s, idf)
    ld(utS_s, utS)
    ld(on1_s, on1)
    ld(op2_s, op2)
    ld(idb_s, idb, sem="gdma", eng="gpsimd")
    ld(bq_s[:], bq_d.rearrange("(a p) -> p a", p=128), slow=True)
    ld(bk_s[:], bk_d.rearrange("(a p) -> p a", p=128), slow=True)
    ld(bqkv_s[:], bqkv_d.rearrange("(a p) -> p a", p=hp2_), slow=True)
    ld(rb_s[:], rb_d)
    ld(coff_s, coff)
    io8_bc = SMALL[:, 32:32 + E]
    ld(io8_bc, io8.broadcast_to((128, E)))
    ld(SMALL[:, 225:226], trm)

    p.wait("tensor", "ld")
    for wsrc, dsth, dstl in ((rw_d, rwT_h, rwT_l), (wg_d, wgT_h, wgT_l)):
        for dc in range(DC):
            p.wait("sync", "mm")
            ld(rws_st, wsrc[:, dc * 128:(dc + 1) * 128])
            p.wait("tensor", "ld")
            p.do("tensor", lambda e, dc=dc: e.transpose(
                PS[5][:, dc * E:(dc + 1) * E], rws_st[0:E, :], idf_s[0:E, 0:E]),
                sem="mm" if True else None)
        p.wait("vector", "mm")
        p.do("vector", lambda e: e.tensor_copy(rwT_f[:, 0:DC * E], PS[5][:, 0:DC * E]), sem="vc")
        split_hilo(rwT_f[:, 0:DC * E], dsth[:, 0:DC * E], dstl[:, 0:DC * E])
        p.wait("tensor", "vc")

    p.do("gpsimd", lambda e: e.iota(IDS, pattern=[[1, A_]], base=0, channel_multiplier=A_,
                                    allow_small_or_imprecise_dtypes=True), sem="gc")
    p.do("gpsimd", lambda e: e.iota(SMALL[:, 200:200 + TT], pattern=[[128, TT]], base=0, channel_multiplier=1,
                                    allow_small_or_imprecise_dtypes=True), sem="gc")
    p.wait("vector", "gc")
    p.wait("vector", "ld")
    p.do("vector", lambda e: e.tensor_scalar(SMALL[:, 200:200 + TT], SMALL[:, 200:200 + TT], coff_s[:, 0:1], None, ALU.add), sem="vc")
    p.do("vector", lambda e: e.tensor_copy(OWN, SMALL[:, 200:200 + TT]), sem="vc")
    nlb = (N + 2 * T) // 128
    p.do("vector", lambda e: e.memset(u32v(Mm, 800, nlb), 0), sem="vc")
    p.wait("sync", "vc")
    p.dma("sync", lambda e: e.dma_start(out=Lpart.ap().rearrange("(p a) o -> p (a o)", p=128),
                                       in_=u32v(Mm, 800, nlb)), sem="st")

    p.wait("tensor", "gdma")
    for oc in range(DC):
        wof = Dm[:, 0:D]
        p.wait("sync", "vc")
        p.wait("sync", "mm")
        ld(wof, weo_d[oc * 128:(oc + 1) * 128, :])
        woh = bc16(Dm, D, D); wol = bc16(Dm, D + D // 2, D)
        p.wait("scalar", "ld")
        split_hilo(wof, woh, wol)
        for half, src in ((0, woh), (1, wol)):
            p.wait("tensor", "vc")
            for dc in range(DC):
                p.do("tensor", lambda e, s=src, dc=dc: e.transpose(
                    PTv[:, dc * 128:(dc + 1) * 128], s[:, dc * 128:(dc + 1) * 128], idb_s),
                    sem="mm" if dc == DC - 1 else None)
            p.wait("vector", "mm")
            dst = bc16(C, 0, DC * 2 * D).rearrange("p (a c) -> p a c", a=DC)[:, :, half * D + oc * 128:half * D + (oc + 1) * 128]
            p.do("vector", lambda e, d=dst: e.tensor_copy(
                d, PTv[:, 0:DC * 128].rearrange("p (a c) -> p a c", a=DC)), sem="vc")
    p.wait("scalar", "vc")
    p.dma("scalar", lambda e: e.dma_start(out=wet_in.ap().rearrange("(a p) c -> p a c", p=128),
                                          in_=bc16(C, 0, DC * 2 * D).rearrange("p (a c) -> p a c", a=DC)), sem="vld")
    p.wait("gpsimd", "vld")
    p.do("gpsimd", lambda e: e.collective_compute(
        "AllGather", ALU.bypass, replica_groups=[list(range(NC))],
        ins=[wet_in.ap()], outs=[wet_all.ap()]), sem="ccw")

    MU = Mm[:, 0:TT]; RS = Mm[:, TT:2 * TT]
    MU1 = Mm[:, 2 * TT:3 * TT]; RS1 = Mm[:, 3 * TT:4 * TT]
    MU2 = Mm[:, 4 * TT:5 * TT]; RS2 = Mm[:, 5 * TT:6 * TT]
    CW = Mm[:, 6 * TT:6 * TT + TT * E]
    negc0 = SMALL[:, 0:1]
    negcH = SMALL[:, 4:4 + 2 * ((NH + 1) // 2)]

    xnh = lambda dc: bc16(B, 0, 2 * DC * T)[:, dc * T:(dc + 1) * T]
    xnl = lambda dc: bc16(B, 0, 2 * DC * T)[:, DC * T + dc * T:DC * T + (dc + 1) * T]

    NCH = (D + 511) // 512
    def stats_pass(xt, MUc, RSc, j):
        p.wait("vector", "vld")
        bst = Em[:, 0:6 * NCH]; bag = Em[:, 8 * NCH:8 * NCH + 2]
        for ch in range(NCH):
            p.do("vector", lambda e, x=xt, ch=ch: e.bn_stats(
                Em[:, 6 * ch:6 * ch + 6], x[:, 512 * ch:min(512 * (ch + 1), D)]))
        p.do("vector", lambda e, b=bst, a=bag: e.bn_aggr(a, b))
        p.do("vector", lambda e, a=bag, j=j, M=MUc: e.tensor_copy(M[:, j:j + 1], a[:, 0:1]))
        p.do("vector", lambda e, a=bag: e.tensor_single_scalar(Em[:, 8 * NCH + 4:8 * NCH + 5], a[:, 1:2], 1e-5, ALU.add), sem="vc")
        p.wait("scalar", "vc")
        p.do("scalar", lambda e: e.activation(Em[:, 8 * NCH + 5:8 * NCH + 6], Em[:, 8 * NCH + 4:8 * NCH + 5], ACT.Sqrt), sem="sc")
        p.wait("vector", "sc")
        p.do("vector", lambda e, j=j, R=RSc: e.reciprocal(R[:, j:j + 1], Em[:, 8 * NCH + 5:8 * NCH + 6]), sem="vc")

    def split_transpose_to(dst_arena, j, src_f32):
        hi_t = bc16(Em2, 0, D); lo_t = bc16(Em2, D // 2, D)
        split_hilo(src_f32, hi_t, lo_t)
        for src, base in ((hi_t, 0), (lo_t, DC * T)):
            p.wait("tensor", "vc")
            for dc in range(DC):
                p.do("tensor", lambda e, s=src, dc=dc: e.transpose(
                    PTv[:, dc * 128:(dc + 1) * 128], s[:, dc * 128:(dc + 1) * 128], idb_s),
                    sem="mm" if dc == DC - 1 else None)
            p.wait("vector", "mm")
            dstap = bc16(dst_arena, 0, 2 * DC * T).rearrange("p (c t) -> p c t", c=2 * DC)[:, base // T:base // T + DC, j * 128:(j + 1) * 128]
            p.do("vector", lambda e, d=dstap: e.tensor_copy(d, PTv[:, 0:DC * 128].rearrange("p (c t) -> p c t", c=DC)), sem="vc")
        p.wait("tensor", "vc")

    p.wait("vector", "vld")
    for j in range(TT):
        xt = A[:, j * D:(j + 1) * D]
        ld(xt, xown[j * 128:(j + 1) * 128, :])
        p.wait("vector", "ld")
        stats_pass(xt, MU, RS, j)
        xnt = Em[:, 1024:1024 + D] if D <= 1024 else Em[:, 0:D]
        p.do("vector", lambda e, x=xt, o=xnt, j=j: e.tensor_scalar(o, x, MU[:, j:j + 1], RS[:, j:j + 1], ALU.subtract, ALU.mult), sem="vc")
        split_transpose_to(B, j, xnt)

    LgF = Em2[0:E, 0:T]
    for ts in range(T // NQ):
        p.wait("tensor", "vc")
        for pp in range(3):
            for dc in range(DC):
                lh = (rwT_h if pp < 2 else rwT_l)[:, dc * E:(dc + 1) * E]
                rh = (xnh(dc) if pp != 1 else xnl(dc))[:, ts * NQ:(ts + 1) * NQ]
                first = (pp == 0 and dc == 0); last = (pp == 2 and dc == DC - 1)
                p.do("tensor", lambda e, l=lh, r=rh, f=first, s=last: e.matmul(
                    PS[6][0:E, 0:NQ], l, r, start=f, stop=s), sem="mm" if last else None)
        p.wait("vector", "mm")
        p.do("vector", lambda e, ts=ts: e.tensor_scalar(
            LgF[:, ts * NQ:(ts + 1) * NQ], PS[6][0:E, 0:NQ], rb_s[0:E, 0:1], None, ALU.add), sem="vc")
    p.wait("tensor", "vc")
    for j in range(TT):
        p.wait("tensor", "vc")
        p.do("tensor", lambda e, j=j: e.transpose(PS[5][:, 0:E], LgF[:, j * 128:(j + 1) * 128], idf_s[0:E, 0:E]), sem="mm")
        p.wait("vector", "mm")
        p.wait("vector", "vld", p.v("vld"))
        stg0 = Em[:, 0:16]
        p.do("vector", lambda e, s=stg0: e.tensor_copy(s[:, 0:E], PS[5][:, 0:E]))
        p.do("vector", lambda e, j=j, s=stg0: e.tensor_copy(s[:, E:E + 1], MU[:, j:j + 1]))
        p.do("vector", lambda e, j=j, s=stg0: e.tensor_copy(s[:, E + 1:E + 2], RS[:, j:j + 1]))
        p.do("vector", lambda e, s=stg0: e.memset(s[:, E + 2:16], 0.0), sem="vc")
        p.wait("scalar", "vc")
        p.dma("scalar", lambda e, j=j, s=stg0: e.dma_start(out=rt_in.ap()[j * 128:(j + 1) * 128, :], in_=s), sem="vld")
    p.wait("gpsimd", "vld")
    p.do("gpsimd", lambda e: e.collective_compute(
        "AllGather", ALU.bypass, replica_groups=[list(range(NC))],
        ins=[rt_in.ap()], outs=[rt_all.ap()]), sem="ccr")

    def load_w_full(wd):
        for dc in range(DC):
            blk = Dm[:, 0:D]
            p.wait("sync", "vc")
            ld(blk, wd[dc * 128:(dc + 1) * 128, :])
            p.wait("scalar", "ld")
            hi = C[:].bitcast(BF16)[:, dc * D:(dc + 1) * D]
            lo = C[:].bitcast(BF16)[:, DC * D + dc * D:DC * D + (dc + 1) * D]
            split_hilo(blk, hi, lo)
        p.wait("tensor", "vc")

    def wch(hi_side, oc, dc):
        base = (0 if hi_side else DC * D) + dc * D + oc * 128
        return C[:].bitcast(BF16)[:, base:base + 128]


    for wi, wd in enumerate((wq_d, wk_d, wv_d)):
        load_w_full(wd)
        if wi < 2:
            for ts in range(T // NQ):
                for oc in range(DC):
                    p.wait("tensor", "vc")
                    p.wait("vector", "vld", p.v("vld"))
                    pso = PS[0] if oc % 2 == 0 else PS[1]
                    for pp in range(3):
                        for dc in range(DC):
                            lh = wch(pp < 2, oc, dc)
                            rh = (xnh(dc) if pp != 1 else xnl(dc))[:, ts * NQ:(ts + 1) * NQ]
                            first = (pp == 0 and dc == 0); last = (pp == 2 and dc == DC - 1)
                            p.do("tensor", lambda e, o=pso, l=lh, r=rh, f=first, s=last: e.matmul(
                                o[:, 0:NQ], l, r, start=f, stop=s), sem="mm" if last else None)
                    p.wait("vector", "mm")
                    q32 = Em[:, 0:NQ]
                    bias = bq_s if wi == 0 else bk_s
                    p.do("vector", lambda e, o=q32, i=pso, b=bias, oc=oc: e.tensor_scalar(
                        o, i[:, 0:NQ], b[:, oc:oc + 1], None, ALU.add), sem="vc")
                    hi_t = bc16(Em, NQ, NQ); lo_t = bc16(Em, NQ + NQ // 2, NQ)
                    split_hilo(q32, hi_t, lo_t)
                    if wi == 0:
                        dh_ = bc16(A, 0, 2 * DC * T)[:, oc * T + ts * NQ:oc * T + (ts + 1) * NQ]
                        dl_ = bc16(A, 0, 2 * DC * T)[:, DC * T + oc * T + ts * NQ:DC * T + oc * T + (ts + 1) * NQ]
                        p.do("vector", lambda e, d=dh_, s=hi_t: e.tensor_copy(d, s))
                        p.do("vector", lambda e, d=dl_, s=lo_t: e.tensor_copy(d, s), sem="vc")
                    else:
                        p.wait("scalar", "vc")
                        p.dma("scalar", lambda e, s=hi_t, oc=oc, ts=ts: e.dma_start(
                            out=ksp_h.ap()[oc * 128:(oc + 1) * 128, ts * NQ:(ts + 1) * NQ], in_=s), sem="vld")
                        p.wait("scalar", "vc")
                        p.dma("scalar", lambda e, s=lo_t, oc=oc, ts=ts: e.dma_start(
                            out=ksp_l.ap()[oc * 128:(oc + 1) * 128, ts * NQ:(ts + 1) * NQ], in_=s), sem="vld")
                    sq = Em[:, 1536:1536 + NQ] if NQ <= 512 else Em[:, NQ:2 * NQ]
                    p.do("scalar", lambda e, s=sq, i=q32: e.activation(s, i, ACT.Square), sem="sc")
                    p.wait("tensor", "sc")
                    p.do("tensor", lambda e, s=sq, f=(oc == 0), l=(oc == DC - 1): e.matmul(
                        PS[6][0:1, 0:NQ], on1_s, s, start=f, stop=l), sem="mm" if oc == DC - 1 else None)
                p.wait("vector", "mm")
                ncol = 16 if wi == 0 else 17
                p.do("vector", lambda e, c=ncol: e.reduce_max(SMALL[0:1, 64 + c:65 + c], PS[6][0:1, 0:NQ], axis=AX))
                if ts == 0:
                    p.do("vector", lambda e, c=ncol: e.tensor_copy(SMALL[0:1, c:c + 1], SMALL[0:1, 64 + c:65 + c]), sem="vc")
                else:
                    p.do("vector", lambda e, c=ncol: e.tensor_tensor(SMALL[0:1, c:c + 1], SMALL[0:1, c:c + 1], SMALL[0:1, 64 + c:65 + c], ALU.max), sem="vc")
        else:
            bv_bc = Em2[:, D:2 * D] if D <= 1024 else Em2[:, 0:D]
            ld(bv_bc, bvr_d.broadcast_to((128, D)), sem="ld2")
            p.wait("vector", "ld2")
            for j in range(TT):
                for half in range(D // NQ):
                    p.wait("tensor", "vc")
                    p.wait("vector", "vld", p.v("vld"))
                    pso = PS[2] if half % 2 == 0 else PS[3]
                    for ri in range(NQ // 128):
                        oc2 = half * (NQ // 128) + ri
                        for pp in range(3):
                            for dc in range(DC):
                                lh = (xnh(dc) if pp != 2 else xnl(dc))[:, j * 128:(j + 1) * 128]
                                rh = wch(pp != 1, oc2, dc)
                                first = (pp == 0 and dc == 0); last = (pp == 2 and dc == DC - 1)
                                p.do("tensor", lambda e, o=pso, l=lh, r=rh, f=first, s=last, ri=ri: e.matmul(
                                    o[:, ri * 128:(ri + 1) * 128], l, r, start=f, stop=s),
                                    sem="mm" if last and ri == NQ // 128 - 1 else None)
                    p.wait("vector", "mm")
                    vv = Em[:, 0:NQ]
                    p.do("vector", lambda e, o=vv, i=pso, h=half, b=bv_bc: e.tensor_add(
                        o, i[:, 0:NQ], b[:, h * NQ:(h + 1) * NQ]), sem="vc")
                    p.wait("scalar", "vc")
                    p.dma("scalar", lambda e, s=vv, j=j, h=half: e.dma_start(
                        out=vsp.ap()[j * 128:(j + 1) * 128, h * NQ:(h + 1) * NQ], in_=s), sem="vld")

    p.wait("scalar", "vc")
    p.do("scalar", lambda e: e.activation(SMALL[0:1, 18:19], SMALL[0:1, 16:17], ACT.Sqrt))
    p.do("scalar", lambda e: e.activation(SMALL[0:1, 19:20], SMALL[0:1, 17:18], ACT.Sqrt), sem="sc")
    p.wait("vector", "sc")
    p.do("vector", lambda e: e.tensor_tensor(SMALL[0:1, 20:21], SMALL[0:1, 18:19], SMALL[0:1, 19:20], ALU.mult))
    p.do("vector", lambda e: e.tensor_single_scalar(SMALL[0:1, 21:22], SMALL[0:1, 20:21], -L0SC, ALU.mult), sem="vc")
    p.wait("scalar", "vc")
    p.dma("scalar", lambda e: e.dma_start(out=bcb.ap()[0:1, 0:1], in_=SMALL[0:1, 21:22]), sem="vld")
    p.wait("sync", "vld")
    ld(negc0, bcb.ap()[0:1, 0:1].broadcast_to((128, 1)), sem="ld2")

    p.wait("sync", "ccr")
    p.wait("sync", "mm")
    RAW = B[:, 0:16 * A_]
    ld(RAW, rt_all.ap().rearrange("(p a) c -> p (a c)", p=128), sem="ld2")
    RAW3 = RAW.rearrange("p (a c) -> p a c", c=16)
    LG = RAW3[:, :, 0:E]
    co = [16 * A_]

    def arr(n):
        r = B[:, co[0]:co[0] + n]
        co[0] += n
        return r

    V0 = arr(A_); V1 = arr(A_)
    OH0 = arr(A_ * E); OH1 = arr(A_ * E); LM = arr(A_ * E)
    HTEA = arr(A_ * E); PRF = arr(A_ * E)
    BS = arr(E)
    P0s = arr(A_); P1s = arr(A_); E0v = arr(A_); E1v = arr(A_)
    W0 = arr(A_); W1 = arr(A_); VA0 = arr(A_); VA1 = arr(A_)
    DC0 = arr(A_); DC1 = arr(A_); DSS0 = arr(A_); DSS1 = arr(A_)
    T1 = arr(A_); T2 = arr(A_ * E)
    TR0 = arr(A_); TR1 = arr(A_)

    d3 = lambda t: t.rearrange("p (a c) -> p a c", c=E)
    dea = lambda t: t.rearrange("p (c a) -> p c a", c=E)
    c1 = lambda t: t.rearrange("p (a o) -> p a o", o=1)
    V = "vector"

    p.wait("gpsimd", "mm")
    p.do("gpsimd", lambda e: e.iota(TR0, pattern=[[2, A_]], base=N, channel_multiplier=0,
                                    allow_small_or_imprecise_dtypes=True), sem="gc")
    p.wait(V, "ld2")
    p.wait(V, "gc")
    p.do(V, lambda e: e.tensor_scalar(TR0, TR0, SMALL[:, 225:226], None, ALU.add))
    p.do(V, lambda e: e.tensor_single_scalar(TR1, TR0, 1.0, ALU.add), sem="vc")
    p.do(V, lambda e: e.reduce_max(c1(V0), LG, axis=AX))
    p.do(V, lambda e: e.tensor_tensor(d3(OH0), LG, c1(V0).broadcast_to((128, A_, E)), ALU.is_equal))
    p.do(V, lambda e: e.scalar_tensor_tensor(d3(LM), d3(OH0), -1e30, LG, ALU.mult, ALU.add))
    p.do(V, lambda e: e.reduce_max(c1(V1), d3(LM), axis=AX))
    p.do(V, lambda e: e.tensor_tensor(d3(OH1), d3(LM), c1(V1).broadcast_to((128, A_, E)), ALU.is_equal))
    p.do(V, lambda e: e.tensor_tensor(dea(HTEA).rearrange("p c a -> p a c"), d3(OH0), d3(OH1), ALU.add))
    for c_ in range(E):
        p.do(V, lambda e, c_=c_: e.tensor_tensor_scan(
            dea(PRF)[:, c_, :], dea(HTEA)[:, c_, :], dea(HTEA)[:, c_, :], 0.0, ALU.add, ALU.bypass),
            sem="vc" if c_ == E - 1 else None)
    p.wait("tensor", "vc")
    p.do("tensor", lambda e: e.matmul(PS[6][:, 0:E], utS_s, dea(PRF)[:, :, A_ - 1], start=True, stop=True), sem="mm")
    p.wait(V, "mm")
    p.do(V, lambda e: e.tensor_copy(BS, PS[6][:, 0:E]))
    p.do(V, lambda e: e.tensor_tensor(dea(PRF), dea(PRF),
                                      BS.rearrange("p (c o) -> p c o", o=1).broadcast_to((128, E, A_)), ALU.add))
    for OH, Ps_, Ev_ in ((OH0, P0s, E0v), (OH1, P1s, E1v)):
        p.do(V, lambda e, OH=OH: e.tensor_tensor(d3(T2), d3(OH), dea(PRF).rearrange("p c a -> p a c"), ALU.mult))
        p.do(V, lambda e, Ps_=Ps_: e.reduce_sum(c1(Ps_), d3(T2), axis=AX))
        p.do(V, lambda e, OH=OH: e.tensor_tensor(d3(T2), d3(OH),
             io8_bc.rearrange("p (o c) -> p o c", o=1).broadcast_to((128, A_, E)), ALU.mult))
        p.do(V, lambda e, Ev_=Ev_: e.reduce_sum(c1(Ev_), d3(T2), axis=AX))
    p.do(V, lambda e: e.tensor_tensor(T1, V0, V1, ALU.subtract), sem="vc")
    p.wait("scalar", "vc")
    p.do("scalar", lambda e: e.activation(W0, T1, ACT.Sigmoid), sem="sc")
    p.wait(V, "sc")
    p.do(V, lambda e: e.tensor_single_scalar(W1, W0, 1.0, ALU.subtract))
    p.do(V, lambda e: e.tensor_single_scalar(W1, W1, -1.0, ALU.mult))
    for si_, (Ps_, Ev_, Va, Wv_, Dc, Ds) in enumerate(((P0s, E0v, VA0, W0, DC0, DSS0), (P1s, E1v, VA1, W1, DC1, DSS1))):
        p.do(V, lambda e, Ps_=Ps_, Va=Va: e.tensor_single_scalar(Va, Ps_, float(CAP), ALU.is_le))
        p.do(V, lambda e, Wv_=Wv_, Va=Va: e.tensor_tensor(Wv_, Wv_, Va, ALU.mult))
        p.do(V, lambda e, Ev_=Ev_, Ps_=Ps_, Dc=Dc: e.scalar_tensor_tensor(Dc, Ev_, float(CAP), Ps_, ALU.mult, ALU.add))
        p.do(V, lambda e, Dc=Dc: e.tensor_single_scalar(Dc, Dc, 1.0, ALU.subtract))
        TRt = TR0 if si_ == 0 else TR1
        p.do(V, lambda e, Ds=Ds, Dc=Dc, Va=Va: e.tensor_tensor(Ds, Dc, Va, ALU.mult))
        p.do(V, lambda e, Va=Va, T1=T1: e.tensor_scalar(T1, Va, -1.0, -1.0, ALU.mult, ALU.subtract))
        p.do(V, lambda e, T1=T1, TRt=TRt: e.tensor_tensor(T1, T1, TRt, ALU.mult))
        p.do(V, lambda e, Ds=Ds, T1=T1: e.tensor_tensor(Ds, Ds, T1, ALU.add))
        p.do(V, lambda e, Dc=Dc: e.tensor_single_scalar(Dc, Dc, float(N - 1), ALU.min), sem="vc")
    RTW3 = RTS[:, 0:4 * A_].rearrange("p (a c) -> p a c", c=4)
    p.do(V, lambda e: e.tensor_copy(RTW3[:, :, 0:1], c1(W0)))
    p.do(V, lambda e: e.tensor_copy(RTW3[:, :, 1:2], c1(W1)))
    p.do(V, lambda e: e.tensor_copy(RTW3[:, :, 2:3], c1(DSS0)))
    p.do(V, lambda e: e.tensor_copy(RTW3[:, :, 3:4], c1(DSS1)), sem="vc")
    RTD3 = u32v(RTS, 4 * A_, 2 * A_).rearrange("p (a c) -> p a c", c=2)
    p.do(V, lambda e: e.tensor_copy(RTD3[:, :, 0:1], c1(DC0)))
    p.do(V, lambda e: e.tensor_copy(RTD3[:, :, 1:2], c1(DC1)), sem="vc")
    p.wait("scalar", "vc")
    p.dma("scalar", lambda e: e.dma_start(out=rtw_d.ap().rearrange("(p a) c -> p a c", p=128), in_=RTW3), sem="vld")
    p.wait("scalar", "vc")
    p.dma("scalar", lambda e: e.dma_start(out=rtd_d.ap().rearrange("(p a) c -> p a c", p=128), in_=RTD3), sem="vld")
    p.wait("tensor", "vc")
    p.wait("sync", "vld")
    p.wait("sync", "mm")
    p.wait("sync", "vc")
    ld(bc16(B, 0, DC * T).rearrange("p (a c) -> p a c", a=DC),
       ksp_h.ap().rearrange("(a p) c -> p a c", p=128), sem="ld2")
    ld(bc16(B, DC * T // 2, DC * T).rearrange("p (a c) -> p a c", a=DC),
       ksp_l.ap().rearrange("(a p) c -> p a c", p=128), sem="ld2")
    kpre = p.v("ld2")
    p.wait("gpsimd", "vc")
    p.wait("gpsimd", "st")
    p.wait("gpsimd", "gc")
    p.wait("gpsimd", "ld2", kpre)
    p.wait("gpsimd", "vld")
    for j in range(TT):
        p.wait("gpsimd", "vc", p.v("vc"))
        p.dma("gpsimd", lambda e, j=j: e.indirect_dma_start(
            out=SMALL[:, 128:132], out_offset=None, in_=rtw_d.ap(),
            in_offset=IndirectOffsetOnAxis(ap=OWN[:, j:j + 1], axis=0)), sem="gdma")
        p.wait("vector", "gdma")
        p.do(V, lambda e: e.tensor_copy(DS[:, 0:2], SMALL[:, 130:132]), sem="vc")
        p.wait("gpsimd", "vc")
        for si in range(2):
            p.dma("gpsimd", lambda e, j=j, si=si: e.indirect_dma_start(
                out=Lpart.ap(), out_offset=IndirectOffsetOnAxis(ap=DS[:, si:si + 1], axis=0),
                in_=OWN[:, j:j + 1], in_offset=None), sem="gdma", handshake=False)
    p.qsync("gpsimd", "gdma")
    p.do("gpsimd", lambda e: e.collective_compute(
        "AllReduce", ALU.add, replica_groups=[list(range(NC))],
        ins=[Lpart.ap()], outs=[Lb.ap()]), sem="ccl")

    KTQ = TT // QG
    nqt = NQ // 128
    p.wait("tensor", "ld2", kpre)
    svld = []
    for qg in range(QG):
        ktmax = (qg + 1) * KTQ
        ksc = {}
        for kt in range(ktmax):
            kh = bc16(B, 0, DC * T); kl = bc16(B, DC * T // 2, DC * T)
            pso = PS[0] if kt % 2 == 0 else PS[1]
            if kt >= 2:
                p.wait("tensor", "sc", ksc[kt - 2])
            for pp in range(3):
                for dc in range(DC):
                    lh = (kh if pp < 2 else kl)[:, dc * T + kt * 128:dc * T + kt * 128 + 128]
                    qh_ = bc16(A, 0, 2 * DC * T)[:, dc * T + qg * NQ:dc * T + (qg + 1) * NQ]
                    ql_ = bc16(A, 0, 2 * DC * T)[:, DC * T + dc * T + qg * NQ:DC * T + dc * T + (qg + 1) * NQ]
                    rh = qh_ if pp != 1 else ql_
                    first = (pp == 0 and dc == 0); last = (pp == 2 and dc == DC - 1)
                    p.do("tensor", lambda e, o=pso, l=lh, r=rh, f=first, s=last: e.matmul(
                        o[:, 0:NQ], l, r, start=f, stop=s), sem="mm" if last else None)
            p.wait("scalar", "mm")
            Edst = C[:, kt * NQ:(kt + 1) * NQ]
            p.do("scalar", lambda e, d=Edst, i=pso: e.activation(
                d, i[:, 0:NQ], ACT.Exp, bias=negc0[:, 0:1], scale=L0SC), sem="sc")
            ksc[kt] = p.v("sc")
            if (kt + 1) * 128 > qg * NQ:
                p.wait("gpsimd", "sc")
                base = qg * NQ - kt * 128
                p.do("gpsimd", lambda e, d=Edst, b=base: e.affine_select(
                    d, d, pattern=[[1, NQ]], compare_op=ALU.is_ge, fill=0.0, base=b, channel_multiplier=-1), sem="gc")
        p.wait("tensor", "sc")
        p.wait("tensor", "gc")
        for dh2 in range(D // NQ):
            vmm = {}
            vld2 = {}

            def v_load(kt, dh2=dh2, qg=qg, vmm=vmm, vld2=vld2):
                s = kt % 2
                if kt >= 2:
                    p.wait("sync", "mm", vmm[kt - 2])
                else:
                    p.wait("sync", "mm")
                ld(Em2[:, s * NQ:s * NQ + NQ], vsp.ap()[kt * 128:(kt + 1) * 128, dh2 * NQ:(dh2 + 1) * NQ], sem="ld2")
                vld2[kt] = p.v("ld2")

            v_load(0)
            if ktmax > 1:
                v_load(1)
            for kt in range(ktmax):
                vv = Em2[:, (kt % 2) * NQ:(kt % 2) * NQ + NQ]
                p.wait("tensor", "ld2", vld2[kt])
                for qt in range(nqt):
                    ktlim = qg * KTQ + ((qt * 128 + 127) // 128) + 1
                    ktlim = min(ktmax, qg * KTQ + qt + 1) if NQ == 512 else ktmax
                    if kt >= ktlim:
                        continue
                    p.do("tensor", lambda e, o=PS[2 + qt], l=C[:, kt * NQ + qt * 128:kt * NQ + qt * 128 + 128],
                         r=vv, f=(kt == 0), s=(kt == ktlim - 1): e.matmul(
                        o[:, 0:NQ], l, r, start=f, stop=s), sem="mm" if qt == nqt - 1 else None)
                vmm[kt] = p.v("mm")
                if kt + 2 < ktmax:
                    v_load(kt + 2)
            if dh2 == 0:
                for qt in range(nqt):
                    ktlim = min(ktmax, qg * KTQ + qt + 1) if NQ == 512 else ktmax
                    for kt in range(ktlim):
                        p.do("tensor", lambda e, l=C[:, kt * NQ + qt * 128:kt * NQ + qt * 128 + 128],
                             qt=qt, f=(kt == 0), s=(kt == ktlim - 1): e.matmul(
                            PS[6][:, qt:qt + 1], l, on1_s, start=f, stop=s, skip_group_check=True),
                            sem="mm" if kt == ktlim - 1 else None)
                p.wait("vector", "mm")
                p.do("vector", lambda e: e.reciprocal(SMALL[:, 64:64 + nqt], PS[6][:, 0:nqt]), sem="vc")
            p.wait("vector", "mm")
            for qt in range(nqt):
                tq = qg * nqt + qt
                sst = Em[:, 1024 + (qt % 2) * NQ:1024 + (qt % 2) * NQ + NQ]
                if len(svld) >= 2:
                    p.wait("vector", "vld", svld[-2])
                p.do("vector", lambda e, o=sst, i=PS[2 + qt], qt=qt:
                     e.tensor_scalar(o, i[:, 0:NQ], SMALL[:, 64 + qt:64 + qt + 1], None, ALU.mult), sem="vc")
                p.wait("scalar", "vc")
                p.dma("scalar", lambda e, s2=sst, tq=tq, dh2=dh2: e.dma_start(
                    out=ssp.ap()[tq * 128:(tq + 1) * 128, dh2 * NQ:(dh2 + 1) * NQ], in_=s2), sem="vld")
                svld.append(p.v("vld"))
            p.wait("tensor", "vc")

    p.wait("gpsimd", "ccl")
    if DBGV:
        p.dma("gpsimd", lambda e: e.dma_start(out=dbgLb.ap(), in_=Lb.ap()), sem="gdma")
        p.dma("gpsimd", lambda e: e.dma_start(out=dbgRtw.ap(), in_=rtw_d.ap()), sem="gdma")
        p.wait("gpsimd", "gdma")
    for j in range(TT):
        p.dma("gpsimd", lambda e, j=j: e.indirect_dma_start(
            out=LBT[:, j:j + 1], out_offset=None, in_=Lb.ap(),
            in_offset=IndirectOffsetOnAxis(ap=OWN[:, j:j + 1], axis=0)), sem="gdma")
    p.wait("gpsimd", "gdma")
    for j in range(TT):
        xg = Em[:, 0:D] if D <= 1024 else Em[:, 0:D]
        stg = Em[:, 1536:1552]
        p.wait("gpsimd", "vc", p.v("vc"))
        p.dma("gpsimd", lambda e, x=xg, j=j: e.indirect_dma_start(
            out=x, out_offset=None, in_=xfull,
            in_offset=IndirectOffsetOnAxis(ap=LBT[:, j:j + 1], axis=0)), sem="gdma")
        p.dma("gpsimd", lambda e, s=stg, j=j: e.indirect_dma_start(
            out=s, out_offset=None, in_=rt_all.ap(),
            in_offset=IndirectOffsetOnAxis(ap=LBT[:, j:j + 1], axis=0)), sem="gdma")
        p.wait("vector", "gdma")
        xe = Em[:, 512:512 + D] if D <= 1024 else Em[:, D:2 * D]
        p.do("vector", lambda e, x=xg, o=xe, s=stg: e.tensor_scalar(
            o, x, s[:, E:E + 1], s[:, E + 1:E + 2], ALU.subtract, ALU.mult), sem="vc")
        split_transpose_to(A, j, xe)
    xeh = lambda dc: bc16(A, 0, 2 * DC * T)[:, dc * T:(dc + 1) * T]
    xel = lambda dc: bc16(A, 0, 2 * DC * T)[:, DC * T + dc * T:DC * T + (dc + 1) * T]

    UH = HC * TCH
    for tc in range(T // TCH):
        units = [(hc, mi) for hc in range(HC) for mi in range(2)]
        mm_hist = {}
        vc_hist = {}

        uld = {}

        def u_load(u, units=units, mm_hist=mm_hist, vc_hist=vc_hist, tc=tc, uld=uld):
            hc, mi = units[u]
            s = u % 2
            wd = wx_d if mi == 0 else vx_d
            if u >= 2:
                p.wait("sync", "mm", mm_hist[u - 2])
                p.wait("sync", "vc", vc_hist[u - 2])
            else:
                p.wait("sync", "mm")
                p.wait("sync", "vc")
            blk = Dm[:, s * 2 * D:s * 2 * D + D]
            ld(blk, wd[hc, :, :, :].rearrange("p a c -> p (a c)"))
            uld[u] = p.v("ld")

        u_load(0)
        u_load(1)
        for u, (hc, mi) in enumerate(units):
            s = u % 2
            blk = Dm[:, s * 2 * D:s * 2 * D + D]
            hi = bc16(Dm, s * 2 * D + D, D); lo = bc16(Dm, s * 2 * D + D + D // 2, D)
            p.wait("scalar", "ld", uld[u])
            if u >= 2:
                p.wait("scalar", "mm", mm_hist[u - 2])
            split_hilo(blk, hi, lo)
            vc_hist[u] = p.v("vc")
            pso = PS[4 + (u % 4)]
            p.wait("tensor", "vc")
            for pp in range(3):
                for dc in range(DC):
                    lh = (hi if pp < 2 else lo).rearrange("p (a c) -> p a c", a=DC)[:, dc, :]
                    rh = (xeh(dc) if pp != 1 else xel(dc))[:, tc * TCH:(tc + 1) * TCH]
                    first = (pp == 0 and dc == 0); last = (pp == 2 and dc == DC - 1)
                    p.do("tensor", lambda e, o=pso, l=lh, r=rh, f=first, s=last: e.matmul(
                        o[:, 0:TCH], l, r, start=f, stop=s), sem="mm" if last else None)
            mm_hist[u] = p.v("mm")
            if u + 2 < len(units):
                u_load(u + 2)
            if mi == 1:
                pa = PS[4 + ((u - 1) % 4)]; pb = PS[4 + (u % 4)]
                p.wait("scalar", "mm")
                if u >= 4:
                    p.wait("scalar", "vc", vc_hist[u - 4])
                sl = Em[:, (hc % 2) * TCH:(hc % 2) * TCH + TCH]
                p.do("scalar", lambda e, o=sl, i=pb: e.activation(o, i[:, 0:TCH], ACT.Silu), sem="sc")
                p.wait("vector", "sc")
                uu = Em[:, 1024 + (hc % 2) * TCH:1024 + (hc % 2) * TCH + TCH] if TCH <= 512 else Em[:, TCH:2 * TCH]
                p.do("vector", lambda e, o=uu, s2=sl, i=pa: e.tensor_tensor(o, i[:, 0:TCH], s2, ALU.mult), sem="vc")
                split_hilo(uu, bc16(C, 0, 2 * UH)[:, hc * TCH:(hc + 1) * TCH],
                           bc16(C, 0, 2 * UH)[:, UH + hc * TCH:UH + (hc + 1) * TCH])
                vc_hist[u] = p.v("vc")
        p.wait("tensor", "vc")
        ymm = {}
        yld = {}

        def y_load(hc, tc=tc, ymm=ymm, yld=yld):
            s = hc % 2
            if hc >= 2:
                p.wait("sync", "mm", ymm[hc - 2])
            else:
                p.wait("sync", "mm")
                p.wait("sync", "vc")
            blk = Dm[:, s * 2 * D:s * 2 * D + D]
            ld(blk, wox_d[hc * 128:(hc + 1) * 128, :])
            yld[hc] = p.v("ld")

        y_load(0)
        y_load(1)
        for hc in range(HC):
            s = hc % 2
            blk = Dm[:, s * 2 * D:s * 2 * D + D]
            hi = bc16(Dm, s * 2 * D + D, D); lo = bc16(Dm, s * 2 * D + D + D // 2, D)
            p.wait("scalar", "ld", yld[hc])
            if hc >= 2:
                p.wait("scalar", "mm", ymm[hc - 2])
            split_hilo(blk, hi, lo)
            p.wait("tensor", "vc")
            for ts4 in range(TCH // 128):
                uh_ = bc16(C, 0, 2 * UH)[:, hc * TCH + ts4 * 128:hc * TCH + ts4 * 128 + 128]
                ul_ = bc16(C, 0, 2 * UH)[:, UH + hc * TCH + ts4 * 128:UH + hc * TCH + ts4 * 128 + 128]
                for dh2 in range(D // NQ):
                    pso = PS[ts4 * (D // NQ) + dh2]
                    for pp in range(3):
                        lh = uh_ if pp < 2 else ul_
                        rh = (hi if pp != 1 else lo)[:, dh2 * NQ:(dh2 + 1) * NQ]
                        first = (pp == 0 and hc == 0); last = (pp == 2 and hc == HC - 1)
                        p.do("tensor", lambda e, o=pso, l=lh, r=rh, f=first, s=last: e.matmul(
                            o[:, 0:NQ], l, r, start=f, stop=s),
                            sem="mm" if pp == 2 else None)
            ymm[hc] = p.v("mm")
            if hc + 2 < HC:
                y_load(hc + 2)
        p.wait("vector", "mm")
        for ts4 in range(TCH // 128):
            for dh2 in range(D // NQ):
                yv = Em[:, 0:NQ]
                p.do("vector", lambda e, o=yv, i=PS[ts4 * (D // NQ) + dh2]: e.tensor_copy(o, i[:, 0:NQ]), sem="vc")
                p.wait("scalar", "vc")
                p.dma("scalar", lambda e, s=yv, tc=tc, ts4=ts4, dh2=dh2: e.dma_start(
                    out=y_in.ap()[tc * TCH + ts4 * 128:tc * TCH + (ts4 + 1) * 128, dh2 * NQ:(dh2 + 1) * NQ],
                    in_=s), sem="vld")
                p.wait("vector", "vld")
    p.wait("gpsimd", "vld")
    p.do("gpsimd", lambda e: e.collective_compute(
        "AllGather", ALU.bypass, replica_groups=[list(range(NC))],
        ins=[y_in.ap()], outs=[y_all.ap()]), sem="ccy")

    p.wait("gpsimd", "ccy")
    if DBGV:
        p.dma("gpsimd", lambda e: e.dma_start(out=dbgY.ap(), in_=y_all.ap()), sem="gdma")
        p.dma("gpsimd", lambda e: e.dma_start(out=dbgS.ap(), in_=ssp.ap()), sem="gdma")
        p.dma("gpsimd", lambda e: e.dma_start(out=dbgB.ap(), in_=bcb.ap()), sem="gdma")
        p.dma("gpsimd", lambda e: e.dma_start(out=dbgVs.ap(), in_=vsp.ap()), sem="gdma")
        p.dma("gpsimd", lambda e: e.dma_start(out=dbgKh.ap(), in_=ksp_h.ap()), sem="gdma")
        p.dma("gpsimd", lambda e: e.dma_start(out=dbgKl.ap(), in_=ksp_l.ap()), sem="gdma")
        p.wait("gpsimd", "gdma")
    p.wait("vector", "mm")
    for j in range(TT):
        p.wait("sync", "vc")
        ld(Dm[:, 0:D], ssp.ap()[j * 128:(j + 1) * 128, :], sem="ld2")
        rdv = u32v(SMALL, 128, 2)
        rwv = SMALL[:, 132:136]
        p.wait("gpsimd", "vc", p.v("vc"))
        p.dma("gpsimd", lambda e, d=rdv, j=j: e.indirect_dma_start(
            out=d, out_offset=None, in_=rtd_d.ap(),
            in_offset=IndirectOffsetOnAxis(ap=OWN[:, j:j + 1], axis=0)), sem="gdma")
        p.dma("gpsimd", lambda e, d=rwv, j=j: e.indirect_dma_start(
            out=d, out_offset=None, in_=rtw_d.ap(),
            in_offset=IndirectOffsetOnAxis(ap=OWN[:, j:j + 1], axis=0)), sem="gdma")
        p.wait("gpsimd", "gdma")
        g0 = Em[:, 0:D]; g1v = Em2[:, 0:D]
        p.dma("gpsimd", lambda e, d=g0, i=rdv: e.indirect_dma_start(
            out=d, out_offset=None, in_=y_all.ap(),
            in_offset=IndirectOffsetOnAxis(ap=i[:, 0:1], axis=0)), sem="gdma")
        p.dma("gpsimd", lambda e, d=g1v, i=rdv: e.indirect_dma_start(
            out=d, out_offset=None, in_=y_all.ap(),
            in_offset=IndirectOffsetOnAxis(ap=i[:, 1:2], axis=0)), sem="gdma")
        xr = Em2[:, D:2 * D] if D <= 1024 else Em2[:, 0:D]
        p.wait("sync", "vc")
        ld(xr, xown[j * 128:(j + 1) * 128, :], sem="ld")
        p.wait("vector", "gdma")
        p.wait("vector", "ld")
        p.wait("vector", "ld2")
        mo = Em[:, D:2 * D] if D <= 1024 else Em[:, 0:D]
        p.do("vector", lambda e, o=mo, a=g0, w=rwv: e.tensor_scalar(o, a, w[:, 0:1], None, ALU.mult))
        p.do("vector", lambda e, o=mo, a=g1v, w=rwv: e.scalar_tensor_tensor(o, a, w[:, 1:2], o, ALU.mult, ALU.add), sem="vc")
        p.wait("scalar", "vc")
        gs = Em[:, 0:D]
        p.do("scalar", lambda e, o=gs, i=mo: e.activation(o, i, ACT.Silu), sem="sc")
        p.wait("vector", "sc")
        x1t = A[:, j * D:(j + 1) * D]
        p.do("vector", lambda e, o=x1t, s=Dm[:, 0:D], g=gs: e.tensor_tensor(o, s, g, ALU.mult))
        p.do("vector", lambda e, o=x1t, x=xr: e.tensor_tensor(o, o, x, ALU.add), sem="vc")
        p.wait("scalar", "vc")
        p.dma("scalar", lambda e, s=x1t, j=j: e.dma_start(out=x1sp.ap()[j * 128:(j + 1) * 128, :], in_=s), sem="vld")

    if DBGV:
        p.wait("gpsimd", "vld")
        p.dma("gpsimd", lambda e: e.dma_start(out=dbgX1.ap(), in_=x1sp.ap()), sem="gdma")
        p.wait("gpsimd", "gdma")
    for j in range(TT):
        x1t = A[:, j * D:(j + 1) * D]
        stats_pass(x1t, MU1, RS1, j)
        xnt = Em[:, 1024:1024 + D] if D <= 1024 else Em[:, 0:D]
        p.do("vector", lambda e, x=x1t, o=xnt, j=j: e.tensor_scalar(
            o, x, MU1[:, j:j + 1], RS1[:, j:j + 1], ALU.subtract, ALU.mult), sem="vc")
        split_transpose_to(B, j, xnt)
        for pp in range(3):
            for dc in range(DC):
                lh = (xnh(dc) if pp != 2 else xnl(dc))[:, j * 128:(j + 1) * 128]
                rh = (wgT_h if pp != 1 else wgT_l)[:, dc * E:(dc + 1) * E]
                first = (pp == 0 and dc == 0); last = (pp == 2 and dc == DC - 1)
                p.do("tensor", lambda e, l=lh, r=rh, f=first, s=last: e.matmul(
                    PS[6][:, 0:E], l, r, start=f, stop=s), sem="mm" if last else None)
        p.wait("vector", "mm")
        lgt = Em[:, 0:E]
        p.do("vector", lambda e, o=lgt: e.tensor_copy(o, PS[6][:, 0:E]))
        v0 = Em[:, 16:17]; v1 = Em[:, 17:18]; oh0 = Em[:, 32:32 + E]; oh1 = Em[:, 48:48 + E]; lm = Em[:, 64:64 + E]
        p.do(V, lambda e, l=lgt, o=v0: e.reduce_max(o, l, axis=AX))
        p.do(V, lambda e, l=lgt, m=v0, o=oh0: e.tensor_scalar(o, l, m[:, 0:1], None, ALU.is_equal))
        p.do(V, lambda e, l=lgt, o=lm, h=oh0: e.scalar_tensor_tensor(o, h, -1e30, l, ALU.mult, ALU.add))
        p.do(V, lambda e, l=lm, o=v1: e.reduce_max(o, l, axis=AX))
        p.do(V, lambda e, l=lm, m=v1, o=oh1: e.tensor_scalar(o, l, m[:, 0:1], None, ALU.is_equal))
        p.do(V, lambda e, a=v0, b=v1, o=Em[:, 18:19]: e.tensor_tensor(o, a, b, ALU.subtract), sem="vc")
        p.wait("scalar", "vc")
        p.do("scalar", lambda e, i=Em[:, 18:19], o=Em[:, 19:20]: e.activation(o, i, ACT.Sigmoid), sem="sc")
        p.wait(V, "sc")
        p.do(V, lambda e, w=Em[:, 19:20], o=Em[:, 20:21]: e.tensor_single_scalar(o, w, 1.0, ALU.subtract))
        p.do(V, lambda e, o=Em[:, 20:21]: e.tensor_single_scalar(o, o, -1.0, ALU.mult))
        p.do(V, lambda e, h=oh0, w=Em[:, 19:20], o=CW[:, j * E:(j + 1) * E]: e.tensor_scalar(o, h, w[:, 0:1], None, ALU.mult))
        p.do(V, lambda e, h=oh1, w=Em[:, 20:21], o=CW[:, j * E:(j + 1) * E]: e.scalar_tensor_tensor(
            o, h, w[:, 0:1], o, ALU.mult, ALU.add), sem="vc")

    p.wait("sync", "ccw")
    p.wait("vector", "vld")
    wmm = {}

    wld = {}

    def wet_load(e_, wmm=wmm, wld=wld):
        if e_ >= 1:
            p.wait("sync", "mm", wmm[e_ - 1])
        ld(bc16(C, 0, DC * 2 * D).rearrange("p (a c) -> p a c", a=DC),
           wet_all.ap()[e_ * D:(e_ + 1) * D, :].rearrange("(a p) c -> p a c", p=128), sem="ld2")
        wld[e_] = p.v("ld2")

    p.wait("sync", "mm")
    wet_load(0)
    zvch = []
    for e_ in range(E):
        wet_sb = bc16(C, 0, DC * 2 * D)
        beb = Em2[:, 1024:1024 + D] if D <= 1024 else Em2[:, D:2 * D]
        p.wait("sync", "vc")
        ld(beb, be_d[e_:e_ + 1, :].broadcast_to((128, D)), sem="ld")
        p.wait("tensor", "ld2", wld[e_])
        for j in range(TT):
            for dh2 in range(D // NQ):
                z = (e_ * TT + j) * (D // NQ) + dh2
                if z >= 2:
                    p.wait("tensor", "vc", zvch[z - 2])
                pso = PS[0] if z % 2 == 0 else PS[1]
                for pp in range(3):
                    for dc in range(DC):
                        lh = (xnh(dc) if pp != 2 else xnl(dc))[:, j * 128:(j + 1) * 128]
                        rh = wet_sb.rearrange("p (a c) -> p a c", a=DC)[:, dc, :][:, (0 if pp != 1 else D) + dh2 * NQ:(0 if pp != 1 else D) + (dh2 + 1) * NQ]
                        first = (pp == 0 and dc == 0); last = (pp == 2 and dc == DC - 1)
                        p.do("tensor", lambda e, o=pso, l=lh, r=rh, f=first, s=last: e.matmul(
                            o[:, 0:NQ], l, r, start=f, stop=s), sem="mm" if last else None)
                p.wait("vector", "mm")
                p.wait("vector", "ld")
                ta = Em[:, (z % 2) * NQ:(z % 2) * NQ + NQ]
                p.do("vector", lambda e, o=ta, i=pso, b=beb, dh2=dh2: e.tensor_tensor(
                    o, i[:, 0:NQ], b[:, dh2 * NQ:(dh2 + 1) * NQ], ALU.add))
                accs = A[:, j * D + dh2 * NQ:j * D + (dh2 + 1) * NQ]
                if e_ == 0:
                    p.do("vector", lambda e, o=accs, i=ta, j=j: e.tensor_scalar(
                        o, i, CW[:, j * E:j * E + 1], None, ALU.mult), sem="vc")
                else:
                    p.do("vector", lambda e, o=accs, i=ta, j=j, e_=e_: e.scalar_tensor_tensor(
                        o, i, CW[:, j * E + e_:j * E + e_ + 1], o, ALU.mult, ALU.add), sem="vc")
                zvch.append(p.v("vc"))
        wmm[e_] = p.v("mm")
        if e_ + 1 < E:
            wet_load(e_ + 1)
    p.wait("scalar", "vc")
    for j in range(TT):
        p.do("scalar", lambda e, j=j: e.activation(A[:, j * D:(j + 1) * D], A[:, j * D:(j + 1) * D], ACT.Silu), sem="sc")

    lnw_bc = Dm[:, 0:D]; lnb_bc = Dm[:, D:2 * D]
    p.wait("sync", "mm")
    p.wait("sync", "vc")
    ld(lnw_bc, lnw_d.broadcast_to((128, D)), sem="ld2")
    ld(lnb_bc, lnb_d.broadcast_to((128, D)), sem="ld2")
    p.wait("vector", "ld2")
    for j in range(TT):
        x1t = Em[:, 0:D] if D <= 1024 else Em[:, 0:D]
        p.wait("sync", "vc")
        p.wait("sync", "sc")
        ld(x1t, x1sp.ap()[j * 128:(j + 1) * 128, :], sem="ld")
        p.wait("vector", "ld")
        xn1t = Em[:, 1024:1024 + D] if D <= 1024 else Em[:, D:2 * D]
        p.do("vector", lambda e, x=x1t, o=xn1t, j=j: e.tensor_scalar(
            o, x, MU1[:, j:j + 1], RS1[:, j:j + 1], ALU.subtract, ALU.mult), sem="vc")
        bst = Em2[:, 0:6 * NCH]; bag = Em2[:, 8 * NCH:8 * NCH + 2]
        for ch in range(NCH):
            p.do("vector", lambda e, x=xn1t, ch=ch: e.bn_stats(
                Em2[:, 6 * ch:6 * ch + 6], x[:, 512 * ch:min(512 * (ch + 1), D)]))
        p.do("vector", lambda e, b=bst, a=bag: e.bn_aggr(a, b))
        p.do("vector", lambda e, a=bag, j=j: e.tensor_copy(MU2[:, j:j + 1], a[:, 0:1]))
        p.do("vector", lambda e, a=bag: e.tensor_single_scalar(Em2[:, 8 * NCH + 4:8 * NCH + 5], a[:, 1:2], 1e-5, ALU.add), sem="vc")
        p.wait("scalar", "vc")
        p.do("scalar", lambda e: e.activation(Em2[:, 8 * NCH + 5:8 * NCH + 6], Em2[:, 8 * NCH + 4:8 * NCH + 5], ACT.Sqrt), sem="sc")
        p.wait("vector", "sc")
        p.do("vector", lambda e, j=j: e.reciprocal(RS2[:, j:j + 1], Em2[:, 8 * NCH + 5:8 * NCH + 6]), sem="vc")
        znt = Em2[:, 1024:1024 + D] if D <= 1024 else Em2[:, D:2 * D]
        p.do("vector", lambda e, x=xn1t, o=znt, j=j: e.tensor_scalar(
            o, x, MU2[:, j:j + 1], RS2[:, j:j + 1], ALU.subtract, ALU.mult))
        p.do("vector", lambda e, o=znt: e.tensor_tensor(o, o, lnw_bc, ALU.mult))
        p.do("vector", lambda e, o=znt: e.tensor_tensor(o, o, lnb_bc, ALU.add), sem="vc")
        split_transpose_to(B, j, znt)
    znh = xnh; znl = xnl

    hp2 = 2 * DH
    NPH = NH // 2
    Qh = bc16(Dm, 0, T); Ql = bc16(Dm, T // 2, T)
    Kh = bc16(Dm, T, T); Kl = bc16(Dm, T + T // 2, T)
    VA0c = 0
    VAb = T

    for hp in range(NPH):
        for part in range(3):
            colbase = part * D + hp * hp2
            wful = Em2[:, 0:DC * hp2]
            p.wait("sync", "mm")
            p.wait("sync", "vc")
            blkid = part * (D // hp2) * 0 + part * NPH + hp if False else part * (NH // 2) + hp
            ld(wful, wqkv_d[blkid, :, :, :].rearrange("p a c -> p (a c)"))
            p.wait("scalar", "ld")
            wh = bc16(Em2, DC * hp2 // 2 * 0 + 512, DC * hp2) if False else bc16(Em2, 512, DC * hp2)
            wl = bc16(Em2, 512 + DC * hp2 // 2, DC * hp2)
            split_hilo(wful, wh, wl)
            p.wait("tensor", "vc")
            for ts in range(T // NQ):
                p.wait("tensor", "vc")
                pso = PS[0] if ts % 2 == 0 else PS[1]
                for pp in range(3):
                    for dc in range(DC):
                        lh = (wh if pp < 2 else wl).rearrange("p (a c) -> p a c", a=DC)[:, dc, :]
                        rh = (znh(dc) if pp != 1 else znl(dc))[:, ts * NQ:(ts + 1) * NQ]
                        first = (pp == 0 and dc == 0); last = (pp == 2 and dc == DC - 1)
                        p.do("tensor", lambda e, o=pso, l=lh, r=rh, f=first, s=last: e.matmul(
                            o[0:hp2, 0:NQ], l, r, start=f, stop=s), sem="mm" if last else None)
                p.wait("vector", "mm")
                ev = Em[:, 0:NQ]
                bcol = part * (D // hp2) + hp
                p.do("vector", lambda e, o=ev, i=pso, bcol=bcol: e.tensor_scalar(
                    o[0:hp2, :], i[0:hp2, 0:NQ], bqkv_s[0:hp2, bcol:bcol + 1], None, ALU.add), sem="vc")
                if part < 2:
                    hi_t = bc16(Em, 512, NQ); lo_t = bc16(Em, 512 + NQ // 2, NQ)
                    split_hilo(ev, hi_t, lo_t)
                    dsth = Qh if part == 0 else Kh
                    dstl = Ql if part == 0 else Kl
                    p.do("vector", lambda e, d=dsth, s=hi_t, ts=ts: e.tensor_copy(
                        d[0:hp2, ts * NQ:(ts + 1) * NQ], s[0:hp2, :]))
                    p.do("vector", lambda e, d=dstl, s=lo_t, ts=ts: e.tensor_copy(
                        d[0:hp2, ts * NQ:(ts + 1) * NQ], s[0:hp2, :]), sem="vc")
                    sqb = Em[:, 1024:1024 + NQ]
                    p.do("scalar", lambda e, o=sqb, i=ev: e.activation(o[0:hp2, :], i[0:hp2, :], ACT.Square), sem="sc")
                    p.wait("tensor", "sc")
                    p.do("tensor", lambda e, s=sqb: e.matmul(PS[6][0:2, 0:NQ], op2_s[0:hp2, :], s[0:hp2, :],
                                                             start=True, stop=True), sem="mm")
                    p.wait("vector", "mm")
                    ncol = 70 if part == 0 else 71
                    p.do("vector", lambda e, c=ncol: e.reduce_max(SMALL[0:2, 76 + c - 70:77 + c - 70], PS[6][0:2, 0:NQ], axis=AX))
                    if ts == 0:
                        p.do("vector", lambda e, c=ncol: e.tensor_copy(SMALL[0:2, c:c + 1], SMALL[0:2, 76 + c - 70:77 + c - 70]), sem="vc")
                    else:
                        p.do("vector", lambda e, c=ncol: e.tensor_tensor(SMALL[0:2, c:c + 1], SMALL[0:2, c:c + 1], SMALL[0:2, 76 + c - 70:77 + c - 70], ALU.max), sem="vc")
                else:
                    p.do("vector", lambda e, s=ev, ts=ts: e.tensor_copy(
                        C[0:hp2, ts * NQ:(ts + 1) * NQ], s[0:hp2, :]), sem="vc")
        p.wait("tensor", "vc")
        for tt2 in range(TT):
            p.wait("tensor", "vc")
            p.do("tensor", lambda e, tt2=tt2: e.transpose(PS[4][:, 0:2 * DH], C[0:2 * DH, tt2 * 128:(tt2 + 1) * 128], idf_s[0:hp2, 0:hp2]), sem="mm")
            p.wait("vector", "mm")
            p.do("vector", lambda e, tt2=tt2: e.tensor_copy(C[:, VAb + tt2 * 132:VAb + tt2 * 132 + DH], PS[4][:, 0:DH]))
            p.do("vector", lambda e, tt2=tt2: e.tensor_copy(C[:, VAb + tt2 * 132 + 66:VAb + tt2 * 132 + 66 + DH], PS[4][:, DH:2 * DH]))
            p.do("vector", lambda e, tt2=tt2: e.memset(C[:, VAb + tt2 * 132 + DH:VAb + tt2 * 132 + DH + 1], 1.0))
            p.do("vector", lambda e, tt2=tt2: e.memset(C[:, VAb + tt2 * 132 + 66 + DH:VAb + tt2 * 132 + 66 + DH + 1], 1.0), sem="vc")
        p.wait("vector", "vld")
        p.wait("scalar", "vc")
        p.do("scalar", lambda e: e.activation(SMALL[0:2, 72:73], SMALL[0:2, 70:71], ACT.Sqrt))
        p.do("scalar", lambda e: e.activation(SMALL[0:2, 73:74], SMALL[0:2, 71:72], ACT.Sqrt), sem="sc")
        p.wait("vector", "sc")
        p.do("vector", lambda e: e.tensor_tensor(SMALL[0:2, 74:75], SMALL[0:2, 72:73], SMALL[0:2, 73:74], ALU.mult))
        p.do("vector", lambda e: e.tensor_single_scalar(SMALL[0:2, 75:76], SMALL[0:2, 74:75], -L1SC, ALU.mult), sem="vc")
        p.wait("scalar", "vc")
        p.dma("scalar", lambda e, hp=hp: e.dma_start(out=bcb.ap()[0:1, 8 + 2 * hp:8 + 2 * hp + 2].transpose([1, 0]),
                                                    in_=SMALL[0:2, 75:76]), sem="vld")
        p.wait("sync", "vld")
        ld(negcH[:, 2 * hp:2 * hp + 2], bcb.ap()[0:1, 8 + 2 * hp:8 + 2 * hp + 2].broadcast_to((128, 2)), sem="ld2")

        for qq in range(T // NQ):
            for h2 in range(2):
                p.wait("tensor", "vc")
                p.wait("scalar", "ld2", p.v("ld2"))
                po = PS[2] if h2 == 0 else PS[3]
                sc_hist = []

                def emit_pv(kt, hp=hp, h2=h2, po=po):
                    Ehb = Em[:, (kt % 2) * NQ:(kt % 2) * NQ + NQ]
                    p.wait("tensor", "sc", sc_hist[kt])
                    p.do("tensor", lambda e, o=po, kt=kt, r=Ehb: e.matmul(
                        o[0:DH + 1, 0:NQ], C[:, VAb + kt * 132 + 66 * h2:VAb + kt * 132 + 66 * h2 + DH + 1], r,
                        start=(kt == 0), stop=(kt == TT - 1)), sem="mm")

                for kt in range(TT):
                    pss = PS[0] if kt % 2 == 0 else PS[1]
                    if kt >= 2:
                        p.wait("tensor", "sc", sc_hist[kt - 2])
                    for pp in range(3):
                        lh = (Kh if pp < 2 else Kl)[h2 * DH:(h2 + 1) * DH, kt * 128:(kt + 1) * 128]
                        rh = (Qh if pp != 1 else Ql)[h2 * DH:(h2 + 1) * DH, qq * NQ:(qq + 1) * NQ]
                        first = (pp == 0); last = (pp == 2)
                        p.do("tensor", lambda e, o=pss, l=lh, r=rh, f=first, s=last: e.matmul(
                            o[:, 0:NQ], l, r, start=f, stop=s), sem="mm" if last else None)
                    p.wait("scalar", "mm")
                    Ehb = Em[:, (kt % 2) * NQ:(kt % 2) * NQ + NQ]
                    p.do("scalar", lambda e, d=Ehb, i=pss, hp=hp, h2=h2: e.activation(
                        d, i[:, 0:NQ], ACT.Exp, bias=negcH[:, 2 * hp + h2:2 * hp + h2 + 1], scale=L1SC), sem="sc")
                    sc_hist.append(p.v("sc"))
                    if kt >= 1:
                        emit_pv(kt - 1)
                emit_pv(TT - 1)
                p.wait("vector", "mm")
                p.wait("vector", "vld")
                p.do("vector", lambda e, i=po: e.reciprocal(Em2[0:1, 0:NQ], i[DH:DH + 1, 0:NQ]), sem="vc")
                p.wait("scalar", "vc")
                p.dma("scalar", lambda e: e.dma_start(out=bcb.ap()[0:1, 1024:1024 + NQ], in_=Em2[0:1, 0:NQ]), sem="vld")
                p.wait("sync", "vld")
                p.wait("sync", "vc")
                ld(Em2[0:DH, 0:NQ], bcb.ap()[0:1, 1024:1024 + NQ].broadcast_to((DH, NQ)), sem="ld2")
                p.wait("vector", "ld2")
                mslice = Em[:, 1024:1024 + NQ]
                p.do("vector", lambda e, i=po: e.tensor_tensor(
                    mslice[0:DH, :] if DH <= 128 else None, i[0:DH, 0:NQ], Em2[0:DH, 0:NQ], ALU.mult), sem="vc")
                p.wait("scalar", "vc")
                p.dma("scalar", lambda e, hp=hp, h2=h2, qq=qq: e.dma_start(
                    out=msp.ap()[hp * hp2 + h2 * DH:hp * hp2 + (h2 + 1) * DH, qq * NQ:(qq + 1) * NQ],
                    in_=mslice[0:DH, :]), sem="vld")
                p.wait("vector", "vld")

    wph = bc16(C, 0, D * DC)
    wpl = bc16(C, (D * DC) // 2, D * DC)
    for dc in range(DC):
        blk = Em2[:, 0:D]
        p.wait("sync", "vc")
        p.wait("sync", "sc")
        ld(blk, wproj_d[dc * 128:(dc + 1) * 128, :])
        p.wait("scalar", "ld")
        split_hilo(blk, wph[:, dc * D:(dc + 1) * D], wpl[:, dc * D:(dc + 1) * D])
    bpb = Dm[:, 0:D]
    p.wait("sync", "sc")
    ld(bpb, bproj_d.broadcast_to((128, D)), sem="ld")
    p.wait("tensor", "vc")
    p.wait("vector", "vld")
    for qq2 in range(T // NQ):
        mchq = B[:, 0:DC * NQ]
        p.wait("sync", "mm")
        p.wait("sync", "vc")
        if qq2 == 0:
            p.wait("sync", "vld")
        ld(mchq.rearrange("p (a c) -> p a c", a=DC),
           msp.ap()[:, qq2 * NQ:(qq2 + 1) * NQ].rearrange("(a p) c -> p a c", p=128), sem="ld2")
        p.wait("scalar", "ld2")
        mh = bc16(B, DC * NQ, DC * NQ); ml_ = bc16(B, DC * NQ + DC * NQ // 2, DC * NQ)
        split_hilo(mchq, mh, ml_)
        p.wait("tensor", "vc")
        for qt in [qq2 * (NQ // 128) + i for i in range(NQ // 128)]:
            qo = (qt - qq2 * (NQ // 128)) * 128
            for dh2 in range(D // NQ):
                p.wait("tensor", "vc")
                pso = PS[4] if dh2 % 2 == 0 else PS[5]
                for pp in range(3):
                    for dc in range(DC):
                        lh = (mh if pp < 2 else ml_).rearrange("p (a c) -> p a c", a=DC)[:, dc, :][:, qo:qo + 128]
                        rh = (wph if pp != 1 else wpl)[:, dc * D + dh2 * NQ:dc * D + (dh2 + 1) * NQ]
                        first = (pp == 0 and dc == 0); last = (pp == 2 and dc == DC - 1)
                        p.do("tensor", lambda e, o=pso, l=lh, r=rh, f=first, s=last: e.matmul(
                            o[:, 0:NQ], l, r, start=f, stop=s), sem="mm" if last else None)
            x1t = Em[:, 0:D]
            p.wait("sync", "vc")
            ld(x1t, x1sp.ap()[qt * 128:(qt + 1) * 128, :], sem="ld")
            p.wait("vector", "mm")
            p.wait("vector", "ld")
            xn1t = Em[:, 1024:1024 + D] if D <= 1024 else Em[:, D:2 * D]
            p.do("vector", lambda e, x=x1t, o=xn1t, qt=qt: e.tensor_scalar(
                o, x, MU1[:, qt:qt + 1], RS1[:, qt:qt + 1], ALU.subtract, ALU.mult), sem="vc")
            for dh2 in range(D // NQ):
                pso = PS[4] if dh2 % 2 == 0 else PS[5]
                st_ = Em2[:, 0:NQ]
                p.do("vector", lambda e, o=st_, i=pso, dh2=dh2: e.tensor_tensor(
                    o, i[:, 0:NQ], bpb[:, dh2 * NQ:(dh2 + 1) * NQ], ALU.add))
                p.do("vector", lambda e, o=st_, z=xn1t, dh2=dh2: e.tensor_tensor(
                    o, o, z[:, dh2 * NQ:(dh2 + 1) * NQ], ALU.add))
                p.do("vector", lambda e, o=st_, g=A, qt=qt, dh2=dh2: e.tensor_tensor(
                    o, o, g[:, qt * D + dh2 * NQ:qt * D + (dh2 + 1) * NQ], ALU.mult))
                p.do("vector", lambda e, o=st_, x=x1t, dh2=dh2: e.tensor_tensor(
                    o, o, x[:, dh2 * NQ:(dh2 + 1) * NQ], ALU.add), sem="vc")
                p.wait("scalar", "vc")
                p.dma("scalar", lambda e, s=st_, qt=qt, dh2=dh2: e.dma_start(
                    out=out_d[qt * 128:(qt + 1) * 128, dh2 * NQ:(dh2 + 1) * NQ], in_=s), sem="vld")
                p.wait("vector", "vld")

    p.wait("gpsimd", "vld")
    p.wait("gpsimd", "st")
    p.wait("gpsimd", "gdma")
    p.wait("sync", "vld")

    with nc.Block() as block:
        for name in ("sync", "tensor", "vector", "scalar", "gpsimd"):
            def make_body(nm):
                def body(eng):
                    for fn in p.items[nm]:
                        fn(eng)
                return body
            getattr(block, name)(make_body(name))

    es.close()
    return nc


_NC_CACHE = {}


def _get_nc(key, cfg):
    if key not in _NC_CACHE:
        _NC_CACHE[key] = build_nc(cfg)
    return _NC_CACHE[key]


def make_in_maps(I, cfg):
    T, D, HF, NH, E, NC = cfg["T"], cfg["D"], cfg["HF"], cfg["NH"], cfg["E"], cfg["NC"]
    N = NC * T
    x = np.ascontiguousarray(I["L0_attn_x"], dtype=np.float32)
    xf = np.ascontiguousarray(x.reshape(N, D))
    c = np.ascontiguousarray
    maps = []
    for ci in range(NC):
        m = dict(
            xfull=xf,
            xown=c(x[ci]),
            coff=np.full((128, 1), float(ci * T), dtype=np.float32),
            wq=c(I["L0_attn_Wq"]), wk=c(I["L0_attn_Wk"]), wv=c(I["L0_attn_Wv"]),
            bq=c(I["L0_attn_bq"]), bk=c(I["L0_attn_bk"]), bv=c(I["L0_attn_bv"]),
            bvr=c(I["L0_attn_bv"].reshape(1, D)),
            rw=c(I["L0_ffn_router_w"]), rb=c(I["L0_ffn_router_b"].reshape(E, 1)),
            wx=c(I["L0_ffn_W"][ci].reshape(D // 128, 128, HF // 128, 128).transpose(2, 1, 0, 3)),
            vx=c(I["L0_ffn_V"][ci].reshape(D // 128, 128, HF // 128, 128).transpose(2, 1, 0, 3)),
            wox=c(I["L0_ffn_W_out"][ci]),
            lnw=c(I["L1_attn_ln_w"].reshape(1, D)), lnb=c(I["L1_attn_ln_b"].reshape(1, D)),
            wqkv=c(I["L1_attn_W_qkv"].reshape(D // 128, 128, 3, NH // 2, 2 * D // NH).transpose(2, 3, 1, 0, 4).reshape(3 * NH // 2, 128, D // 128, 2 * D // NH)),
            bqkv=c(I["L1_attn_b_qkv"]),
            wproj=c(I["L1_attn_W_proj"]), bproj=c(I["L1_attn_b_proj"].reshape(1, D)),
            wg=c(I["L1_ffn_Wg"]), weo=c(I["L1_ffn_We"][ci]), be=c(I["L1_ffn_be"]),
        )
        maps.append(m)
    return maps


def kernel(**inputs):
    cfg = FULL_CFG
    nc = _get_nc("full", cfg)
    in_maps = make_in_maps(inputs, cfg)
    res = run_bass_kernel_spmd(nc, in_maps, core_ids=list(range(cfg["NC"])))
    out = np.stack([np.asarray(res.results[ci]["out"]) for ci in range(cfg["NC"])], axis=0)
    return out.astype(np.float32)


# version 20 (final)
# speedup vs baseline: 1.3331x; 1.0245x over previous
import os
import numpy as np
from contextlib import ExitStack

import concourse.bass as bass
import concourse.mybir as mybir
from concourse.bass import IndirectOffsetOnAxis
from concourse.bass_utils import run_bass_kernel_spmd
import ml_dtypes

F32 = mybir.dt.float32
BF16 = mybir.dt.bfloat16
U32 = mybir.dt.uint32
AX = mybir.AxisListType.X
ALU = mybir.AluOpType
ACT = mybir.ActivationFunctionType

FULL_CFG = dict(T=2048, D=1024, HF=2048, NH=16, E=8, NC=8)


class Prog:
    def __init__(self, nc):
        self.nc = nc
        self.items = {k: [] for k in ("sync", "tensor", "vector", "scalar", "gpsimd")}
        self.sems = {}
        self.cnt = {}

    def sem(self, name, es):
        self.sems[name] = es.enter_context(self.nc.semaphore(name))
        self.cnt[name] = 0

    def do(self, eng, fn, sem=None, n=1):
        if eng == "vector":
            hv = self.sems["vc"]
            prior = self.cnt["vc"]
            self.items[eng].append(lambda e, h=hv, v=prior: e.wait_ge(h, v))
            if sem is None or sem == "vc":
                sem, n = "vc", 1
            else:
                self.cnt["vc"] += 1
                self.cnt[sem] += n
                h2 = self.sems[sem]

                def wrapped2(e, fn=fn, h2=h2, n=n, hv=hv):
                    fn(e).then_inc(h2, n).then_inc(hv, 1)

                self.items[eng].append(wrapped2)
                return
        if sem is not None:
            self.cnt[sem] += n
            h = self.sems[sem]

            def wrapped(e, fn=fn, h=h, n=n):
                fn(e).then_inc(h, n)

            self.items[eng].append(wrapped)
        else:
            self.items[eng].append(fn)

    QSEM = {"sync": "qs", "scalar": "qa", "gpsimd": "qg"}

    def dma(self, eng, fn, sem=None, handshake=True):
        q = self.QSEM[eng]
        self.cnt[q] += 16
        hq = self.sems[q]
        qv = self.cnt[q]
        self.items[eng].append(lambda e, fn=fn, hq=hq: fn(e).then_inc(hq, 16))
        if handshake:
            self.qsync(eng, sem)

    def qsync(self, eng, sem):
        q = self.QSEM[eng]
        hq = self.sems[q]
        qv = self.cnt[q]
        self.cnt[sem] += 1
        hs = self.sems[sem]
        self.items[eng].append(lambda e, hq=hq, qv=qv, hs=hs: e.wait_ge(hq, qv).then_inc(hs, 1))

    def wait(self, eng, sem, val=None):
        h = self.sems[sem]
        v = self.cnt[sem] if val is None else val
        self.items[eng].append(lambda e, h=h, v=v: e.wait_ge(h, v))

    def v(self, sem):
        return self.cnt[sem]


def build_nc(cfg):
    T, D, HF, NH, E, NC = cfg["T"], cfg["D"], cfg["HF"], cfg["NH"], cfg["E"], cfg["NC"]
    DC = D // 128
    TT = T // 128
    N = NC * T
    CAP = T
    DH = D // NH
    HC = HF // 128
    A_ = N // 128
    NQ = 512 if T >= 512 else T
    QG = T // NQ
    L0SC = 1.0 / float(np.sqrt(D))
    L1SC = 1.0 / float(np.sqrt(DH))

    nc = bass.Bass("TRN2", target_bir_lowering=False)

    def din(name, shape, dt=F32):
        return nc.dram_tensor(name, list(shape), dt, kind="ExternalInput").ap()

    xfull = din("xfull", (N, D))
    xown = din("xown", (T, D))
    coff = din("coff", (128, 1))
    wq_d = din("wq", (D, D)); wk_d = din("wk", (D, D)); wv_d = din("wv", (D, D))
    bq_d = din("bq", (D,)); bk_d = din("bk", (D,)); bv_d = din("bv", (D,))
    rw_d = din("rw", (E, D)); rb_d = din("rb", (E, 1))
    wx_d = din("wx", (HF // 128, 128, D // 128, 128)); vx_d = din("vx", (HF // 128, 128, D // 128, 128))
    wox_d = din("wox", (HF, D))
    lnw_d = din("lnw", (1, D)); lnb_d = din("lnb", (1, D))
    wqkv_d = din("wqkv", (3 * NH // 2, 128, D // 128, 2 * D // NH)); bqkv_d = din("bqkv", (3 * D,))
    wproj_d = din("wproj", (D, D)); bproj_d = din("bproj", (1, D))
    wg_d = din("wg", (E, D)); weo_d = din("weo", (D, D)); be_d = din("be", (E, D))
    bvr_d = din("bvr", (1, D))
    out_d = nc.dram_tensor("out", [T, D], F32, kind="ExternalOutput").ap()

    rt_in = nc.dram_tensor("rt_in", [T, 16], F32)
    rt_all = nc.dram_tensor("rt_all", [N, 16], F32)
    y_in = nc.dram_tensor("y_in", [T, D], F32)
    y_all = nc.dram_tensor("y_all", [N, D], F32)
    wet_in = nc.dram_tensor("wet_in", [D, 2 * D], BF16)
    wet_all = nc.dram_tensor("wet_all", [E * D, 2 * D], BF16)
    DBGV = os.environ.get("DBG_LB") == "1"
    Lb = nc.dram_tensor("Lb", [N + 2 * T, 1], U32)
    Lpart = nc.dram_tensor("Lpart", [N + 2 * T, 1], U32)
    rtw_d = nc.dram_tensor("rtw", [N, 4], F32)
    if DBGV:
        dbgLb = nc.dram_tensor("dbgLb", [N + 2 * T, 1], U32, kind="ExternalOutput")
        dbgRtw = nc.dram_tensor("dbgRtw", [N, 4], F32, kind="ExternalOutput")
        dbgY = nc.dram_tensor("dbgY", [N, D], F32, kind="ExternalOutput")
        dbgS = nc.dram_tensor("dbgS", [T, D], F32, kind="ExternalOutput")
        dbgX1 = nc.dram_tensor("dbgX1", [T, D], F32, kind="ExternalOutput")
        dbgB = nc.dram_tensor("dbgB", [1, 4096], F32, kind="ExternalOutput")
        dbgVs = nc.dram_tensor("dbgVs", [T, D], F32, kind="ExternalOutput")
        dbgKh = nc.dram_tensor("dbgKh", [D, T], BF16, kind="ExternalOutput")
        dbgKl = nc.dram_tensor("dbgKl", [D, T], BF16, kind="ExternalOutput")
    rtd_d = nc.dram_tensor("rtd", [N, 2], U32)
    ksp_h = nc.dram_tensor("ksp_h", [D, T], BF16)
    ksp_l = nc.dram_tensor("ksp_l", [D, T], BF16)
    vsp = nc.dram_tensor("vsp", [T, D], F32)
    x1sp = nc.dram_tensor("x1sp", [T, D], F32)
    ssp = nc.dram_tensor("ssp", [T, D], F32)
    msp = nc.dram_tensor("msp", [D, T], F32)
    bcb = nc.dram_tensor("bcb", [1, 4096], F32)

    idf = nc.inline_tensor(np.eye(128, dtype=np.float32), name="idf").ap()
    idb = nc.inline_tensor(np.eye(128).astype(ml_dtypes.bfloat16), name="idb").ap()
    utS = nc.inline_tensor(np.triu(np.ones((128, 128), np.float32), 1), name="utS").ap()
    on1 = nc.inline_tensor(np.ones((128, 1), np.float32), name="on1").ap()
    op2_np = np.zeros((128, 2), np.float32)
    op2_np[0:DH, 0] = 1.0
    op2_np[DH:2 * DH, 1] = 1.0
    op2 = nc.inline_tensor(op2_np, name="op2").ap()
    io8 = nc.inline_tensor(np.arange(E, dtype=np.float32).reshape(1, E), name="io8").ap()
    trm = nc.inline_tensor((np.arange(128) % (128 // NC) * 2.0 * A_).astype(np.float32).reshape(128, 1),
                           name="trm").ap()

    es = ExitStack()
    sb = lambda name, shape, dt=F32: es.enter_context(nc.sbuf_tensor(name, list(shape), dt))
    psf = lambda name: es.enter_context(nc.psum_tensor(name, [128, 512], F32))

    TCH = 512 if T >= 512 else T
    A = sb("A", (128, TT * D))
    B = sb("B", (128, max(TT * D, 100 * A_)))
    C = sb("C", (128, max(TT * NQ, HC * TCH, DC * DC * 128, DC * D, T + TT * 132 + (NH // 2) * NQ)))
    Dm = sb("Dm", (128, 4096 if D >= 1024 else 4 * D))
    Em = sb("Em", (128, 2048))
    Em2 = sb("Em2", (128, 2048))
    Mm = sb("Mm", (128, 960))
    RTS = sb("RTS", (128, 6 * A_))
    K2 = sb("K2", (128, 1664))
    K2u = K2[:].bitcast(U32)
    DS = K2u[:, 973:973 + 256]
    IDS = K2u[:, 685:685 + A_]
    IDT = K2u[:, 813:813 + 128]
    OWN = K2u[:, 941:941 + TT]
    LBT = K2u[:, 957:957 + TT]
    SMALL = K2[:, 1229:1229 + 256]

    PS = [psf(f"PS{i}") for i in range(8)]
    PTv = PS[7][:].bitcast(BF16)

    p = Prog(nc)
    for s in ("ld", "ld2", "st", "vld", "gdma", "ccw", "ccr", "ccy", "ccl", "vc", "sc", "mm", "gc",
              "qs", "qa", "qg"):
        p.sem(s, es)

    def bc16(t, c0_f32, n_bf16):
        return t[:].bitcast(BF16)[:, 2 * c0_f32:2 * c0_f32 + n_bf16]

    def u32v(t, c0, n):
        return t[:].bitcast(U32)[:, c0:c0 + n]

    def split_hilo(src_ap, hi_ap, lo_ap):
        p.wait("scalar", "mm")
        p.wait("scalar", "vc")
        p.do("scalar", lambda e, s=src_ap, h=hi_ap: e.activation(h, s, ACT.Copy), sem="sc")
        p.wait("vector", "sc")
        p.do("vector", lambda e, s=src_ap, h=hi_ap, l=lo_ap: e.tensor_sub(l, s, h), sem="vc")

    def ld(dst, src, sem="ld", eng="sync", slow=False):
        p.dma(eng, lambda e, d=dst, s=src, sl=slow: e.dma_start(out=d, in_=s, allow_slow_non_contiguous=sl), sem=sem)

    hp2_ = 2 * (D // NH)
    K2b = K2[:].bitcast(BF16)
    idf_s = K2[:, 0:128]
    utS_s = K2[:, 128:256]
    idb_s = K2b[:, 512:640]
    on1_s = K2[:, 320:321]
    op2_s = K2[:, 321:323]
    bq_s = K2[:, 323:323 + DC]; bk_s = K2[:, 331:331 + DC]
    bqkv_s = K2[0:hp2_, 469:469 + 3 * D // hp2_]
    rb_s = K2[0:E, 339:340]
    coff_s = K2[:, 340:341]
    rws_st = K2[0:E, 341:469]
    rwT_h = K2b[:, 1114:1114 + DC * E]; rwT_l = K2b[:, 1178:1178 + DC * E]
    wgT_h = K2b[:, 1242:1242 + DC * E]; wgT_l = K2b[:, 1306:1306 + DC * E]
    rwT_f = K2[:, 493:493 + DC * E]

    ld(idf_s, idf)
    ld(utS_s, utS)
    ld(on1_s, on1)
    ld(op2_s, op2)
    ld(idb_s, idb, sem="gdma", eng="gpsimd")
    ld(bq_s[:], bq_d.rearrange("(a p) -> p a", p=128), slow=True)
    ld(bk_s[:], bk_d.rearrange("(a p) -> p a", p=128), slow=True)
    ld(bqkv_s[:], bqkv_d.rearrange("(a p) -> p a", p=hp2_), slow=True)
    ld(rb_s[:], rb_d)
    ld(coff_s, coff)
    io8_bc = SMALL[:, 32:32 + E]
    ld(io8_bc, io8.broadcast_to((128, E)))
    ld(SMALL[:, 225:226], trm)

    p.wait("tensor", "ld")
    for wsrc, dsth, dstl in ((rw_d, rwT_h, rwT_l), (wg_d, wgT_h, wgT_l)):
        for dc in range(DC):
            p.wait("sync", "mm")
            ld(rws_st, wsrc[:, dc * 128:(dc + 1) * 128])
            p.wait("tensor", "ld")
            p.do("tensor", lambda e, dc=dc: e.transpose(
                PS[5][:, dc * E:(dc + 1) * E], rws_st[0:E, :], idf_s[0:E, 0:E]),
                sem="mm" if True else None)
        p.wait("vector", "mm")
        p.do("vector", lambda e: e.tensor_copy(rwT_f[:, 0:DC * E], PS[5][:, 0:DC * E]), sem="vc")
        split_hilo(rwT_f[:, 0:DC * E], dsth[:, 0:DC * E], dstl[:, 0:DC * E])
        p.wait("tensor", "vc")

    p.do("gpsimd", lambda e: e.iota(IDS, pattern=[[1, A_]], base=0, channel_multiplier=A_,
                                    allow_small_or_imprecise_dtypes=True), sem="gc")
    p.do("gpsimd", lambda e: e.iota(SMALL[:, 200:200 + TT], pattern=[[128, TT]], base=0, channel_multiplier=1,
                                    allow_small_or_imprecise_dtypes=True), sem="gc")
    p.wait("vector", "gc")
    p.wait("vector", "ld")
    p.do("vector", lambda e: e.tensor_scalar(SMALL[:, 200:200 + TT], SMALL[:, 200:200 + TT], coff_s[:, 0:1], None, ALU.add), sem="vc")
    p.do("vector", lambda e: e.tensor_copy(OWN, SMALL[:, 200:200 + TT]), sem="vc")
    nlb = (N + 2 * T) // 128
    p.do("vector", lambda e: e.memset(u32v(Mm, 800, nlb), 0), sem="vc")
    p.wait("sync", "vc")
    p.dma("sync", lambda e: e.dma_start(out=Lpart.ap().rearrange("(p a) o -> p (a o)", p=128),
                                       in_=u32v(Mm, 800, nlb)), sem="st")

    p.wait("tensor", "gdma")
    for oc in range(DC):
        wof = Dm[:, 0:D]
        p.wait("sync", "vc")
        p.wait("sync", "mm")
        ld(wof, weo_d[oc * 128:(oc + 1) * 128, :])
        woh = bc16(Dm, D, D); wol = bc16(Dm, D + D // 2, D)
        p.wait("scalar", "ld")
        split_hilo(wof, woh, wol)
        for half, src in ((0, woh), (1, wol)):
            p.wait("tensor", "vc")
            for dc in range(DC):
                p.do("tensor", lambda e, s=src, dc=dc: e.transpose(
                    PTv[:, dc * 128:(dc + 1) * 128], s[:, dc * 128:(dc + 1) * 128], idb_s),
                    sem="mm" if dc == DC - 1 else None)
            p.wait("vector", "mm")
            dst = bc16(C, 0, DC * 2 * D).rearrange("p (a c) -> p a c", a=DC)[:, :, half * D + oc * 128:half * D + (oc + 1) * 128]
            p.do("vector", lambda e, d=dst: e.tensor_copy(
                d, PTv[:, 0:DC * 128].rearrange("p (a c) -> p a c", a=DC)), sem="vc")
    p.wait("scalar", "vc")
    p.dma("scalar", lambda e: e.dma_start(out=wet_in.ap().rearrange("(a p) c -> p a c", p=128),
                                          in_=bc16(C, 0, DC * 2 * D).rearrange("p (a c) -> p a c", a=DC)), sem="vld")
    p.wait("gpsimd", "vld")
    p.do("gpsimd", lambda e: e.collective_compute(
        "AllGather", ALU.bypass, replica_groups=[list(range(NC))],
        ins=[wet_in.ap()], outs=[wet_all.ap()]), sem="ccw")

    MU = Mm[:, 0:TT]; RS = Mm[:, TT:2 * TT]
    MU1 = Mm[:, 2 * TT:3 * TT]; RS1 = Mm[:, 3 * TT:4 * TT]
    MU2 = Mm[:, 4 * TT:5 * TT]; RS2 = Mm[:, 5 * TT:6 * TT]
    CW = Mm[:, 6 * TT:6 * TT + TT * E]
    negc0 = SMALL[:, 0:1]
    negcH = SMALL[:, 4:4 + 2 * ((NH + 1) // 2)]

    xnh = lambda dc: bc16(B, 0, 2 * DC * T)[:, dc * T:(dc + 1) * T]
    xnl = lambda dc: bc16(B, 0, 2 * DC * T)[:, DC * T + dc * T:DC * T + (dc + 1) * T]

    NCH = (D + 511) // 512
    def stats_pass(xt, MUc, RSc, j):
        p.wait("vector", "vld")
        bst = Em[:, 0:6 * NCH]; bag = Em[:, 8 * NCH:8 * NCH + 2]
        for ch in range(NCH):
            p.do("vector", lambda e, x=xt, ch=ch: e.bn_stats(
                Em[:, 6 * ch:6 * ch + 6], x[:, 512 * ch:min(512 * (ch + 1), D)]))
        p.do("vector", lambda e, b=bst, a=bag: e.bn_aggr(a, b))
        p.do("vector", lambda e, a=bag, j=j, M=MUc: e.tensor_copy(M[:, j:j + 1], a[:, 0:1]))
        p.do("vector", lambda e, a=bag: e.tensor_single_scalar(Em[:, 8 * NCH + 4:8 * NCH + 5], a[:, 1:2], 1e-5, ALU.add), sem="vc")
        p.wait("scalar", "vc")
        p.do("scalar", lambda e: e.activation(Em[:, 8 * NCH + 5:8 * NCH + 6], Em[:, 8 * NCH + 4:8 * NCH + 5], ACT.Sqrt), sem="sc")
        p.wait("vector", "sc")
        p.do("vector", lambda e, j=j, R=RSc: e.reciprocal(R[:, j:j + 1], Em[:, 8 * NCH + 5:8 * NCH + 6]), sem="vc")

    def split_transpose_to(dst_arena, j, src_f32):
        hi_t = bc16(Em2, 0, D); lo_t = bc16(Em2, D // 2, D)
        split_hilo(src_f32, hi_t, lo_t)
        for src, base in ((hi_t, 0), (lo_t, DC * T)):
            p.wait("tensor", "vc")
            for dc in range(DC):
                p.do("tensor", lambda e, s=src, dc=dc: e.transpose(
                    PTv[:, dc * 128:(dc + 1) * 128], s[:, dc * 128:(dc + 1) * 128], idb_s),
                    sem="mm" if dc == DC - 1 else None)
            p.wait("vector", "mm")
            dstap = bc16(dst_arena, 0, 2 * DC * T).rearrange("p (c t) -> p c t", c=2 * DC)[:, base // T:base // T + DC, j * 128:(j + 1) * 128]
            p.do("vector", lambda e, d=dstap: e.tensor_copy(d, PTv[:, 0:DC * 128].rearrange("p (c t) -> p c t", c=DC)), sem="vc")
        p.wait("tensor", "vc")

    p.wait("vector", "vld")
    for j in range(TT):
        xt = A[:, j * D:(j + 1) * D]
        ld(xt, xown[j * 128:(j + 1) * 128, :])
        p.wait("vector", "ld")
        stats_pass(xt, MU, RS, j)
        xnt = Em[:, 1024:1024 + D] if D <= 1024 else Em[:, 0:D]
        p.do("vector", lambda e, x=xt, o=xnt, j=j: e.tensor_scalar(o, x, MU[:, j:j + 1], RS[:, j:j + 1], ALU.subtract, ALU.mult), sem="vc")
        split_transpose_to(B, j, xnt)

    LgF = Em2[0:E, 0:T]
    for ts in range(T // NQ):
        p.wait("tensor", "vc")
        for pp in range(3):
            for dc in range(DC):
                lh = (rwT_h if pp < 2 else rwT_l)[:, dc * E:(dc + 1) * E]
                rh = (xnh(dc) if pp != 1 else xnl(dc))[:, ts * NQ:(ts + 1) * NQ]
                first = (pp == 0 and dc == 0); last = (pp == 2 and dc == DC - 1)
                p.do("tensor", lambda e, l=lh, r=rh, f=first, s=last: e.matmul(
                    PS[6][0:E, 0:NQ], l, r, start=f, stop=s), sem="mm" if last else None)
        p.wait("vector", "mm")
        p.do("vector", lambda e, ts=ts: e.tensor_scalar(
            LgF[:, ts * NQ:(ts + 1) * NQ], PS[6][0:E, 0:NQ], rb_s[0:E, 0:1], None, ALU.add), sem="vc")
    p.wait("tensor", "vc")
    for j in range(TT):
        p.wait("tensor", "vc")
        p.do("tensor", lambda e, j=j: e.transpose(PS[5][:, 0:E], LgF[:, j * 128:(j + 1) * 128], idf_s[0:E, 0:E]), sem="mm")
        p.wait("vector", "mm")
        p.wait("vector", "vld", p.v("vld"))
        stg0 = Em[:, 0:16]
        p.do("vector", lambda e, s=stg0: e.tensor_copy(s[:, 0:E], PS[5][:, 0:E]))
        p.do("vector", lambda e, j=j, s=stg0: e.tensor_copy(s[:, E:E + 1], MU[:, j:j + 1]))
        p.do("vector", lambda e, j=j, s=stg0: e.tensor_copy(s[:, E + 1:E + 2], RS[:, j:j + 1]))
        p.do("vector", lambda e, s=stg0: e.memset(s[:, E + 2:16], 0.0), sem="vc")
        p.wait("scalar", "vc")
        p.dma("scalar", lambda e, j=j, s=stg0: e.dma_start(out=rt_in.ap()[j * 128:(j + 1) * 128, :], in_=s), sem="vld")
    p.wait("gpsimd", "vld")
    p.do("gpsimd", lambda e: e.collective_compute(
        "AllGather", ALU.bypass, replica_groups=[list(range(NC))],
        ins=[rt_in.ap()], outs=[rt_all.ap()]), sem="ccr")

    def load_w_full(wd):
        for dc in range(DC):
            blk = Dm[:, 0:D]
            p.wait("sync", "vc")
            ld(blk, wd[dc * 128:(dc + 1) * 128, :])
            p.wait("scalar", "ld")
            hi = C[:].bitcast(BF16)[:, dc * D:(dc + 1) * D]
            lo = C[:].bitcast(BF16)[:, DC * D + dc * D:DC * D + (dc + 1) * D]
            split_hilo(blk, hi, lo)
        p.wait("tensor", "vc")

    def wch(hi_side, oc, dc):
        base = (0 if hi_side else DC * D) + dc * D + oc * 128
        return C[:].bitcast(BF16)[:, base:base + 128]


    for wi, wd in enumerate((wq_d, wk_d, wv_d)):
        load_w_full(wd)
        if wi < 2:
            for ts in range(T // NQ):
                for oc in range(DC):
                    p.wait("tensor", "vc")
                    p.wait("vector", "vld", p.v("vld"))
                    pso = PS[0] if oc % 2 == 0 else PS[1]
                    for pp in range(3):
                        for dc in range(DC):
                            lh = wch(pp < 2, oc, dc)
                            rh = (xnh(dc) if pp != 1 else xnl(dc))[:, ts * NQ:(ts + 1) * NQ]
                            first = (pp == 0 and dc == 0); last = (pp == 2 and dc == DC - 1)
                            p.do("tensor", lambda e, o=pso, l=lh, r=rh, f=first, s=last: e.matmul(
                                o[:, 0:NQ], l, r, start=f, stop=s), sem="mm" if last else None)
                    p.wait("vector", "mm")
                    q32 = Em[:, 0:NQ]
                    bias = bq_s if wi == 0 else bk_s
                    p.do("vector", lambda e, o=q32, i=pso, b=bias, oc=oc: e.tensor_scalar(
                        o, i[:, 0:NQ], b[:, oc:oc + 1], None, ALU.add), sem="vc")
                    hi_t = bc16(Em, NQ, NQ); lo_t = bc16(Em, NQ + NQ // 2, NQ)
                    split_hilo(q32, hi_t, lo_t)
                    if wi == 0:
                        dh_ = bc16(A, 0, 2 * DC * T)[:, oc * T + ts * NQ:oc * T + (ts + 1) * NQ]
                        dl_ = bc16(A, 0, 2 * DC * T)[:, DC * T + oc * T + ts * NQ:DC * T + oc * T + (ts + 1) * NQ]
                        p.do("vector", lambda e, d=dh_, s=hi_t: e.tensor_copy(d, s))
                        p.do("vector", lambda e, d=dl_, s=lo_t: e.tensor_copy(d, s), sem="vc")
                    else:
                        p.wait("scalar", "vc")
                        p.dma("scalar", lambda e, s=hi_t, oc=oc, ts=ts: e.dma_start(
                            out=ksp_h.ap()[oc * 128:(oc + 1) * 128, ts * NQ:(ts + 1) * NQ], in_=s), sem="vld")
                        p.wait("scalar", "vc")
                        p.dma("scalar", lambda e, s=lo_t, oc=oc, ts=ts: e.dma_start(
                            out=ksp_l.ap()[oc * 128:(oc + 1) * 128, ts * NQ:(ts + 1) * NQ], in_=s), sem="vld")
                    sq = Em[:, 1536:1536 + NQ] if NQ <= 512 else Em[:, NQ:2 * NQ]
                    p.do("scalar", lambda e, s=sq, i=q32: e.activation(s, i, ACT.Square), sem="sc")
                    p.wait("tensor", "sc")
                    p.do("tensor", lambda e, s=sq, f=(oc == 0), l=(oc == DC - 1): e.matmul(
                        PS[6][0:1, 0:NQ], on1_s, s, start=f, stop=l), sem="mm" if oc == DC - 1 else None)
                p.wait("vector", "mm")
                ncol = 16 if wi == 0 else 17
                p.do("vector", lambda e, c=ncol: e.reduce_max(SMALL[0:1, 64 + c:65 + c], PS[6][0:1, 0:NQ], axis=AX))
                if ts == 0:
                    p.do("vector", lambda e, c=ncol: e.tensor_copy(SMALL[0:1, c:c + 1], SMALL[0:1, 64 + c:65 + c]), sem="vc")
                else:
                    p.do("vector", lambda e, c=ncol: e.tensor_tensor(SMALL[0:1, c:c + 1], SMALL[0:1, c:c + 1], SMALL[0:1, 64 + c:65 + c], ALU.max), sem="vc")
        else:
            bv_bc = Em2[:, D:2 * D] if D <= 1024 else Em2[:, 0:D]
            ld(bv_bc, bvr_d.broadcast_to((128, D)), sem="ld2")
            p.wait("vector", "ld2")
            for j in range(TT):
                for half in range(D // NQ):
                    p.wait("tensor", "vc")
                    p.wait("vector", "vld", p.v("vld"))
                    pso = PS[2] if half % 2 == 0 else PS[3]
                    for ri in range(NQ // 128):
                        oc2 = half * (NQ // 128) + ri
                        for pp in range(3):
                            for dc in range(DC):
                                lh = (xnh(dc) if pp != 2 else xnl(dc))[:, j * 128:(j + 1) * 128]
                                rh = wch(pp != 1, oc2, dc)
                                first = (pp == 0 and dc == 0); last = (pp == 2 and dc == DC - 1)
                                p.do("tensor", lambda e, o=pso, l=lh, r=rh, f=first, s=last, ri=ri: e.matmul(
                                    o[:, ri * 128:(ri + 1) * 128], l, r, start=f, stop=s),
                                    sem="mm" if last and ri == NQ // 128 - 1 else None)
                    p.wait("vector", "mm")
                    vv = Em[:, 0:NQ]
                    p.do("vector", lambda e, o=vv, i=pso, h=half, b=bv_bc: e.tensor_add(
                        o, i[:, 0:NQ], b[:, h * NQ:(h + 1) * NQ]), sem="vc")
                    p.wait("scalar", "vc")
                    p.dma("scalar", lambda e, s=vv, j=j, h=half: e.dma_start(
                        out=vsp.ap()[j * 128:(j + 1) * 128, h * NQ:(h + 1) * NQ], in_=s), sem="vld")

    p.wait("scalar", "vc")
    p.do("scalar", lambda e: e.activation(SMALL[0:1, 18:19], SMALL[0:1, 16:17], ACT.Sqrt))
    p.do("scalar", lambda e: e.activation(SMALL[0:1, 19:20], SMALL[0:1, 17:18], ACT.Sqrt), sem="sc")
    p.wait("vector", "sc")
    p.do("vector", lambda e: e.tensor_tensor(SMALL[0:1, 20:21], SMALL[0:1, 18:19], SMALL[0:1, 19:20], ALU.mult))
    p.do("vector", lambda e: e.tensor_single_scalar(SMALL[0:1, 21:22], SMALL[0:1, 20:21], -L0SC, ALU.mult), sem="vc")
    p.wait("scalar", "vc")
    p.dma("scalar", lambda e: e.dma_start(out=bcb.ap()[0:1, 0:1], in_=SMALL[0:1, 21:22]), sem="vld")
    p.wait("sync", "vld")
    ld(negc0, bcb.ap()[0:1, 0:1].broadcast_to((128, 1)), sem="ld2")

    p.wait("sync", "ccr")
    p.wait("sync", "mm")
    RAW = B[:, 0:16 * A_]
    ld(RAW, rt_all.ap().rearrange("(p a) c -> p (a c)", p=128), sem="ld2")
    RAW3 = RAW.rearrange("p (a c) -> p a c", c=16)
    LG = RAW3[:, :, 0:E]
    co = [16 * A_]

    def arr(n):
        r = B[:, co[0]:co[0] + n]
        co[0] += n
        return r

    V0 = arr(A_); V1 = arr(A_)
    OH0 = arr(A_ * E); OH1 = arr(A_ * E); LM = arr(A_ * E)
    HTEA = arr(A_ * E); PRF = arr(A_ * E)
    BS = arr(E)
    P0s = arr(A_); P1s = arr(A_); E0v = arr(A_); E1v = arr(A_)
    W0 = arr(A_); W1 = arr(A_); VA0 = arr(A_); VA1 = arr(A_)
    DC0 = arr(A_); DC1 = arr(A_); DSS0 = arr(A_); DSS1 = arr(A_)
    T1 = arr(A_); T2 = arr(A_ * E)
    TR0 = arr(A_); TR1 = arr(A_)

    d3 = lambda t: t.rearrange("p (a c) -> p a c", c=E)
    dea = lambda t: t.rearrange("p (c a) -> p c a", c=E)
    c1 = lambda t: t.rearrange("p (a o) -> p a o", o=1)
    V = "vector"

    p.wait("gpsimd", "mm")
    p.do("gpsimd", lambda e: e.iota(TR0, pattern=[[2, A_]], base=N, channel_multiplier=0,
                                    allow_small_or_imprecise_dtypes=True), sem="gc")
    p.wait(V, "ld2")
    p.wait(V, "gc")
    p.do(V, lambda e: e.tensor_scalar(TR0, TR0, SMALL[:, 225:226], None, ALU.add))
    p.do(V, lambda e: e.tensor_single_scalar(TR1, TR0, 1.0, ALU.add), sem="vc")
    p.do(V, lambda e: e.reduce_max(c1(V0), LG, axis=AX))
    p.do(V, lambda e: e.tensor_tensor(d3(OH0), LG, c1(V0).broadcast_to((128, A_, E)), ALU.is_equal))
    p.do(V, lambda e: e.scalar_tensor_tensor(d3(LM), d3(OH0), -1e30, LG, ALU.mult, ALU.add))
    p.do(V, lambda e: e.reduce_max(c1(V1), d3(LM), axis=AX))
    p.do(V, lambda e: e.tensor_tensor(d3(OH1), d3(LM), c1(V1).broadcast_to((128, A_, E)), ALU.is_equal))
    p.do(V, lambda e: e.tensor_tensor(dea(HTEA).rearrange("p c a -> p a c"), d3(OH0), d3(OH1), ALU.add))
    for c_ in range(E):
        p.do(V, lambda e, c_=c_: e.tensor_tensor_scan(
            dea(PRF)[:, c_, :], dea(HTEA)[:, c_, :], dea(HTEA)[:, c_, :], 0.0, ALU.add, ALU.bypass),
            sem="vc" if c_ == E - 1 else None)
    p.wait("tensor", "vc")
    p.do("tensor", lambda e: e.matmul(PS[6][:, 0:E], utS_s, dea(PRF)[:, :, A_ - 1], start=True, stop=True), sem="mm")
    p.wait(V, "mm")
    p.do(V, lambda e: e.tensor_copy(BS, PS[6][:, 0:E]))
    p.do(V, lambda e: e.tensor_tensor(dea(PRF), dea(PRF),
                                      BS.rearrange("p (c o) -> p c o", o=1).broadcast_to((128, E, A_)), ALU.add))
    for OH, Ps_, Ev_ in ((OH0, P0s, E0v), (OH1, P1s, E1v)):
        p.do(V, lambda e, OH=OH: e.tensor_tensor(d3(T2), d3(OH), dea(PRF).rearrange("p c a -> p a c"), ALU.mult))
        p.do(V, lambda e, Ps_=Ps_: e.reduce_sum(c1(Ps_), d3(T2), axis=AX))
        p.do(V, lambda e, OH=OH: e.tensor_tensor(d3(T2), d3(OH),
             io8_bc.rearrange("p (o c) -> p o c", o=1).broadcast_to((128, A_, E)), ALU.mult))
        p.do(V, lambda e, Ev_=Ev_: e.reduce_sum(c1(Ev_), d3(T2), axis=AX))
    p.do(V, lambda e: e.tensor_tensor(T1, V0, V1, ALU.subtract), sem="vc")
    p.wait("scalar", "vc")
    p.do("scalar", lambda e: e.activation(W0, T1, ACT.Sigmoid), sem="sc")
    p.wait(V, "sc")
    p.do(V, lambda e: e.tensor_single_scalar(W1, W0, 1.0, ALU.subtract))
    p.do(V, lambda e: e.tensor_single_scalar(W1, W1, -1.0, ALU.mult))
    for si_, (Ps_, Ev_, Va, Wv_, Dc, Ds) in enumerate(((P0s, E0v, VA0, W0, DC0, DSS0), (P1s, E1v, VA1, W1, DC1, DSS1))):
        p.do(V, lambda e, Ps_=Ps_, Va=Va: e.tensor_single_scalar(Va, Ps_, float(CAP), ALU.is_le))
        p.do(V, lambda e, Wv_=Wv_, Va=Va: e.tensor_tensor(Wv_, Wv_, Va, ALU.mult))
        p.do(V, lambda e, Ev_=Ev_, Ps_=Ps_, Dc=Dc: e.scalar_tensor_tensor(Dc, Ev_, float(CAP), Ps_, ALU.mult, ALU.add))
        p.do(V, lambda e, Dc=Dc: e.tensor_single_scalar(Dc, Dc, 1.0, ALU.subtract))
        TRt = TR0 if si_ == 0 else TR1
        p.do(V, lambda e, Ds=Ds, Dc=Dc, Va=Va: e.tensor_tensor(Ds, Dc, Va, ALU.mult))
        p.do(V, lambda e, Va=Va, T1=T1: e.tensor_scalar(T1, Va, -1.0, -1.0, ALU.mult, ALU.subtract))
        p.do(V, lambda e, T1=T1, TRt=TRt: e.tensor_tensor(T1, T1, TRt, ALU.mult))
        p.do(V, lambda e, Ds=Ds, T1=T1: e.tensor_tensor(Ds, Ds, T1, ALU.add))
        p.do(V, lambda e, Dc=Dc: e.tensor_single_scalar(Dc, Dc, float(N - 1), ALU.min), sem="vc")
    RTW3 = RTS[:, 0:4 * A_].rearrange("p (a c) -> p a c", c=4)
    p.do(V, lambda e: e.tensor_copy(RTW3[:, :, 0:1], c1(W0)))
    p.do(V, lambda e: e.tensor_copy(RTW3[:, :, 1:2], c1(W1)))
    p.do(V, lambda e: e.tensor_copy(RTW3[:, :, 2:3], c1(DSS0)))
    p.do(V, lambda e: e.tensor_copy(RTW3[:, :, 3:4], c1(DSS1)), sem="vc")
    RTD3 = u32v(RTS, 4 * A_, 2 * A_).rearrange("p (a c) -> p a c", c=2)
    p.do(V, lambda e: e.tensor_copy(RTD3[:, :, 0:1], c1(DC0)))
    p.do(V, lambda e: e.tensor_copy(RTD3[:, :, 1:2], c1(DC1)), sem="vc")
    p.wait("scalar", "vc")
    p.dma("scalar", lambda e: e.dma_start(out=rtw_d.ap().rearrange("(p a) c -> p a c", p=128), in_=RTW3), sem="vld")
    p.wait("scalar", "vc")
    p.dma("scalar", lambda e: e.dma_start(out=rtd_d.ap().rearrange("(p a) c -> p a c", p=128), in_=RTD3), sem="vld")
    p.wait("tensor", "vc")
    p.wait("sync", "vld")
    p.wait("sync", "mm")
    p.wait("sync", "vc")
    ld(bc16(B, 0, DC * T).rearrange("p (a c) -> p a c", a=DC),
       ksp_h.ap().rearrange("(a p) c -> p a c", p=128), sem="ld2")
    ld(bc16(B, DC * T // 2, DC * T).rearrange("p (a c) -> p a c", a=DC),
       ksp_l.ap().rearrange("(a p) c -> p a c", p=128), sem="ld2")
    kpre = p.v("ld2")
    p.wait("gpsimd", "vc")
    p.wait("gpsimd", "st")
    p.wait("gpsimd", "gc")
    p.wait("gpsimd", "ld2", kpre)
    p.wait("gpsimd", "vld")
    for j in range(TT):
        p.wait("gpsimd", "vc", p.v("vc"))
        p.dma("gpsimd", lambda e, j=j: e.indirect_dma_start(
            out=SMALL[:, 128:132], out_offset=None, in_=rtw_d.ap(),
            in_offset=IndirectOffsetOnAxis(ap=OWN[:, j:j + 1], axis=0)), sem="gdma")
        p.wait("vector", "gdma")
        p.do(V, lambda e: e.tensor_copy(DS[:, 0:2], SMALL[:, 130:132]), sem="vc")
        p.wait("gpsimd", "vc")
        for si in range(2):
            p.dma("gpsimd", lambda e, j=j, si=si: e.indirect_dma_start(
                out=Lpart.ap(), out_offset=IndirectOffsetOnAxis(ap=DS[:, si:si + 1], axis=0),
                in_=OWN[:, j:j + 1], in_offset=None), sem="gdma", handshake=False)
    p.qsync("gpsimd", "gdma")
    p.do("gpsimd", lambda e: e.collective_compute(
        "AllReduce", ALU.add, replica_groups=[list(range(NC))],
        ins=[Lpart.ap()], outs=[Lb.ap()]), sem="ccl")

    KTQ = TT // QG
    nqt = NQ // 128
    p.wait("tensor", "ld2", kpre)
    svld = []
    for qg in range(QG):
        ktmax = (qg + 1) * KTQ
        ksc = {}
        for kt in range(ktmax):
            kh = bc16(B, 0, DC * T); kl = bc16(B, DC * T // 2, DC * T)
            pso = PS[0] if kt % 2 == 0 else PS[1]
            if kt >= 2:
                p.wait("tensor", "sc", ksc[kt - 2])
            for pp in range(3):
                for dc in range(DC):
                    lh = (kh if pp < 2 else kl)[:, dc * T + kt * 128:dc * T + kt * 128 + 128]
                    qh_ = bc16(A, 0, 2 * DC * T)[:, dc * T + qg * NQ:dc * T + (qg + 1) * NQ]
                    ql_ = bc16(A, 0, 2 * DC * T)[:, DC * T + dc * T + qg * NQ:DC * T + dc * T + (qg + 1) * NQ]
                    rh = qh_ if pp != 1 else ql_
                    first = (pp == 0 and dc == 0); last = (pp == 2 and dc == DC - 1)
                    p.do("tensor", lambda e, o=pso, l=lh, r=rh, f=first, s=last: e.matmul(
                        o[:, 0:NQ], l, r, start=f, stop=s), sem="mm" if last else None)
            p.wait("scalar", "mm")
            Edst = C[:, kt * NQ:(kt + 1) * NQ]
            p.do("scalar", lambda e, d=Edst, i=pso: e.activation(
                d, i[:, 0:NQ], ACT.Exp, bias=negc0[:, 0:1], scale=L0SC), sem="sc")
            ksc[kt] = p.v("sc")
            if (kt + 1) * 128 > qg * NQ:
                p.wait("gpsimd", "sc")
                base = qg * NQ - kt * 128
                p.do("gpsimd", lambda e, d=Edst, b=base: e.affine_select(
                    d, d, pattern=[[1, NQ]], compare_op=ALU.is_ge, fill=0.0, base=b, channel_multiplier=-1), sem="gc")
        p.wait("tensor", "sc")
        p.wait("tensor", "gc")
        for dh2 in range(D // NQ):
            vmm = {}
            vld2 = {}

            def v_load(kt, dh2=dh2, qg=qg, vmm=vmm, vld2=vld2):
                s = kt % 2
                if kt >= 2:
                    p.wait("sync", "mm", vmm[kt - 2])
                else:
                    p.wait("sync", "mm")
                ld(Em2[:, s * NQ:s * NQ + NQ], vsp.ap()[kt * 128:(kt + 1) * 128, dh2 * NQ:(dh2 + 1) * NQ], sem="ld2")
                vld2[kt] = p.v("ld2")

            v_load(0)
            if ktmax > 1:
                v_load(1)
            for kt in range(ktmax):
                vv = Em2[:, (kt % 2) * NQ:(kt % 2) * NQ + NQ]
                p.wait("tensor", "ld2", vld2[kt])
                for qt in range(nqt):
                    ktlim = qg * KTQ + ((qt * 128 + 127) // 128) + 1
                    ktlim = min(ktmax, qg * KTQ + qt + 1) if NQ == 512 else ktmax
                    if kt >= ktlim:
                        continue
                    p.do("tensor", lambda e, o=PS[2 + qt], l=C[:, kt * NQ + qt * 128:kt * NQ + qt * 128 + 128],
                         r=vv, f=(kt == 0), s=(kt == ktlim - 1): e.matmul(
                        o[:, 0:NQ], l, r, start=f, stop=s), sem="mm" if qt == nqt - 1 else None)
                vmm[kt] = p.v("mm")
                if kt + 2 < ktmax:
                    v_load(kt + 2)
            if dh2 == 0:
                for qt in range(nqt):
                    ktlim = min(ktmax, qg * KTQ + qt + 1) if NQ == 512 else ktmax
                    for kt in range(ktlim):
                        p.do("tensor", lambda e, l=C[:, kt * NQ + qt * 128:kt * NQ + qt * 128 + 128],
                             qt=qt, f=(kt == 0), s=(kt == ktlim - 1): e.matmul(
                            PS[6][:, qt:qt + 1], l, on1_s, start=f, stop=s, skip_group_check=True),
                            sem="mm" if kt == ktlim - 1 else None)
                p.wait("vector", "mm")
                p.do("vector", lambda e: e.reciprocal(SMALL[:, 64:64 + nqt], PS[6][:, 0:nqt]), sem="vc")
            p.wait("vector", "mm")
            for qt in range(nqt):
                tq = qg * nqt + qt
                sst = Em[:, 1024 + (qt % 2) * NQ:1024 + (qt % 2) * NQ + NQ]
                if len(svld) >= 2:
                    p.wait("vector", "vld", svld[-2])
                p.do("vector", lambda e, o=sst, i=PS[2 + qt], qt=qt:
                     e.tensor_scalar(o, i[:, 0:NQ], SMALL[:, 64 + qt:64 + qt + 1], None, ALU.mult), sem="vc")
                p.wait("scalar", "vc")
                p.dma("scalar", lambda e, s2=sst, tq=tq, dh2=dh2: e.dma_start(
                    out=ssp.ap()[tq * 128:(tq + 1) * 128, dh2 * NQ:(dh2 + 1) * NQ], in_=s2), sem="vld")
                svld.append(p.v("vld"))
            p.wait("tensor", "vc")

    p.wait("gpsimd", "ccl")
    if DBGV:
        p.dma("gpsimd", lambda e: e.dma_start(out=dbgLb.ap(), in_=Lb.ap()), sem="gdma")
        p.dma("gpsimd", lambda e: e.dma_start(out=dbgRtw.ap(), in_=rtw_d.ap()), sem="gdma")
        p.wait("gpsimd", "gdma")
    for j in range(TT):
        p.dma("gpsimd", lambda e, j=j: e.indirect_dma_start(
            out=LBT[:, j:j + 1], out_offset=None, in_=Lb.ap(),
            in_offset=IndirectOffsetOnAxis(ap=OWN[:, j:j + 1], axis=0)), sem="gdma")
    p.wait("gpsimd", "gdma")
    vcj = {}
    gldj = {}
    for j in range(TT + 1):
        if j < TT:
            Es = Em if j % 2 == 0 else Dm
            xg = Es[:, 0:D]
            stg = Es[:, 3 * D // 2:3 * D // 2 + 16]
            if j >= 2:
                p.wait("gpsimd", "vc", vcj[j - 2])
            else:
                p.wait("gpsimd", "vc", p.v("vc"))
            p.dma("gpsimd", lambda e, x=xg, j=j: e.indirect_dma_start(
                out=x, out_offset=None, in_=xfull,
                in_offset=IndirectOffsetOnAxis(ap=LBT[:, j:j + 1], axis=0)), sem="gdma")
            p.dma("gpsimd", lambda e, s=stg, j=j: e.indirect_dma_start(
                out=s, out_offset=None, in_=rt_all.ap(),
                in_offset=IndirectOffsetOnAxis(ap=LBT[:, j:j + 1], axis=0)), sem="gdma")
            gldj[j] = p.v("gdma")
        if j >= 1:
            jj = j - 1
            Es = Em if jj % 2 == 0 else Dm
            xg = Es[:, 0:D]
            stg = Es[:, 3 * D // 2:3 * D // 2 + 16]
            p.wait("vector", "gdma", gldj[jj])
            xe = Es[:, D // 2:D // 2 + D]
            p.do("vector", lambda e, x=xg, o=xe, s=stg: e.tensor_scalar(
                o, x, s[:, E:E + 1], s[:, E + 1:E + 2], ALU.subtract, ALU.mult), sem="vc")
            split_transpose_to(A, jj, xe)
            vcj[jj] = p.v("vc")
    xeh = lambda dc: bc16(A, 0, 2 * DC * T)[:, dc * T:(dc + 1) * T]
    xel = lambda dc: bc16(A, 0, 2 * DC * T)[:, DC * T + dc * T:DC * T + (dc + 1) * T]

    UH = HC * TCH
    for tc in range(T // TCH):
        units = [(hc, mi) for hc in range(HC) for mi in range(2)]
        mm_hist = {}
        vc_hist = {}

        uld = {}

        def u_load(u, units=units, mm_hist=mm_hist, vc_hist=vc_hist, tc=tc, uld=uld):
            hc, mi = units[u]
            s = u % 2
            wd = wx_d if mi == 0 else vx_d
            if u >= 2:
                p.wait("sync", "mm", mm_hist[u - 2])
                p.wait("sync", "vc", vc_hist[u - 2])
            else:
                p.wait("sync", "mm")
                p.wait("sync", "vc")
            blk = Dm[:, s * 2 * D:s * 2 * D + D]
            ld(blk, wd[hc, :, :, :].rearrange("p a c -> p (a c)"))
            uld[u] = p.v("ld")

        u_load(0)
        u_load(1)
        for u, (hc, mi) in enumerate(units):
            s = u % 2
            blk = Dm[:, s * 2 * D:s * 2 * D + D]
            hi = bc16(Dm, s * 2 * D + D, D); lo = bc16(Dm, s * 2 * D + D + D // 2, D)
            p.wait("scalar", "ld", uld[u])
            if u >= 2:
                p.wait("scalar", "mm", mm_hist[u - 2])
            split_hilo(blk, hi, lo)
            vc_hist[u] = p.v("vc")
            pso = PS[4 + (u % 4)]
            p.wait("tensor", "vc")
            for pp in range(3):
                for dc in range(DC):
                    lh = (hi if pp < 2 else lo).rearrange("p (a c) -> p a c", a=DC)[:, dc, :]
                    rh = (xeh(dc) if pp != 1 else xel(dc))[:, tc * TCH:(tc + 1) * TCH]
                    first = (pp == 0 and dc == 0); last = (pp == 2 and dc == DC - 1)
                    p.do("tensor", lambda e, o=pso, l=lh, r=rh, f=first, s=last: e.matmul(
                        o[:, 0:TCH], l, r, start=f, stop=s), sem="mm" if last else None)
            mm_hist[u] = p.v("mm")
            if u + 2 < len(units):
                u_load(u + 2)
            if mi == 1:
                pa = PS[4 + ((u - 1) % 4)]; pb = PS[4 + (u % 4)]
                p.wait("scalar", "mm")
                if u >= 4:
                    p.wait("scalar", "vc", vc_hist[u - 4])
                sl = Em[:, (hc % 2) * TCH:(hc % 2) * TCH + TCH]
                p.do("scalar", lambda e, o=sl, i=pb: e.activation(o, i[:, 0:TCH], ACT.Silu), sem="sc")
                p.wait("vector", "sc")
                uu = Em[:, 1024 + (hc % 2) * TCH:1024 + (hc % 2) * TCH + TCH] if TCH <= 512 else Em[:, TCH:2 * TCH]
                p.do("vector", lambda e, o=uu, s2=sl, i=pa: e.tensor_tensor(o, i[:, 0:TCH], s2, ALU.mult), sem="vc")
                split_hilo(uu, bc16(C, 0, 2 * UH)[:, hc * TCH:(hc + 1) * TCH],
                           bc16(C, 0, 2 * UH)[:, UH + hc * TCH:UH + (hc + 1) * TCH])
                vc_hist[u] = p.v("vc")
        p.wait("tensor", "vc")
        ymm = {}
        yld = {}

        def y_load(hc, tc=tc, ymm=ymm, yld=yld):
            s = hc % 2
            if hc >= 2:
                p.wait("sync", "mm", ymm[hc - 2])
            else:
                p.wait("sync", "mm")
                p.wait("sync", "vc")
            blk = Dm[:, s * 2 * D:s * 2 * D + D]
            ld(blk, wox_d[hc * 128:(hc + 1) * 128, :])
            yld[hc] = p.v("ld")

        y_load(0)
        y_load(1)
        for hc in range(HC):
            s = hc % 2
            blk = Dm[:, s * 2 * D:s * 2 * D + D]
            hi = bc16(Dm, s * 2 * D + D, D); lo = bc16(Dm, s * 2 * D + D + D // 2, D)
            p.wait("scalar", "ld", yld[hc])
            if hc >= 2:
                p.wait("scalar", "mm", ymm[hc - 2])
            split_hilo(blk, hi, lo)
            p.wait("tensor", "vc")
            for ts4 in range(TCH // 128):
                uh_ = bc16(C, 0, 2 * UH)[:, hc * TCH + ts4 * 128:hc * TCH + ts4 * 128 + 128]
                ul_ = bc16(C, 0, 2 * UH)[:, UH + hc * TCH + ts4 * 128:UH + hc * TCH + ts4 * 128 + 128]
                for dh2 in range(D // NQ):
                    pso = PS[ts4 * (D // NQ) + dh2]
                    for pp in range(3):
                        lh = uh_ if pp < 2 else ul_
                        rh = (hi if pp != 1 else lo)[:, dh2 * NQ:(dh2 + 1) * NQ]
                        first = (pp == 0 and hc == 0); last = (pp == 2 and hc == HC - 1)
                        p.do("tensor", lambda e, o=pso, l=lh, r=rh, f=first, s=last: e.matmul(
                            o[:, 0:NQ], l, r, start=f, stop=s),
                            sem="mm" if pp == 2 else None)
            ymm[hc] = p.v("mm")
            if hc + 2 < HC:
                y_load(hc + 2)
        p.wait("vector", "mm")
        for ts4 in range(TCH // 128):
            for dh2 in range(D // NQ):
                yv = Em[:, 0:NQ]
                p.do("vector", lambda e, o=yv, i=PS[ts4 * (D // NQ) + dh2]: e.tensor_copy(o, i[:, 0:NQ]), sem="vc")
                p.wait("scalar", "vc")
                p.dma("scalar", lambda e, s=yv, tc=tc, ts4=ts4, dh2=dh2: e.dma_start(
                    out=y_in.ap()[tc * TCH + ts4 * 128:tc * TCH + (ts4 + 1) * 128, dh2 * NQ:(dh2 + 1) * NQ],
                    in_=s), sem="vld")
                p.wait("vector", "vld")
    p.wait("gpsimd", "vld")
    p.do("gpsimd", lambda e: e.collective_compute(
        "AllGather", ALU.bypass, replica_groups=[list(range(NC))],
        ins=[y_in.ap()], outs=[y_all.ap()]), sem="ccy")

    p.wait("gpsimd", "ccy")
    if DBGV:
        p.dma("gpsimd", lambda e: e.dma_start(out=dbgY.ap(), in_=y_all.ap()), sem="gdma")
        p.dma("gpsimd", lambda e: e.dma_start(out=dbgS.ap(), in_=ssp.ap()), sem="gdma")
        p.dma("gpsimd", lambda e: e.dma_start(out=dbgB.ap(), in_=bcb.ap()), sem="gdma")
        p.dma("gpsimd", lambda e: e.dma_start(out=dbgVs.ap(), in_=vsp.ap()), sem="gdma")
        p.dma("gpsimd", lambda e: e.dma_start(out=dbgKh.ap(), in_=ksp_h.ap()), sem="gdma")
        p.dma("gpsimd", lambda e: e.dma_start(out=dbgKl.ap(), in_=ksp_l.ap()), sem="gdma")
        p.wait("gpsimd", "gdma")
    p.wait("vector", "mm")
    for j in range(TT):
        p.wait("sync", "vc")
        ld(Dm[:, 0:D], ssp.ap()[j * 128:(j + 1) * 128, :], sem="ld2")
        rdv = u32v(SMALL, 128, 2)
        rwv = SMALL[:, 132:136]
        p.wait("gpsimd", "vc", p.v("vc"))
        p.dma("gpsimd", lambda e, d=rdv, j=j: e.indirect_dma_start(
            out=d, out_offset=None, in_=rtd_d.ap(),
            in_offset=IndirectOffsetOnAxis(ap=OWN[:, j:j + 1], axis=0)), sem="gdma")
        p.dma("gpsimd", lambda e, d=rwv, j=j: e.indirect_dma_start(
            out=d, out_offset=None, in_=rtw_d.ap(),
            in_offset=IndirectOffsetOnAxis(ap=OWN[:, j:j + 1], axis=0)), sem="gdma")
        p.wait("gpsimd", "gdma")
        g0 = Em[:, 0:D]; g1v = Em2[:, 0:D]
        p.dma("gpsimd", lambda e, d=g0, i=rdv: e.indirect_dma_start(
            out=d, out_offset=None, in_=y_all.ap(),
            in_offset=IndirectOffsetOnAxis(ap=i[:, 0:1], axis=0)), sem="gdma")
        p.dma("gpsimd", lambda e, d=g1v, i=rdv: e.indirect_dma_start(
            out=d, out_offset=None, in_=y_all.ap(),
            in_offset=IndirectOffsetOnAxis(ap=i[:, 1:2], axis=0)), sem="gdma")
        xr = Em2[:, D:2 * D] if D <= 1024 else Em2[:, 0:D]
        p.wait("sync", "vc")
        ld(xr, xown[j * 128:(j + 1) * 128, :], sem="ld")
        p.wait("vector", "gdma")
        p.wait("vector", "ld")
        p.wait("vector", "ld2")
        mo = Em[:, D:2 * D] if D <= 1024 else Em[:, 0:D]
        p.do("vector", lambda e, o=mo, a=g0, w=rwv: e.tensor_scalar(o, a, w[:, 0:1], None, ALU.mult))
        p.do("vector", lambda e, o=mo, a=g1v, w=rwv: e.scalar_tensor_tensor(o, a, w[:, 1:2], o, ALU.mult, ALU.add), sem="vc")
        p.wait("scalar", "vc")
        gs = Em[:, 0:D]
        p.do("scalar", lambda e, o=gs, i=mo: e.activation(o, i, ACT.Silu), sem="sc")
        p.wait("vector", "sc")
        x1t = A[:, j * D:(j + 1) * D]
        p.do("vector", lambda e, o=x1t, s=Dm[:, 0:D], g=gs: e.tensor_tensor(o, s, g, ALU.mult))
        p.do("vector", lambda e, o=x1t, x=xr: e.tensor_tensor(o, o, x, ALU.add), sem="vc")
        p.wait("scalar", "vc")
        p.dma("scalar", lambda e, s=x1t, j=j: e.dma_start(out=x1sp.ap()[j * 128:(j + 1) * 128, :], in_=s), sem="vld")

    if DBGV:
        p.wait("gpsimd", "vld")
        p.dma("gpsimd", lambda e: e.dma_start(out=dbgX1.ap(), in_=x1sp.ap()), sem="gdma")
        p.wait("gpsimd", "gdma")
    for j in range(TT):
        x1t = A[:, j * D:(j + 1) * D]
        stats_pass(x1t, MU1, RS1, j)
        xnt = Em[:, 1024:1024 + D] if D <= 1024 else Em[:, 0:D]
        p.do("vector", lambda e, x=x1t, o=xnt, j=j: e.tensor_scalar(
            o, x, MU1[:, j:j + 1], RS1[:, j:j + 1], ALU.subtract, ALU.mult), sem="vc")
        split_transpose_to(B, j, xnt)
        for pp in range(3):
            for dc in range(DC):
                lh = (xnh(dc) if pp != 2 else xnl(dc))[:, j * 128:(j + 1) * 128]
                rh = (wgT_h if pp != 1 else wgT_l)[:, dc * E:(dc + 1) * E]
                first = (pp == 0 and dc == 0); last = (pp == 2 and dc == DC - 1)
                p.do("tensor", lambda e, l=lh, r=rh, f=first, s=last: e.matmul(
                    PS[6][:, 0:E], l, r, start=f, stop=s), sem="mm" if last else None)
        p.wait("vector", "mm")
        lgt = Em[:, 0:E]
        p.do("vector", lambda e, o=lgt: e.tensor_copy(o, PS[6][:, 0:E]))
        v0 = Em[:, 16:17]; v1 = Em[:, 17:18]; oh0 = Em[:, 32:32 + E]; oh1 = Em[:, 48:48 + E]; lm = Em[:, 64:64 + E]
        p.do(V, lambda e, l=lgt, o=v0: e.reduce_max(o, l, axis=AX))
        p.do(V, lambda e, l=lgt, m=v0, o=oh0: e.tensor_scalar(o, l, m[:, 0:1], None, ALU.is_equal))
        p.do(V, lambda e, l=lgt, o=lm, h=oh0: e.scalar_tensor_tensor(o, h, -1e30, l, ALU.mult, ALU.add))
        p.do(V, lambda e, l=lm, o=v1: e.reduce_max(o, l, axis=AX))
        p.do(V, lambda e, l=lm, m=v1, o=oh1: e.tensor_scalar(o, l, m[:, 0:1], None, ALU.is_equal))
        p.do(V, lambda e, a=v0, b=v1, o=Em[:, 18:19]: e.tensor_tensor(o, a, b, ALU.subtract), sem="vc")
        p.wait("scalar", "vc")
        p.do("scalar", lambda e, i=Em[:, 18:19], o=Em[:, 19:20]: e.activation(o, i, ACT.Sigmoid), sem="sc")
        p.wait(V, "sc")
        p.do(V, lambda e, w=Em[:, 19:20], o=Em[:, 20:21]: e.tensor_single_scalar(o, w, 1.0, ALU.subtract))
        p.do(V, lambda e, o=Em[:, 20:21]: e.tensor_single_scalar(o, o, -1.0, ALU.mult))
        p.do(V, lambda e, h=oh0, w=Em[:, 19:20], o=CW[:, j * E:(j + 1) * E]: e.tensor_scalar(o, h, w[:, 0:1], None, ALU.mult))
        p.do(V, lambda e, h=oh1, w=Em[:, 20:21], o=CW[:, j * E:(j + 1) * E]: e.scalar_tensor_tensor(
            o, h, w[:, 0:1], o, ALU.mult, ALU.add), sem="vc")

    p.wait("sync", "ccw")
    p.wait("vector", "vld")
    wmm = {}

    wld = {}

    def wet_load(e_, wmm=wmm, wld=wld):
        if e_ >= 1:
            p.wait("sync", "mm", wmm[e_ - 1])
        ld(bc16(C, 0, DC * 2 * D).rearrange("p (a c) -> p a c", a=DC),
           wet_all.ap()[e_ * D:(e_ + 1) * D, :].rearrange("(a p) c -> p a c", p=128), sem="ld2")
        wld[e_] = p.v("ld2")

    p.wait("sync", "mm")
    wet_load(0)
    zvch = []
    for e_ in range(E):
        wet_sb = bc16(C, 0, DC * 2 * D)
        beb = Em2[:, 1024:1024 + D] if D <= 1024 else Em2[:, D:2 * D]
        p.wait("sync", "vc")
        ld(beb, be_d[e_:e_ + 1, :].broadcast_to((128, D)), sem="ld")
        p.wait("tensor", "ld2", wld[e_])
        for j in range(TT):
            for dh2 in range(D // NQ):
                z = (e_ * TT + j) * (D // NQ) + dh2
                if z >= 2:
                    p.wait("tensor", "vc", zvch[z - 2])
                pso = PS[0] if z % 2 == 0 else PS[1]
                for pp in range(3):
                    for dc in range(DC):
                        lh = (xnh(dc) if pp != 2 else xnl(dc))[:, j * 128:(j + 1) * 128]
                        rh = wet_sb.rearrange("p (a c) -> p a c", a=DC)[:, dc, :][:, (0 if pp != 1 else D) + dh2 * NQ:(0 if pp != 1 else D) + (dh2 + 1) * NQ]
                        first = (pp == 0 and dc == 0); last = (pp == 2 and dc == DC - 1)
                        p.do("tensor", lambda e, o=pso, l=lh, r=rh, f=first, s=last: e.matmul(
                            o[:, 0:NQ], l, r, start=f, stop=s), sem="mm" if last else None)
                p.wait("vector", "mm")
                p.wait("vector", "ld")
                ta = Em[:, (z % 2) * NQ:(z % 2) * NQ + NQ]
                p.do("vector", lambda e, o=ta, i=pso, b=beb, dh2=dh2: e.tensor_tensor(
                    o, i[:, 0:NQ], b[:, dh2 * NQ:(dh2 + 1) * NQ], ALU.add))
                accs = A[:, j * D + dh2 * NQ:j * D + (dh2 + 1) * NQ]
                if e_ == 0:
                    p.do("vector", lambda e, o=accs, i=ta, j=j: e.tensor_scalar(
                        o, i, CW[:, j * E:j * E + 1], None, ALU.mult), sem="vc")
                else:
                    p.do("vector", lambda e, o=accs, i=ta, j=j, e_=e_: e.scalar_tensor_tensor(
                        o, i, CW[:, j * E + e_:j * E + e_ + 1], o, ALU.mult, ALU.add), sem="vc")
                zvch.append(p.v("vc"))
        wmm[e_] = p.v("mm")
        if e_ + 1 < E:
            wet_load(e_ + 1)
    p.wait("scalar", "vc")
    for j in range(TT):
        p.do("scalar", lambda e, j=j: e.activation(A[:, j * D:(j + 1) * D], A[:, j * D:(j + 1) * D], ACT.Silu), sem="sc")

    lnw_bc = Dm[:, 0:D]; lnb_bc = Dm[:, D:2 * D]
    p.wait("sync", "mm")
    p.wait("sync", "vc")
    ld(lnw_bc, lnw_d.broadcast_to((128, D)), sem="ld2")
    ld(lnb_bc, lnb_d.broadcast_to((128, D)), sem="ld2")
    p.wait("vector", "ld2")
    for j in range(TT):
        x1t = Em[:, 0:D] if D <= 1024 else Em[:, 0:D]
        p.wait("sync", "vc")
        p.wait("sync", "sc")
        ld(x1t, x1sp.ap()[j * 128:(j + 1) * 128, :], sem="ld")
        p.wait("vector", "ld")
        xn1t = Em[:, 1024:1024 + D] if D <= 1024 else Em[:, D:2 * D]
        p.do("vector", lambda e, x=x1t, o=xn1t, j=j: e.tensor_scalar(
            o, x, MU1[:, j:j + 1], RS1[:, j:j + 1], ALU.subtract, ALU.mult), sem="vc")
        bst = Em2[:, 0:6 * NCH]; bag = Em2[:, 8 * NCH:8 * NCH + 2]
        for ch in range(NCH):
            p.do("vector", lambda e, x=xn1t, ch=ch: e.bn_stats(
                Em2[:, 6 * ch:6 * ch + 6], x[:, 512 * ch:min(512 * (ch + 1), D)]))
        p.do("vector", lambda e, b=bst, a=bag: e.bn_aggr(a, b))
        p.do("vector", lambda e, a=bag, j=j: e.tensor_copy(MU2[:, j:j + 1], a[:, 0:1]))
        p.do("vector", lambda e, a=bag: e.tensor_single_scalar(Em2[:, 8 * NCH + 4:8 * NCH + 5], a[:, 1:2], 1e-5, ALU.add), sem="vc")
        p.wait("scalar", "vc")
        p.do("scalar", lambda e: e.activation(Em2[:, 8 * NCH + 5:8 * NCH + 6], Em2[:, 8 * NCH + 4:8 * NCH + 5], ACT.Sqrt), sem="sc")
        p.wait("vector", "sc")
        p.do("vector", lambda e, j=j: e.reciprocal(RS2[:, j:j + 1], Em2[:, 8 * NCH + 5:8 * NCH + 6]), sem="vc")
        znt = Em2[:, 1024:1024 + D] if D <= 1024 else Em2[:, D:2 * D]
        p.do("vector", lambda e, x=xn1t, o=znt, j=j: e.tensor_scalar(
            o, x, MU2[:, j:j + 1], RS2[:, j:j + 1], ALU.subtract, ALU.mult))
        p.do("vector", lambda e, o=znt: e.tensor_tensor(o, o, lnw_bc, ALU.mult))
        p.do("vector", lambda e, o=znt: e.tensor_tensor(o, o, lnb_bc, ALU.add), sem="vc")
        split_transpose_to(B, j, znt)
    znh = xnh; znl = xnl

    hp2 = 2 * DH
    NPH = NH // 2
    Qh = bc16(Dm, 0, T); Ql = bc16(Dm, T // 2, T)
    Kh = bc16(Dm, T, T); Kl = bc16(Dm, T + T // 2, T)
    VA0c = 0
    VAb = T

    for hp in range(NPH):
        for part in range(3):
            colbase = part * D + hp * hp2
            wful = Em2[:, 0:DC * hp2]
            p.wait("sync", "mm")
            p.wait("sync", "vc")
            blkid = part * (D // hp2) * 0 + part * NPH + hp if False else part * (NH // 2) + hp
            ld(wful, wqkv_d[blkid, :, :, :].rearrange("p a c -> p (a c)"))
            p.wait("scalar", "ld")
            wh = bc16(Em2, DC * hp2 // 2 * 0 + 512, DC * hp2) if False else bc16(Em2, 512, DC * hp2)
            wl = bc16(Em2, 512 + DC * hp2 // 2, DC * hp2)
            split_hilo(wful, wh, wl)
            p.wait("tensor", "vc")
            for ts in range(T // NQ):
                p.wait("tensor", "vc")
                pso = PS[0] if ts % 2 == 0 else PS[1]
                for pp in range(3):
                    for dc in range(DC):
                        lh = (wh if pp < 2 else wl).rearrange("p (a c) -> p a c", a=DC)[:, dc, :]
                        rh = (znh(dc) if pp != 1 else znl(dc))[:, ts * NQ:(ts + 1) * NQ]
                        first = (pp == 0 and dc == 0); last = (pp == 2 and dc == DC - 1)
                        p.do("tensor", lambda e, o=pso, l=lh, r=rh, f=first, s=last: e.matmul(
                            o[0:hp2, 0:NQ], l, r, start=f, stop=s), sem="mm" if last else None)
                p.wait("vector", "mm")
                ev = Em[:, 0:NQ]
                bcol = part * (D // hp2) + hp
                p.do("vector", lambda e, o=ev, i=pso, bcol=bcol: e.tensor_scalar(
                    o[0:hp2, :], i[0:hp2, 0:NQ], bqkv_s[0:hp2, bcol:bcol + 1], None, ALU.add), sem="vc")
                if part < 2:
                    hi_t = bc16(Em, 512, NQ); lo_t = bc16(Em, 512 + NQ // 2, NQ)
                    split_hilo(ev, hi_t, lo_t)
                    dsth = Qh if part == 0 else Kh
                    dstl = Ql if part == 0 else Kl
                    p.do("vector", lambda e, d=dsth, s=hi_t, ts=ts: e.tensor_copy(
                        d[0:hp2, ts * NQ:(ts + 1) * NQ], s[0:hp2, :]))
                    p.do("vector", lambda e, d=dstl, s=lo_t, ts=ts: e.tensor_copy(
                        d[0:hp2, ts * NQ:(ts + 1) * NQ], s[0:hp2, :]), sem="vc")
                    sqb = Em[:, 1024:1024 + NQ]
                    p.do("scalar", lambda e, o=sqb, i=ev: e.activation(o[0:hp2, :], i[0:hp2, :], ACT.Square), sem="sc")
                    p.wait("tensor", "sc")
                    p.do("tensor", lambda e, s=sqb: e.matmul(PS[6][0:2, 0:NQ], op2_s[0:hp2, :], s[0:hp2, :],
                                                             start=True, stop=True), sem="mm")
                    p.wait("vector", "mm")
                    ncol = 70 if part == 0 else 71
                    p.do("vector", lambda e, c=ncol: e.reduce_max(SMALL[0:2, 76 + c - 70:77 + c - 70], PS[6][0:2, 0:NQ], axis=AX))
                    if ts == 0:
                        p.do("vector", lambda e, c=ncol: e.tensor_copy(SMALL[0:2, c:c + 1], SMALL[0:2, 76 + c - 70:77 + c - 70]), sem="vc")
                    else:
                        p.do("vector", lambda e, c=ncol: e.tensor_tensor(SMALL[0:2, c:c + 1], SMALL[0:2, c:c + 1], SMALL[0:2, 76 + c - 70:77 + c - 70], ALU.max), sem="vc")
                else:
                    p.do("vector", lambda e, s=ev, ts=ts: e.tensor_copy(
                        C[0:hp2, ts * NQ:(ts + 1) * NQ], s[0:hp2, :]), sem="vc")
        p.wait("tensor", "vc")
        for tt2 in range(TT):
            p.wait("tensor", "vc")
            p.do("tensor", lambda e, tt2=tt2: e.transpose(PS[4][:, 0:2 * DH], C[0:2 * DH, tt2 * 128:(tt2 + 1) * 128], idf_s[0:hp2, 0:hp2]), sem="mm")
            p.wait("vector", "mm")
            p.do("vector", lambda e, tt2=tt2: e.tensor_copy(C[:, VAb + tt2 * 132:VAb + tt2 * 132 + DH], PS[4][:, 0:DH]))
            p.do("vector", lambda e, tt2=tt2: e.tensor_copy(C[:, VAb + tt2 * 132 + 66:VAb + tt2 * 132 + 66 + DH], PS[4][:, DH:2 * DH]))
            p.do("vector", lambda e, tt2=tt2: e.memset(C[:, VAb + tt2 * 132 + DH:VAb + tt2 * 132 + DH + 1], 1.0))
            p.do("vector", lambda e, tt2=tt2: e.memset(C[:, VAb + tt2 * 132 + 66 + DH:VAb + tt2 * 132 + 66 + DH + 1], 1.0), sem="vc")
        p.wait("vector", "vld")
        p.wait("scalar", "vc")
        p.do("scalar", lambda e: e.activation(SMALL[0:2, 72:73], SMALL[0:2, 70:71], ACT.Sqrt))
        p.do("scalar", lambda e: e.activation(SMALL[0:2, 73:74], SMALL[0:2, 71:72], ACT.Sqrt), sem="sc")
        p.wait("vector", "sc")
        p.do("vector", lambda e: e.tensor_tensor(SMALL[0:2, 74:75], SMALL[0:2, 72:73], SMALL[0:2, 73:74], ALU.mult))
        p.do("vector", lambda e: e.tensor_single_scalar(SMALL[0:2, 75:76], SMALL[0:2, 74:75], -L1SC, ALU.mult), sem="vc")
        p.wait("scalar", "vc")
        p.dma("scalar", lambda e, hp=hp: e.dma_start(out=bcb.ap()[0:1, 8 + 2 * hp:8 + 2 * hp + 2].transpose([1, 0]),
                                                    in_=SMALL[0:2, 75:76]), sem="vld")
        p.wait("sync", "vld")
        ld(negcH[:, 2 * hp:2 * hp + 2], bcb.ap()[0:1, 8 + 2 * hp:8 + 2 * hp + 2].broadcast_to((128, 2)), sem="ld2")

        p.do("vector", lambda e: e.memset(Mm[0:1, 300:300 + DH], 1.0), sem="vc")
        for qq in range(T // NQ):
            for h2 in range(2):
                p.wait("tensor", "vc")
                p.wait("scalar", "ld2", p.v("ld2"))
                po = PS[2] if h2 == 0 else PS[3]
                sc_hist = []

                def emit_pv(kt, hp=hp, h2=h2, po=po):
                    Ehb = Em[:, (kt % 2) * NQ:(kt % 2) * NQ + NQ]
                    p.wait("tensor", "sc", sc_hist[kt])
                    p.do("tensor", lambda e, o=po, kt=kt, r=Ehb: e.matmul(
                        o[0:DH + 1, 0:NQ], C[:, VAb + kt * 132 + 66 * h2:VAb + kt * 132 + 66 * h2 + DH + 1], r,
                        start=(kt == 0), stop=(kt == TT - 1)), sem="mm")

                for kt in range(TT):
                    pss = PS[0] if kt % 2 == 0 else PS[1]
                    if kt >= 2:
                        p.wait("tensor", "sc", sc_hist[kt - 2])
                    for pp in range(3):
                        lh = (Kh if pp < 2 else Kl)[h2 * DH:(h2 + 1) * DH, kt * 128:(kt + 1) * 128]
                        rh = (Qh if pp != 1 else Ql)[h2 * DH:(h2 + 1) * DH, qq * NQ:(qq + 1) * NQ]
                        first = (pp == 0); last = (pp == 2)
                        p.do("tensor", lambda e, o=pss, l=lh, r=rh, f=first, s=last: e.matmul(
                            o[:, 0:NQ], l, r, start=f, stop=s), sem="mm" if last else None)
                    p.wait("scalar", "mm")
                    Ehb = Em[:, (kt % 2) * NQ:(kt % 2) * NQ + NQ]
                    p.do("scalar", lambda e, d=Ehb, i=pss, hp=hp, h2=h2: e.activation(
                        d, i[:, 0:NQ], ACT.Exp, bias=negcH[:, 2 * hp + h2:2 * hp + h2 + 1], scale=L1SC), sem="sc")
                    sc_hist.append(p.v("sc"))
                    if kt >= 1:
                        emit_pv(kt - 1)
                emit_pv(TT - 1)
                p.wait("vector", "mm")
                p.wait("vector", "vld")
                p.do("vector", lambda e, i=po: e.reciprocal(Em2[0:1, 0:NQ], i[DH:DH + 1, 0:NQ]), sem="vc")
                p.wait("tensor", "vc")
                p.do("tensor", lambda e: e.matmul(PS[5][0:DH, 0:NQ], Mm[0:1, 300:300 + DH],
                                                  Em2[0:1, 0:NQ], start=True, stop=True), sem="mm")
                p.wait("vector", "mm")
                p.do("vector", lambda e: e.tensor_copy(Em2[0:DH, 0:NQ], PS[5][0:DH, 0:NQ]))
                mslice = Em[:, 1024:1024 + NQ]
                p.do("vector", lambda e, i=po: e.tensor_tensor(
                    mslice[0:DH, :] if DH <= 128 else None, i[0:DH, 0:NQ], Em2[0:DH, 0:NQ], ALU.mult), sem="vc")
                p.wait("scalar", "vc")
                p.dma("scalar", lambda e, hp=hp, h2=h2, qq=qq: e.dma_start(
                    out=msp.ap()[hp * hp2 + h2 * DH:hp * hp2 + (h2 + 1) * DH, qq * NQ:(qq + 1) * NQ],
                    in_=mslice[0:DH, :]), sem="vld")
                p.wait("vector", "vld")

    wph = bc16(C, 0, D * DC)
    wpl = bc16(C, (D * DC) // 2, D * DC)
    for dc in range(DC):
        blk = Em2[:, 0:D]
        p.wait("sync", "vc")
        p.wait("sync", "sc")
        ld(blk, wproj_d[dc * 128:(dc + 1) * 128, :])
        p.wait("scalar", "ld")
        split_hilo(blk, wph[:, dc * D:(dc + 1) * D], wpl[:, dc * D:(dc + 1) * D])
    bpb = Dm[:, 0:D]
    p.wait("sync", "sc")
    ld(bpb, bproj_d.broadcast_to((128, D)), sem="ld")
    p.wait("tensor", "vc")
    p.wait("vector", "vld")
    for qq2 in range(T // NQ):
        mchq = B[:, 0:DC * NQ]
        p.wait("sync", "mm")
        p.wait("sync", "vc")
        if qq2 == 0:
            p.wait("sync", "vld")
        ld(mchq.rearrange("p (a c) -> p a c", a=DC),
           msp.ap()[:, qq2 * NQ:(qq2 + 1) * NQ].rearrange("(a p) c -> p a c", p=128), sem="ld2")
        p.wait("scalar", "ld2")
        mh = bc16(B, DC * NQ, DC * NQ); ml_ = bc16(B, DC * NQ + DC * NQ // 2, DC * NQ)
        split_hilo(mchq, mh, ml_)
        p.wait("tensor", "vc")
        for qt in [qq2 * (NQ // 128) + i for i in range(NQ // 128)]:
            qo = (qt - qq2 * (NQ // 128)) * 128
            for dh2 in range(D // NQ):
                p.wait("tensor", "vc")
                pso = PS[4] if dh2 % 2 == 0 else PS[5]
                for pp in range(3):
                    for dc in range(DC):
                        lh = (mh if pp < 2 else ml_).rearrange("p (a c) -> p a c", a=DC)[:, dc, :][:, qo:qo + 128]
                        rh = (wph if pp != 1 else wpl)[:, dc * D + dh2 * NQ:dc * D + (dh2 + 1) * NQ]
                        first = (pp == 0 and dc == 0); last = (pp == 2 and dc == DC - 1)
                        p.do("tensor", lambda e, o=pso, l=lh, r=rh, f=first, s=last: e.matmul(
                            o[:, 0:NQ], l, r, start=f, stop=s), sem="mm" if last else None)
            x1t = Em[:, 0:D]
            p.wait("sync", "vc")
            ld(x1t, x1sp.ap()[qt * 128:(qt + 1) * 128, :], sem="ld")
            p.wait("vector", "mm")
            p.wait("vector", "ld")
            xn1t = Em[:, 1024:1024 + D] if D <= 1024 else Em[:, D:2 * D]
            p.do("vector", lambda e, x=x1t, o=xn1t, qt=qt: e.tensor_scalar(
                o, x, MU1[:, qt:qt + 1], RS1[:, qt:qt + 1], ALU.subtract, ALU.mult), sem="vc")
            for dh2 in range(D // NQ):
                pso = PS[4] if dh2 % 2 == 0 else PS[5]
                st_ = Em2[:, 0:NQ]
                p.do("vector", lambda e, o=st_, i=pso, dh2=dh2: e.tensor_tensor(
                    o, i[:, 0:NQ], bpb[:, dh2 * NQ:(dh2 + 1) * NQ], ALU.add))
                p.do("vector", lambda e, o=st_, z=xn1t, dh2=dh2: e.tensor_tensor(
                    o, o, z[:, dh2 * NQ:(dh2 + 1) * NQ], ALU.add))
                p.do("vector", lambda e, o=st_, g=A, qt=qt, dh2=dh2: e.tensor_tensor(
                    o, o, g[:, qt * D + dh2 * NQ:qt * D + (dh2 + 1) * NQ], ALU.mult))
                p.do("vector", lambda e, o=st_, x=x1t, dh2=dh2: e.tensor_tensor(
                    o, o, x[:, dh2 * NQ:(dh2 + 1) * NQ], ALU.add), sem="vc")
                p.wait("scalar", "vc")
                p.dma("scalar", lambda e, s=st_, qt=qt, dh2=dh2: e.dma_start(
                    out=out_d[qt * 128:(qt + 1) * 128, dh2 * NQ:(dh2 + 1) * NQ], in_=s), sem="vld")
                p.wait("vector", "vld")

    p.wait("gpsimd", "vld")
    p.wait("gpsimd", "st")
    p.wait("gpsimd", "gdma")
    p.wait("sync", "vld")

    with nc.Block() as block:
        for name in ("sync", "tensor", "vector", "scalar", "gpsimd"):
            def make_body(nm):
                def body(eng):
                    for fn in p.items[nm]:
                        fn(eng)
                return body
            getattr(block, name)(make_body(name))

    es.close()
    return nc


_NC_CACHE = {}


def _get_nc(key, cfg):
    if key not in _NC_CACHE:
        _NC_CACHE[key] = build_nc(cfg)
    return _NC_CACHE[key]


def make_in_maps(I, cfg):
    T, D, HF, NH, E, NC = cfg["T"], cfg["D"], cfg["HF"], cfg["NH"], cfg["E"], cfg["NC"]
    N = NC * T
    x = np.ascontiguousarray(I["L0_attn_x"], dtype=np.float32)
    xf = np.ascontiguousarray(x.reshape(N, D))
    c = np.ascontiguousarray
    maps = []
    for ci in range(NC):
        m = dict(
            xfull=xf,
            xown=c(x[ci]),
            coff=np.full((128, 1), float(ci * T), dtype=np.float32),
            wq=c(I["L0_attn_Wq"]), wk=c(I["L0_attn_Wk"]), wv=c(I["L0_attn_Wv"]),
            bq=c(I["L0_attn_bq"]), bk=c(I["L0_attn_bk"]), bv=c(I["L0_attn_bv"]),
            bvr=c(I["L0_attn_bv"].reshape(1, D)),
            rw=c(I["L0_ffn_router_w"]), rb=c(I["L0_ffn_router_b"].reshape(E, 1)),
            wx=c(I["L0_ffn_W"][ci].reshape(D // 128, 128, HF // 128, 128).transpose(2, 1, 0, 3)),
            vx=c(I["L0_ffn_V"][ci].reshape(D // 128, 128, HF // 128, 128).transpose(2, 1, 0, 3)),
            wox=c(I["L0_ffn_W_out"][ci]),
            lnw=c(I["L1_attn_ln_w"].reshape(1, D)), lnb=c(I["L1_attn_ln_b"].reshape(1, D)),
            wqkv=c(I["L1_attn_W_qkv"].reshape(D // 128, 128, 3, NH // 2, 2 * D // NH).transpose(2, 3, 1, 0, 4).reshape(3 * NH // 2, 128, D // 128, 2 * D // NH)),
            bqkv=c(I["L1_attn_b_qkv"]),
            wproj=c(I["L1_attn_W_proj"]), bproj=c(I["L1_attn_b_proj"].reshape(1, D)),
            wg=c(I["L1_ffn_Wg"]), weo=c(I["L1_ffn_We"][ci]), be=c(I["L1_ffn_be"]),
        )
        maps.append(m)
    return maps


def kernel(**inputs):
    cfg = FULL_CFG
    nc = _get_nc("full", cfg)
    in_maps = make_in_maps(inputs, cfg)
    res = run_bass_kernel_spmd(nc, in_maps, core_ids=list(range(cfg["NC"])))
    out = np.stack([np.asarray(res.results[ci]["out"]) for ci in range(cfg["NC"])], axis=0)
    return out.astype(np.float32)


# version 21
# speedup vs baseline: 1.3527x; 1.0147x over previous
import os
import numpy as np
from contextlib import ExitStack

import concourse.bass as bass
import concourse.mybir as mybir
from concourse.bass import IndirectOffsetOnAxis
from concourse.bass_utils import run_bass_kernel_spmd
import ml_dtypes

F32 = mybir.dt.float32
BF16 = mybir.dt.bfloat16
U32 = mybir.dt.uint32
AX = mybir.AxisListType.X
ALU = mybir.AluOpType
ACT = mybir.ActivationFunctionType

FULL_CFG = dict(T=2048, D=1024, HF=2048, NH=16, E=8, NC=8)


class Prog:
    def __init__(self, nc):
        self.nc = nc
        self.items = {k: [] for k in ("sync", "tensor", "vector", "scalar", "gpsimd")}
        self.sems = {}
        self.cnt = {}

    def sem(self, name, es):
        self.sems[name] = es.enter_context(self.nc.semaphore(name))
        self.cnt[name] = 0

    def do(self, eng, fn, sem=None, n=1):
        if eng == "vector":
            hv = self.sems["vc"]
            prior = self.cnt["vc"]
            self.items[eng].append(lambda e, h=hv, v=prior: e.wait_ge(h, v))
            if sem is None or sem == "vc":
                sem, n = "vc", 1
            else:
                self.cnt["vc"] += 1
                self.cnt[sem] += n
                h2 = self.sems[sem]

                def wrapped2(e, fn=fn, h2=h2, n=n, hv=hv):
                    fn(e).then_inc(h2, n).then_inc(hv, 1)

                self.items[eng].append(wrapped2)
                return
        if sem is not None:
            self.cnt[sem] += n
            h = self.sems[sem]

            def wrapped(e, fn=fn, h=h, n=n):
                fn(e).then_inc(h, n)

            self.items[eng].append(wrapped)
        else:
            self.items[eng].append(fn)

    QSEM = {"sync": "qs", "scalar": "qa", "gpsimd": "qg"}

    def dma(self, eng, fn, sem=None, handshake=True):
        q = self.QSEM[eng]
        self.cnt[q] += 16
        hq = self.sems[q]
        qv = self.cnt[q]
        self.items[eng].append(lambda e, fn=fn, hq=hq: fn(e).then_inc(hq, 16))
        if handshake:
            self.qsync(eng, sem)

    def qsync(self, eng, sem):
        q = self.QSEM[eng]
        hq = self.sems[q]
        qv = self.cnt[q]
        self.cnt[sem] += 1
        hs = self.sems[sem]
        self.items[eng].append(lambda e, hq=hq, qv=qv, hs=hs: e.wait_ge(hq, qv).then_inc(hs, 1))

    def wait(self, eng, sem, val=None):
        h = self.sems[sem]
        v = self.cnt[sem] if val is None else val
        self.items[eng].append(lambda e, h=h, v=v: e.wait_ge(h, v))

    def v(self, sem):
        return self.cnt[sem]


def build_nc(cfg):
    T, D, HF, NH, E, NC = cfg["T"], cfg["D"], cfg["HF"], cfg["NH"], cfg["E"], cfg["NC"]
    DC = D // 128
    TT = T // 128
    N = NC * T
    CAP = T
    DH = D // NH
    HC = HF // 128
    A_ = N // 128
    NQ = 512 if T >= 512 else T
    QG = T // NQ
    L0SC = 1.0 / float(np.sqrt(D))
    L1SC = 1.0 / float(np.sqrt(DH))

    nc = bass.Bass("TRN2", target_bir_lowering=False)

    def din(name, shape, dt=F32):
        return nc.dram_tensor(name, list(shape), dt, kind="ExternalInput").ap()

    xfull = din("xfull", (N, D))
    xown = din("xown", (T, D))
    coff = din("coff", (128, 1))
    wq_d = din("wq", (D, D)); wk_d = din("wk", (D, D)); wv_d = din("wv", (D, D))
    bq_d = din("bq", (D,)); bk_d = din("bk", (D,)); bv_d = din("bv", (D,))
    rw_d = din("rw", (E, D)); rb_d = din("rb", (E, 1))
    wx_d = din("wx", (HF // 128, 128, D // 128, 128)); vx_d = din("vx", (HF // 128, 128, D // 128, 128))
    wox_d = din("wox", (HF, D))
    lnw_d = din("lnw", (1, D)); lnb_d = din("lnb", (1, D))
    wqkv_d = din("wqkv", (3 * NH // 2, 128, D // 128, 2 * D // NH)); bqkv_d = din("bqkv", (3 * D,))
    wproj_d = din("wproj", (D, D)); bproj_d = din("bproj", (1, D))
    wg_d = din("wg", (E, D)); weo_d = din("weo", (D, D)); be_d = din("be", (E, D))
    bvr_d = din("bvr", (1, D))
    out_d = nc.dram_tensor("out", [T, D], F32, kind="ExternalOutput").ap()

    rt_in = nc.dram_tensor("rt_in", [T, 16], F32)
    rt_all = nc.dram_tensor("rt_all", [N, 16], F32, addr_space="Shared")
    y_in = nc.dram_tensor("y_in", [T, D], F32)
    y_all = nc.dram_tensor("y_all", [N, D], F32, addr_space="Shared")
    wet_in = nc.dram_tensor("wet_in", [D, 2 * D], BF16)
    wet_all = nc.dram_tensor("wet_all", [E * D, 2 * D], BF16, addr_space="Shared")
    DBGV = os.environ.get("DBG_LB") == "1"
    Lb = nc.dram_tensor("Lb", [N + 2 * T, 1], U32, addr_space="Shared")
    Lpart = nc.dram_tensor("Lpart", [N + 2 * T, 1], U32)
    rtw_d = nc.dram_tensor("rtw", [N, 4], F32)
    if DBGV:
        dbgLb = nc.dram_tensor("dbgLb", [N + 2 * T, 1], U32, kind="ExternalOutput")
        dbgRtw = nc.dram_tensor("dbgRtw", [N, 4], F32, kind="ExternalOutput")
        dbgY = nc.dram_tensor("dbgY", [N, D], F32, kind="ExternalOutput")
        dbgS = nc.dram_tensor("dbgS", [T, D], F32, kind="ExternalOutput")
        dbgX1 = nc.dram_tensor("dbgX1", [T, D], F32, kind="ExternalOutput")
        dbgB = nc.dram_tensor("dbgB", [1, 4096], F32, kind="ExternalOutput")
        dbgVs = nc.dram_tensor("dbgVs", [T, D], F32, kind="ExternalOutput")
        dbgKh = nc.dram_tensor("dbgKh", [D, T], BF16, kind="ExternalOutput")
        dbgKl = nc.dram_tensor("dbgKl", [D, T], BF16, kind="ExternalOutput")
    rtd_d = nc.dram_tensor("rtd", [N, 2], U32)
    ksp_h = nc.dram_tensor("ksp_h", [D, T], BF16)
    ksp_l = nc.dram_tensor("ksp_l", [D, T], BF16)
    vsp = nc.dram_tensor("vsp", [T, D], F32)
    x1sp = nc.dram_tensor("x1sp", [T, D], F32)
    ssp = nc.dram_tensor("ssp", [T, D], F32)
    msp = nc.dram_tensor("msp", [D, T], F32)
    bcb = nc.dram_tensor("bcb", [1, 4096], F32)

    idf = nc.inline_tensor(np.eye(128, dtype=np.float32), name="idf").ap()
    idb = nc.inline_tensor(np.eye(128).astype(ml_dtypes.bfloat16), name="idb").ap()
    utS = nc.inline_tensor(np.triu(np.ones((128, 128), np.float32), 1), name="utS").ap()
    on1 = nc.inline_tensor(np.ones((128, 1), np.float32), name="on1").ap()
    op2_np = np.zeros((128, 2), np.float32)
    op2_np[0:DH, 0] = 1.0
    op2_np[DH:2 * DH, 1] = 1.0
    op2 = nc.inline_tensor(op2_np, name="op2").ap()
    io8 = nc.inline_tensor(np.arange(E, dtype=np.float32).reshape(1, E), name="io8").ap()
    trm = nc.inline_tensor((np.arange(128) % (128 // NC) * 2.0 * A_).astype(np.float32).reshape(128, 1),
                           name="trm").ap()

    es = ExitStack()
    sb = lambda name, shape, dt=F32: es.enter_context(nc.sbuf_tensor(name, list(shape), dt))
    psf = lambda name: es.enter_context(nc.psum_tensor(name, [128, 512], F32))

    TCH = 512 if T >= 512 else T
    A = sb("A", (128, TT * D))
    B = sb("B", (128, max(TT * D, 100 * A_)))
    C = sb("C", (128, max(TT * NQ, HC * TCH, DC * DC * 128, DC * D, T + TT * 132 + (NH // 2) * NQ)))
    Dm = sb("Dm", (128, 4096 if D >= 1024 else 4 * D))
    Em = sb("Em", (128, 2048))
    Em2 = sb("Em2", (128, 2048))
    Mm = sb("Mm", (128, 960))
    RTS = sb("RTS", (128, 6 * A_))
    K2 = sb("K2", (128, 1664))
    K2u = K2[:].bitcast(U32)
    DS = K2u[:, 973:973 + 256]
    IDS = K2u[:, 685:685 + A_]
    IDT = K2u[:, 813:813 + 128]
    OWN = K2u[:, 941:941 + TT]
    LBT = K2u[:, 957:957 + TT]
    SMALL = K2[:, 1229:1229 + 256]

    PS = [psf(f"PS{i}") for i in range(8)]
    PTv = PS[7][:].bitcast(BF16)

    p = Prog(nc)
    for s in ("ld", "ld2", "st", "vld", "gdma", "ccw", "ccr", "ccy", "ccl", "vc", "sc", "mm", "gc",
              "qs", "qa", "qg"):
        p.sem(s, es)

    def bc16(t, c0_f32, n_bf16):
        return t[:].bitcast(BF16)[:, 2 * c0_f32:2 * c0_f32 + n_bf16]

    def u32v(t, c0, n):
        return t[:].bitcast(U32)[:, c0:c0 + n]

    def split_hilo(src_ap, hi_ap, lo_ap):
        p.wait("scalar", "mm")
        p.wait("scalar", "vc")
        p.do("scalar", lambda e, s=src_ap, h=hi_ap: e.activation(h, s, ACT.Copy), sem="sc")
        p.wait("vector", "sc")
        p.do("vector", lambda e, s=src_ap, h=hi_ap, l=lo_ap: e.tensor_sub(l, s, h), sem="vc")

    def ld(dst, src, sem="ld", eng="sync", slow=False):
        p.dma(eng, lambda e, d=dst, s=src, sl=slow: e.dma_start(out=d, in_=s, allow_slow_non_contiguous=sl), sem=sem)

    hp2_ = 2 * (D // NH)
    K2b = K2[:].bitcast(BF16)
    idf_s = K2[:, 0:128]
    utS_s = K2[:, 128:256]
    idb_s = K2b[:, 512:640]
    on1_s = K2[:, 320:321]
    op2_s = K2[:, 321:323]
    bq_s = K2[:, 323:323 + DC]; bk_s = K2[:, 331:331 + DC]
    bqkv_s = K2[0:hp2_, 469:469 + 3 * D // hp2_]
    rb_s = K2[0:E, 339:340]
    coff_s = K2[:, 340:341]
    rws_st = K2[0:E, 341:469]
    rwT_h = K2b[:, 1114:1114 + DC * E]; rwT_l = K2b[:, 1178:1178 + DC * E]
    wgT_h = K2b[:, 1242:1242 + DC * E]; wgT_l = K2b[:, 1306:1306 + DC * E]
    rwT_f = K2[:, 493:493 + DC * E]

    ld(idf_s, idf)
    ld(utS_s, utS)
    ld(on1_s, on1)
    ld(op2_s, op2)
    ld(idb_s, idb, sem="gdma", eng="gpsimd")
    ld(bq_s[:], bq_d.rearrange("(a p) -> p a", p=128), slow=True)
    ld(bk_s[:], bk_d.rearrange("(a p) -> p a", p=128), slow=True)
    ld(bqkv_s[:], bqkv_d.rearrange("(a p) -> p a", p=hp2_), slow=True)
    ld(rb_s[:], rb_d)
    ld(coff_s, coff)
    io8_bc = SMALL[:, 32:32 + E]
    ld(io8_bc, io8.broadcast_to((128, E)))
    ld(SMALL[:, 225:226], trm)

    p.wait("tensor", "ld")
    for wsrc, dsth, dstl in ((rw_d, rwT_h, rwT_l), (wg_d, wgT_h, wgT_l)):
        for dc in range(DC):
            p.wait("sync", "mm")
            ld(rws_st, wsrc[:, dc * 128:(dc + 1) * 128])
            p.wait("tensor", "ld")
            p.do("tensor", lambda e, dc=dc: e.transpose(
                PS[5][:, dc * E:(dc + 1) * E], rws_st[0:E, :], idf_s[0:E, 0:E]),
                sem="mm" if True else None)
        p.wait("vector", "mm")
        p.do("vector", lambda e: e.tensor_copy(rwT_f[:, 0:DC * E], PS[5][:, 0:DC * E]), sem="vc")
        split_hilo(rwT_f[:, 0:DC * E], dsth[:, 0:DC * E], dstl[:, 0:DC * E])
        p.wait("tensor", "vc")

    p.do("gpsimd", lambda e: e.iota(IDS, pattern=[[1, A_]], base=0, channel_multiplier=A_,
                                    allow_small_or_imprecise_dtypes=True), sem="gc")
    p.do("gpsimd", lambda e: e.iota(SMALL[:, 200:200 + TT], pattern=[[128, TT]], base=0, channel_multiplier=1,
                                    allow_small_or_imprecise_dtypes=True), sem="gc")
    p.wait("vector", "gc")
    p.wait("vector", "ld")
    p.do("vector", lambda e: e.tensor_scalar(SMALL[:, 200:200 + TT], SMALL[:, 200:200 + TT], coff_s[:, 0:1], None, ALU.add), sem="vc")
    p.do("vector", lambda e: e.tensor_copy(OWN, SMALL[:, 200:200 + TT]), sem="vc")
    nlb = (N + 2 * T) // 128
    p.do("vector", lambda e: e.memset(u32v(Mm, 800, nlb), 0), sem="vc")
    p.wait("sync", "vc")
    p.dma("sync", lambda e: e.dma_start(out=Lpart.ap().rearrange("(p a) o -> p (a o)", p=128),
                                       in_=u32v(Mm, 800, nlb)), sem="st")

    p.wait("tensor", "gdma")
    for oc in range(DC):
        wof = Dm[:, 0:D]
        p.wait("sync", "vc")
        p.wait("sync", "mm")
        ld(wof, weo_d[oc * 128:(oc + 1) * 128, :])
        woh = bc16(Dm, D, D); wol = bc16(Dm, D + D // 2, D)
        p.wait("scalar", "ld")
        split_hilo(wof, woh, wol)
        for half, src in ((0, woh), (1, wol)):
            p.wait("tensor", "vc")
            for dc in range(DC):
                p.do("tensor", lambda e, s=src, dc=dc: e.transpose(
                    PTv[:, dc * 128:(dc + 1) * 128], s[:, dc * 128:(dc + 1) * 128], idb_s),
                    sem="mm" if dc == DC - 1 else None)
            p.wait("vector", "mm")
            dst = bc16(C, 0, DC * 2 * D).rearrange("p (a c) -> p a c", a=DC)[:, :, half * D + oc * 128:half * D + (oc + 1) * 128]
            p.do("vector", lambda e, d=dst: e.tensor_copy(
                d, PTv[:, 0:DC * 128].rearrange("p (a c) -> p a c", a=DC)), sem="vc")
    p.wait("scalar", "vc")
    p.dma("scalar", lambda e: e.dma_start(out=wet_in.ap().rearrange("(a p) c -> p a c", p=128),
                                          in_=bc16(C, 0, DC * 2 * D).rearrange("p (a c) -> p a c", a=DC)), sem="vld")
    p.wait("gpsimd", "vld")
    p.do("gpsimd", lambda e: e.collective_compute(
        "AllGather", ALU.bypass, replica_groups=[list(range(NC))],
        ins=[wet_in.ap()], outs=[wet_all.ap()]), sem="ccw")

    MU = Mm[:, 0:TT]; RS = Mm[:, TT:2 * TT]
    MU1 = Mm[:, 2 * TT:3 * TT]; RS1 = Mm[:, 3 * TT:4 * TT]
    MU2 = Mm[:, 4 * TT:5 * TT]; RS2 = Mm[:, 5 * TT:6 * TT]
    CW = Mm[:, 6 * TT:6 * TT + TT * E]
    negc0 = SMALL[:, 0:1]
    negcH = SMALL[:, 4:4 + 2 * ((NH + 1) // 2)]

    xnh = lambda dc: bc16(B, 0, 2 * DC * T)[:, dc * T:(dc + 1) * T]
    xnl = lambda dc: bc16(B, 0, 2 * DC * T)[:, DC * T + dc * T:DC * T + (dc + 1) * T]

    NCH = (D + 511) // 512
    def stats_pass(xt, MUc, RSc, j):
        p.wait("vector", "vld")
        bst = Em[:, 0:6 * NCH]; bag = Em[:, 8 * NCH:8 * NCH + 2]
        for ch in range(NCH):
            p.do("vector", lambda e, x=xt, ch=ch: e.bn_stats(
                Em[:, 6 * ch:6 * ch + 6], x[:, 512 * ch:min(512 * (ch + 1), D)]))
        p.do("vector", lambda e, b=bst, a=bag: e.bn_aggr(a, b))
        p.do("vector", lambda e, a=bag, j=j, M=MUc: e.tensor_copy(M[:, j:j + 1], a[:, 0:1]))
        p.do("vector", lambda e, a=bag: e.tensor_single_scalar(Em[:, 8 * NCH + 4:8 * NCH + 5], a[:, 1:2], 1e-5, ALU.add), sem="vc")
        p.wait("scalar", "vc")
        p.do("scalar", lambda e: e.activation(Em[:, 8 * NCH + 5:8 * NCH + 6], Em[:, 8 * NCH + 4:8 * NCH + 5], ACT.Sqrt), sem="sc")
        p.wait("vector", "sc")
        p.do("vector", lambda e, j=j, R=RSc: e.reciprocal(R[:, j:j + 1], Em[:, 8 * NCH + 5:8 * NCH + 6]), sem="vc")

    def split_transpose_to(dst_arena, j, src_f32):
        hi_t = bc16(Em2, 0, D); lo_t = bc16(Em2, D // 2, D)
        split_hilo(src_f32, hi_t, lo_t)
        for src, base in ((hi_t, 0), (lo_t, DC * T)):
            p.wait("tensor", "vc")
            for dc in range(DC):
                p.do("tensor", lambda e, s=src, dc=dc: e.transpose(
                    PTv[:, dc * 128:(dc + 1) * 128], s[:, dc * 128:(dc + 1) * 128], idb_s),
                    sem="mm" if dc == DC - 1 else None)
            p.wait("vector", "mm")
            dstap = bc16(dst_arena, 0, 2 * DC * T).rearrange("p (c t) -> p c t", c=2 * DC)[:, base // T:base // T + DC, j * 128:(j + 1) * 128]
            p.do("vector", lambda e, d=dstap: e.tensor_copy(d, PTv[:, 0:DC * 128].rearrange("p (c t) -> p c t", c=DC)), sem="vc")
        p.wait("tensor", "vc")

    p.wait("vector", "vld")
    for j in range(TT):
        xt = A[:, j * D:(j + 1) * D]
        ld(xt, xown[j * 128:(j + 1) * 128, :])
        p.wait("vector", "ld")
        stats_pass(xt, MU, RS, j)
        xnt = Em[:, 1024:1024 + D] if D <= 1024 else Em[:, 0:D]
        p.do("vector", lambda e, x=xt, o=xnt, j=j: e.tensor_scalar(o, x, MU[:, j:j + 1], RS[:, j:j + 1], ALU.subtract, ALU.mult), sem="vc")
        split_transpose_to(B, j, xnt)

    LgF = Em2[0:E, 0:T]
    for ts in range(T // NQ):
        p.wait("tensor", "vc")
        for pp in range(3):
            for dc in range(DC):
                lh = (rwT_h if pp < 2 else rwT_l)[:, dc * E:(dc + 1) * E]
                rh = (xnh(dc) if pp != 1 else xnl(dc))[:, ts * NQ:(ts + 1) * NQ]
                first = (pp == 0 and dc == 0); last = (pp == 2 and dc == DC - 1)
                p.do("tensor", lambda e, l=lh, r=rh, f=first, s=last: e.matmul(
                    PS[6][0:E, 0:NQ], l, r, start=f, stop=s), sem="mm" if last else None)
        p.wait("vector", "mm")
        p.do("vector", lambda e, ts=ts: e.tensor_scalar(
            LgF[:, ts * NQ:(ts + 1) * NQ], PS[6][0:E, 0:NQ], rb_s[0:E, 0:1], None, ALU.add), sem="vc")
    p.wait("tensor", "vc")
    for j in range(TT):
        p.wait("tensor", "vc")
        p.do("tensor", lambda e, j=j: e.transpose(PS[5][:, 0:E], LgF[:, j * 128:(j + 1) * 128], idf_s[0:E, 0:E]), sem="mm")
        p.wait("vector", "mm")
        p.wait("vector", "vld", p.v("vld"))
        stg0 = Em[:, 0:16]
        p.do("vector", lambda e, s=stg0: e.tensor_copy(s[:, 0:E], PS[5][:, 0:E]))
        p.do("vector", lambda e, j=j, s=stg0: e.tensor_copy(s[:, E:E + 1], MU[:, j:j + 1]))
        p.do("vector", lambda e, j=j, s=stg0: e.tensor_copy(s[:, E + 1:E + 2], RS[:, j:j + 1]))
        p.do("vector", lambda e, s=stg0: e.memset(s[:, E + 2:16], 0.0), sem="vc")
        p.wait("scalar", "vc")
        p.dma("scalar", lambda e, j=j, s=stg0: e.dma_start(out=rt_in.ap()[j * 128:(j + 1) * 128, :], in_=s), sem="vld")
    p.wait("gpsimd", "vld")
    p.do("gpsimd", lambda e: e.collective_compute(
        "AllGather", ALU.bypass, replica_groups=[list(range(NC))],
        ins=[rt_in.ap()], outs=[rt_all.ap()]), sem="ccr")

    def load_w_full(wd):
        for dc in range(DC):
            blk = Dm[:, 0:D]
            p.wait("sync", "vc")
            ld(blk, wd[dc * 128:(dc + 1) * 128, :])
            p.wait("scalar", "ld")
            hi = C[:].bitcast(BF16)[:, dc * D:(dc + 1) * D]
            lo = C[:].bitcast(BF16)[:, DC * D + dc * D:DC * D + (dc + 1) * D]
            split_hilo(blk, hi, lo)
        p.wait("tensor", "vc")

    def wch(hi_side, oc, dc):
        base = (0 if hi_side else DC * D) + dc * D + oc * 128
        return C[:].bitcast(BF16)[:, base:base + 128]


    for wi, wd in enumerate((wq_d, wk_d, wv_d)):
        load_w_full(wd)
        if wi < 2:
            for ts in range(T // NQ):
                for oc in range(DC):
                    p.wait("tensor", "vc")
                    p.wait("vector", "vld", p.v("vld"))
                    pso = PS[0] if oc % 2 == 0 else PS[1]
                    for pp in range(3):
                        for dc in range(DC):
                            lh = wch(pp < 2, oc, dc)
                            rh = (xnh(dc) if pp != 1 else xnl(dc))[:, ts * NQ:(ts + 1) * NQ]
                            first = (pp == 0 and dc == 0); last = (pp == 2 and dc == DC - 1)
                            p.do("tensor", lambda e, o=pso, l=lh, r=rh, f=first, s=last: e.matmul(
                                o[:, 0:NQ], l, r, start=f, stop=s), sem="mm" if last else None)
                    p.wait("vector", "mm")
                    q32 = Em[:, 0:NQ]
                    bias = bq_s if wi == 0 else bk_s
                    p.do("vector", lambda e, o=q32, i=pso, b=bias, oc=oc: e.tensor_scalar(
                        o, i[:, 0:NQ], b[:, oc:oc + 1], None, ALU.add), sem="vc")
                    hi_t = bc16(Em, NQ, NQ); lo_t = bc16(Em, NQ + NQ // 2, NQ)
                    split_hilo(q32, hi_t, lo_t)
                    if wi == 0:
                        dh_ = bc16(A, 0, 2 * DC * T)[:, oc * T + ts * NQ:oc * T + (ts + 1) * NQ]
                        dl_ = bc16(A, 0, 2 * DC * T)[:, DC * T + oc * T + ts * NQ:DC * T + oc * T + (ts + 1) * NQ]
                        p.do("vector", lambda e, d=dh_, s=hi_t: e.tensor_copy(d, s))
                        p.do("vector", lambda e, d=dl_, s=lo_t: e.tensor_copy(d, s), sem="vc")
                    else:
                        p.wait("scalar", "vc")
                        p.dma("scalar", lambda e, s=hi_t, oc=oc, ts=ts: e.dma_start(
                            out=ksp_h.ap()[oc * 128:(oc + 1) * 128, ts * NQ:(ts + 1) * NQ], in_=s), sem="vld")
                        p.wait("scalar", "vc")
                        p.dma("scalar", lambda e, s=lo_t, oc=oc, ts=ts: e.dma_start(
                            out=ksp_l.ap()[oc * 128:(oc + 1) * 128, ts * NQ:(ts + 1) * NQ], in_=s), sem="vld")
                    sq = Em[:, 1536:1536 + NQ] if NQ <= 512 else Em[:, NQ:2 * NQ]
                    p.do("scalar", lambda e, s=sq, i=q32: e.activation(s, i, ACT.Square), sem="sc")
                    p.wait("tensor", "sc")
                    p.do("tensor", lambda e, s=sq, f=(oc == 0), l=(oc == DC - 1): e.matmul(
                        PS[6][0:1, 0:NQ], on1_s, s, start=f, stop=l), sem="mm" if oc == DC - 1 else None)
                p.wait("vector", "mm")
                ncol = 16 if wi == 0 else 17
                p.do("vector", lambda e, c=ncol: e.reduce_max(SMALL[0:1, 64 + c:65 + c], PS[6][0:1, 0:NQ], axis=AX))
                if ts == 0:
                    p.do("vector", lambda e, c=ncol: e.tensor_copy(SMALL[0:1, c:c + 1], SMALL[0:1, 64 + c:65 + c]), sem="vc")
                else:
                    p.do("vector", lambda e, c=ncol: e.tensor_tensor(SMALL[0:1, c:c + 1], SMALL[0:1, c:c + 1], SMALL[0:1, 64 + c:65 + c], ALU.max), sem="vc")
        else:
            bv_bc = Em2[:, D:2 * D] if D <= 1024 else Em2[:, 0:D]
            ld(bv_bc, bvr_d.broadcast_to((128, D)), sem="ld2")
            p.wait("vector", "ld2")
            for j in range(TT):
                for half in range(D // NQ):
                    p.wait("tensor", "vc")
                    p.wait("vector", "vld", p.v("vld"))
                    pso = PS[2] if half % 2 == 0 else PS[3]
                    for ri in range(NQ // 128):
                        oc2 = half * (NQ // 128) + ri
                        for pp in range(3):
                            for dc in range(DC):
                                lh = (xnh(dc) if pp != 2 else xnl(dc))[:, j * 128:(j + 1) * 128]
                                rh = wch(pp != 1, oc2, dc)
                                first = (pp == 0 and dc == 0); last = (pp == 2 and dc == DC - 1)
                                p.do("tensor", lambda e, o=pso, l=lh, r=rh, f=first, s=last, ri=ri: e.matmul(
                                    o[:, ri * 128:(ri + 1) * 128], l, r, start=f, stop=s),
                                    sem="mm" if last and ri == NQ // 128 - 1 else None)
                    p.wait("vector", "mm")
                    vv = Em[:, 0:NQ]
                    p.do("vector", lambda e, o=vv, i=pso, h=half, b=bv_bc: e.tensor_add(
                        o, i[:, 0:NQ], b[:, h * NQ:(h + 1) * NQ]), sem="vc")
                    p.wait("scalar", "vc")
                    p.dma("scalar", lambda e, s=vv, j=j, h=half: e.dma_start(
                        out=vsp.ap()[j * 128:(j + 1) * 128, h * NQ:(h + 1) * NQ], in_=s), sem="vld")

    p.wait("scalar", "vc")
    p.do("scalar", lambda e: e.activation(SMALL[0:1, 18:19], SMALL[0:1, 16:17], ACT.Sqrt))
    p.do("scalar", lambda e: e.activation(SMALL[0:1, 19:20], SMALL[0:1, 17:18], ACT.Sqrt), sem="sc")
    p.wait("vector", "sc")
    p.do("vector", lambda e: e.tensor_tensor(SMALL[0:1, 20:21], SMALL[0:1, 18:19], SMALL[0:1, 19:20], ALU.mult))
    p.do("vector", lambda e: e.tensor_single_scalar(SMALL[0:1, 21:22], SMALL[0:1, 20:21], -L0SC, ALU.mult), sem="vc")
    p.wait("scalar", "vc")
    p.dma("scalar", lambda e: e.dma_start(out=bcb.ap()[0:1, 0:1], in_=SMALL[0:1, 21:22]), sem="vld")
    p.wait("sync", "vld")
    ld(negc0, bcb.ap()[0:1, 0:1].broadcast_to((128, 1)), sem="ld2")

    p.wait("sync", "ccr")
    p.wait("sync", "mm")
    RAW = B[:, 0:16 * A_]
    ld(RAW, rt_all.ap().rearrange("(p a) c -> p (a c)", p=128), sem="ld2")
    RAW3 = RAW.rearrange("p (a c) -> p a c", c=16)
    LG = RAW3[:, :, 0:E]
    co = [16 * A_]

    def arr(n):
        r = B[:, co[0]:co[0] + n]
        co[0] += n
        return r

    V0 = arr(A_); V1 = arr(A_)
    OH0 = arr(A_ * E); OH1 = arr(A_ * E); LM = arr(A_ * E)
    HTEA = arr(A_ * E); PRF = arr(A_ * E)
    BS = arr(E)
    P0s = arr(A_); P1s = arr(A_); E0v = arr(A_); E1v = arr(A_)
    W0 = arr(A_); W1 = arr(A_); VA0 = arr(A_); VA1 = arr(A_)
    DC0 = arr(A_); DC1 = arr(A_); DSS0 = arr(A_); DSS1 = arr(A_)
    T1 = arr(A_); T2 = arr(A_ * E)
    TR0 = arr(A_); TR1 = arr(A_)

    d3 = lambda t: t.rearrange("p (a c) -> p a c", c=E)
    dea = lambda t: t.rearrange("p (c a) -> p c a", c=E)
    c1 = lambda t: t.rearrange("p (a o) -> p a o", o=1)
    V = "vector"

    p.wait("gpsimd", "mm")
    p.do("gpsimd", lambda e: e.iota(TR0, pattern=[[2, A_]], base=N, channel_multiplier=0,
                                    allow_small_or_imprecise_dtypes=True), sem="gc")
    p.wait(V, "ld2")
    p.wait(V, "gc")
    p.do(V, lambda e: e.tensor_scalar(TR0, TR0, SMALL[:, 225:226], None, ALU.add))
    p.do(V, lambda e: e.tensor_single_scalar(TR1, TR0, 1.0, ALU.add), sem="vc")
    p.do(V, lambda e: e.reduce_max(c1(V0), LG, axis=AX))
    p.do(V, lambda e: e.tensor_tensor(d3(OH0), LG, c1(V0).broadcast_to((128, A_, E)), ALU.is_equal))
    p.do(V, lambda e: e.scalar_tensor_tensor(d3(LM), d3(OH0), -1e30, LG, ALU.mult, ALU.add))
    p.do(V, lambda e: e.reduce_max(c1(V1), d3(LM), axis=AX))
    p.do(V, lambda e: e.tensor_tensor(d3(OH1), d3(LM), c1(V1).broadcast_to((128, A_, E)), ALU.is_equal))
    p.do(V, lambda e: e.tensor_tensor(dea(HTEA).rearrange("p c a -> p a c"), d3(OH0), d3(OH1), ALU.add))
    for c_ in range(E):
        p.do(V, lambda e, c_=c_: e.tensor_tensor_scan(
            dea(PRF)[:, c_, :], dea(HTEA)[:, c_, :], dea(HTEA)[:, c_, :], 0.0, ALU.add, ALU.bypass),
            sem="vc" if c_ == E - 1 else None)
    p.wait("tensor", "vc")
    p.do("tensor", lambda e: e.matmul(PS[6][:, 0:E], utS_s, dea(PRF)[:, :, A_ - 1], start=True, stop=True), sem="mm")
    p.wait(V, "mm")
    p.do(V, lambda e: e.tensor_copy(BS, PS[6][:, 0:E]))
    p.do(V, lambda e: e.tensor_tensor(dea(PRF), dea(PRF),
                                      BS.rearrange("p (c o) -> p c o", o=1).broadcast_to((128, E, A_)), ALU.add))
    for OH, Ps_, Ev_ in ((OH0, P0s, E0v), (OH1, P1s, E1v)):
        p.do(V, lambda e, OH=OH: e.tensor_tensor(d3(T2), d3(OH), dea(PRF).rearrange("p c a -> p a c"), ALU.mult))
        p.do(V, lambda e, Ps_=Ps_: e.reduce_sum(c1(Ps_), d3(T2), axis=AX))
        p.do(V, lambda e, OH=OH: e.tensor_tensor(d3(T2), d3(OH),
             io8_bc.rearrange("p (o c) -> p o c", o=1).broadcast_to((128, A_, E)), ALU.mult))
        p.do(V, lambda e, Ev_=Ev_: e.reduce_sum(c1(Ev_), d3(T2), axis=AX))
    p.do(V, lambda e: e.tensor_tensor(T1, V0, V1, ALU.subtract), sem="vc")
    p.wait("scalar", "vc")
    p.do("scalar", lambda e: e.activation(W0, T1, ACT.Sigmoid), sem="sc")
    p.wait(V, "sc")
    p.do(V, lambda e: e.tensor_single_scalar(W1, W0, 1.0, ALU.subtract))
    p.do(V, lambda e: e.tensor_single_scalar(W1, W1, -1.0, ALU.mult))
    for si_, (Ps_, Ev_, Va, Wv_, Dc, Ds) in enumerate(((P0s, E0v, VA0, W0, DC0, DSS0), (P1s, E1v, VA1, W1, DC1, DSS1))):
        p.do(V, lambda e, Ps_=Ps_, Va=Va: e.tensor_single_scalar(Va, Ps_, float(CAP), ALU.is_le))
        p.do(V, lambda e, Wv_=Wv_, Va=Va: e.tensor_tensor(Wv_, Wv_, Va, ALU.mult))
        p.do(V, lambda e, Ev_=Ev_, Ps_=Ps_, Dc=Dc: e.scalar_tensor_tensor(Dc, Ev_, float(CAP), Ps_, ALU.mult, ALU.add))
        p.do(V, lambda e, Dc=Dc: e.tensor_single_scalar(Dc, Dc, 1.0, ALU.subtract))
        TRt = TR0 if si_ == 0 else TR1
        p.do(V, lambda e, Ds=Ds, Dc=Dc, Va=Va: e.tensor_tensor(Ds, Dc, Va, ALU.mult))
        p.do(V, lambda e, Va=Va, T1=T1: e.tensor_scalar(T1, Va, -1.0, -1.0, ALU.mult, ALU.subtract))
        p.do(V, lambda e, T1=T1, TRt=TRt: e.tensor_tensor(T1, T1, TRt, ALU.mult))
        p.do(V, lambda e, Ds=Ds, T1=T1: e.tensor_tensor(Ds, Ds, T1, ALU.add))
        p.do(V, lambda e, Dc=Dc: e.tensor_single_scalar(Dc, Dc, float(N - 1), ALU.min), sem="vc")
    RTW3 = RTS[:, 0:4 * A_].rearrange("p (a c) -> p a c", c=4)
    p.do(V, lambda e: e.tensor_copy(RTW3[:, :, 0:1], c1(W0)))
    p.do(V, lambda e: e.tensor_copy(RTW3[:, :, 1:2], c1(W1)))
    p.do(V, lambda e: e.tensor_copy(RTW3[:, :, 2:3], c1(DSS0)))
    p.do(V, lambda e: e.tensor_copy(RTW3[:, :, 3:4], c1(DSS1)), sem="vc")
    RTD3 = u32v(RTS, 4 * A_, 2 * A_).rearrange("p (a c) -> p a c", c=2)
    p.do(V, lambda e: e.tensor_copy(RTD3[:, :, 0:1], c1(DC0)))
    p.do(V, lambda e: e.tensor_copy(RTD3[:, :, 1:2], c1(DC1)), sem="vc")
    p.wait("scalar", "vc")
    p.dma("scalar", lambda e: e.dma_start(out=rtw_d.ap().rearrange("(p a) c -> p a c", p=128), in_=RTW3), sem="vld")
    p.wait("scalar", "vc")
    p.dma("scalar", lambda e: e.dma_start(out=rtd_d.ap().rearrange("(p a) c -> p a c", p=128), in_=RTD3), sem="vld")
    p.wait("tensor", "vc")
    p.wait("sync", "vld")
    p.wait("sync", "mm")
    p.wait("sync", "vc")
    ld(bc16(B, 0, DC * T).rearrange("p (a c) -> p a c", a=DC),
       ksp_h.ap().rearrange("(a p) c -> p a c", p=128), sem="ld2")
    ld(bc16(B, DC * T // 2, DC * T).rearrange("p (a c) -> p a c", a=DC),
       ksp_l.ap().rearrange("(a p) c -> p a c", p=128), sem="ld2")
    kpre = p.v("ld2")
    p.wait("gpsimd", "vc")
    p.wait("gpsimd", "st")
    p.wait("gpsimd", "gc")
    p.wait("gpsimd", "ld2", kpre)
    p.wait("gpsimd", "vld")
    for j in range(TT):
        p.wait("gpsimd", "vc", p.v("vc"))
        p.dma("gpsimd", lambda e, j=j: e.indirect_dma_start(
            out=SMALL[:, 128:132], out_offset=None, in_=rtw_d.ap(),
            in_offset=IndirectOffsetOnAxis(ap=OWN[:, j:j + 1], axis=0)), sem="gdma")
        p.wait("vector", "gdma")
        p.do(V, lambda e: e.tensor_copy(DS[:, 0:2], SMALL[:, 130:132]), sem="vc")
        p.wait("gpsimd", "vc")
        for si in range(2):
            p.dma("gpsimd", lambda e, j=j, si=si: e.indirect_dma_start(
                out=Lpart.ap(), out_offset=IndirectOffsetOnAxis(ap=DS[:, si:si + 1], axis=0),
                in_=OWN[:, j:j + 1], in_offset=None), sem="gdma", handshake=False)
    p.qsync("gpsimd", "gdma")
    p.do("gpsimd", lambda e: e.collective_compute(
        "AllReduce", ALU.add, replica_groups=[list(range(NC))],
        ins=[Lpart.ap()], outs=[Lb.ap()]), sem="ccl")

    KTQ = TT // QG
    nqt = NQ // 128
    p.wait("tensor", "ld2", kpre)
    svld = []
    for qg in range(QG):
        ktmax = (qg + 1) * KTQ
        ksc = {}
        for kt in range(ktmax):
            kh = bc16(B, 0, DC * T); kl = bc16(B, DC * T // 2, DC * T)
            pso = PS[0] if kt % 2 == 0 else PS[1]
            if kt >= 2:
                p.wait("tensor", "sc", ksc[kt - 2])
            for pp in range(3):
                for dc in range(DC):
                    lh = (kh if pp < 2 else kl)[:, dc * T + kt * 128:dc * T + kt * 128 + 128]
                    qh_ = bc16(A, 0, 2 * DC * T)[:, dc * T + qg * NQ:dc * T + (qg + 1) * NQ]
                    ql_ = bc16(A, 0, 2 * DC * T)[:, DC * T + dc * T + qg * NQ:DC * T + dc * T + (qg + 1) * NQ]
                    rh = qh_ if pp != 1 else ql_
                    first = (pp == 0 and dc == 0); last = (pp == 2 and dc == DC - 1)
                    p.do("tensor", lambda e, o=pso, l=lh, r=rh, f=first, s=last: e.matmul(
                        o[:, 0:NQ], l, r, start=f, stop=s), sem="mm" if last else None)
            p.wait("scalar", "mm")
            Edst = C[:, kt * NQ:(kt + 1) * NQ]
            p.do("scalar", lambda e, d=Edst, i=pso: e.activation(
                d, i[:, 0:NQ], ACT.Exp, bias=negc0[:, 0:1], scale=L0SC), sem="sc")
            ksc[kt] = p.v("sc")
            if (kt + 1) * 128 > qg * NQ:
                p.wait("gpsimd", "sc")
                base = qg * NQ - kt * 128
                p.do("gpsimd", lambda e, d=Edst, b=base: e.affine_select(
                    d, d, pattern=[[1, NQ]], compare_op=ALU.is_ge, fill=0.0, base=b, channel_multiplier=-1), sem="gc")
        p.wait("tensor", "sc")
        p.wait("tensor", "gc")
        for dh2 in range(D // NQ):
            vmm = {}
            vld2 = {}

            def v_load(kt, dh2=dh2, qg=qg, vmm=vmm, vld2=vld2):
                s = kt % 2
                if kt >= 2:
                    p.wait("sync", "mm", vmm[kt - 2])
                else:
                    p.wait("sync", "mm")
                ld(Em2[:, s * NQ:s * NQ + NQ], vsp.ap()[kt * 128:(kt + 1) * 128, dh2 * NQ:(dh2 + 1) * NQ], sem="ld2")
                vld2[kt] = p.v("ld2")

            v_load(0)
            if ktmax > 1:
                v_load(1)
            for kt in range(ktmax):
                vv = Em2[:, (kt % 2) * NQ:(kt % 2) * NQ + NQ]
                p.wait("tensor", "ld2", vld2[kt])
                for qt in range(nqt):
                    ktlim = qg * KTQ + ((qt * 128 + 127) // 128) + 1
                    ktlim = min(ktmax, qg * KTQ + qt + 1) if NQ == 512 else ktmax
                    if kt >= ktlim:
                        continue
                    p.do("tensor", lambda e, o=PS[2 + qt], l=C[:, kt * NQ + qt * 128:kt * NQ + qt * 128 + 128],
                         r=vv, f=(kt == 0), s=(kt == ktlim - 1): e.matmul(
                        o[:, 0:NQ], l, r, start=f, stop=s), sem="mm" if qt == nqt - 1 else None)
                vmm[kt] = p.v("mm")
                if kt + 2 < ktmax:
                    v_load(kt + 2)
            if dh2 == 0:
                for qt in range(nqt):
                    ktlim = min(ktmax, qg * KTQ + qt + 1) if NQ == 512 else ktmax
                    for kt in range(ktlim):
                        p.do("tensor", lambda e, l=C[:, kt * NQ + qt * 128:kt * NQ + qt * 128 + 128],
                             qt=qt, f=(kt == 0), s=(kt == ktlim - 1): e.matmul(
                            PS[6][:, qt:qt + 1], l, on1_s, start=f, stop=s, skip_group_check=True),
                            sem="mm" if kt == ktlim - 1 else None)
                p.wait("vector", "mm")
                p.do("vector", lambda e: e.reciprocal(SMALL[:, 64:64 + nqt], PS[6][:, 0:nqt]), sem="vc")
            p.wait("vector", "mm")
            for qt in range(nqt):
                tq = qg * nqt + qt
                sst = Em[:, 1024 + (qt % 2) * NQ:1024 + (qt % 2) * NQ + NQ]
                if len(svld) >= 2:
                    p.wait("vector", "vld", svld[-2])
                p.do("vector", lambda e, o=sst, i=PS[2 + qt], qt=qt:
                     e.tensor_scalar(o, i[:, 0:NQ], SMALL[:, 64 + qt:64 + qt + 1], None, ALU.mult), sem="vc")
                p.wait("scalar", "vc")
                p.dma("scalar", lambda e, s2=sst, tq=tq, dh2=dh2: e.dma_start(
                    out=ssp.ap()[tq * 128:(tq + 1) * 128, dh2 * NQ:(dh2 + 1) * NQ], in_=s2), sem="vld")
                svld.append(p.v("vld"))
            p.wait("tensor", "vc")

    p.wait("gpsimd", "ccl")
    if DBGV:
        p.dma("gpsimd", lambda e: e.dma_start(out=dbgLb.ap(), in_=Lb.ap()), sem="gdma")
        p.dma("gpsimd", lambda e: e.dma_start(out=dbgRtw.ap(), in_=rtw_d.ap()), sem="gdma")
        p.wait("gpsimd", "gdma")
    for j in range(TT):
        p.dma("gpsimd", lambda e, j=j: e.indirect_dma_start(
            out=LBT[:, j:j + 1], out_offset=None, in_=Lb.ap(),
            in_offset=IndirectOffsetOnAxis(ap=OWN[:, j:j + 1], axis=0)), sem="gdma")
    p.wait("gpsimd", "gdma")
    vcj = {}
    gldj = {}
    for j in range(TT + 1):
        if j < TT:
            Es = Em if j % 2 == 0 else Dm
            xg = Es[:, 0:D]
            stg = Es[:, 3 * D // 2:3 * D // 2 + 16]
            if j >= 2:
                p.wait("gpsimd", "vc", vcj[j - 2])
            else:
                p.wait("gpsimd", "vc", p.v("vc"))
            p.dma("gpsimd", lambda e, x=xg, j=j: e.indirect_dma_start(
                out=x, out_offset=None, in_=xfull,
                in_offset=IndirectOffsetOnAxis(ap=LBT[:, j:j + 1], axis=0)), sem="gdma")
            p.dma("gpsimd", lambda e, s=stg, j=j: e.indirect_dma_start(
                out=s, out_offset=None, in_=rt_all.ap(),
                in_offset=IndirectOffsetOnAxis(ap=LBT[:, j:j + 1], axis=0)), sem="gdma")
            gldj[j] = p.v("gdma")
        if j >= 1:
            jj = j - 1
            Es = Em if jj % 2 == 0 else Dm
            xg = Es[:, 0:D]
            stg = Es[:, 3 * D // 2:3 * D // 2 + 16]
            p.wait("vector", "gdma", gldj[jj])
            xe = Es[:, D // 2:D // 2 + D]
            p.do("vector", lambda e, x=xg, o=xe, s=stg: e.tensor_scalar(
                o, x, s[:, E:E + 1], s[:, E + 1:E + 2], ALU.subtract, ALU.mult), sem="vc")
            split_transpose_to(A, jj, xe)
            vcj[jj] = p.v("vc")
    xeh = lambda dc: bc16(A, 0, 2 * DC * T)[:, dc * T:(dc + 1) * T]
    xel = lambda dc: bc16(A, 0, 2 * DC * T)[:, DC * T + dc * T:DC * T + (dc + 1) * T]

    UH = HC * TCH
    for tc in range(T // TCH):
        units = [(hc, mi) for hc in range(HC) for mi in range(2)]
        mm_hist = {}
        vc_hist = {}

        uld = {}

        def u_load(u, units=units, mm_hist=mm_hist, vc_hist=vc_hist, tc=tc, uld=uld):
            hc, mi = units[u]
            s = u % 2
            wd = wx_d if mi == 0 else vx_d
            if u >= 2:
                p.wait("sync", "mm", mm_hist[u - 2])
                p.wait("sync", "vc", vc_hist[u - 2])
            else:
                p.wait("sync", "mm")
                p.wait("sync", "vc")
            blk = Dm[:, s * 2 * D:s * 2 * D + D]
            ld(blk, wd[hc, :, :, :].rearrange("p a c -> p (a c)"))
            uld[u] = p.v("ld")

        u_load(0)
        u_load(1)
        for u, (hc, mi) in enumerate(units):
            s = u % 2
            blk = Dm[:, s * 2 * D:s * 2 * D + D]
            hi = bc16(Dm, s * 2 * D + D, D); lo = bc16(Dm, s * 2 * D + D + D // 2, D)
            p.wait("scalar", "ld", uld[u])
            if u >= 2:
                p.wait("scalar", "mm", mm_hist[u - 2])
            split_hilo(blk, hi, lo)
            vc_hist[u] = p.v("vc")
            pso = PS[4 + (u % 4)]
            p.wait("tensor", "vc")
            for pp in range(3):
                for dc in range(DC):
                    lh = (hi if pp < 2 else lo).rearrange("p (a c) -> p a c", a=DC)[:, dc, :]
                    rh = (xeh(dc) if pp != 1 else xel(dc))[:, tc * TCH:(tc + 1) * TCH]
                    first = (pp == 0 and dc == 0); last = (pp == 2 and dc == DC - 1)
                    p.do("tensor", lambda e, o=pso, l=lh, r=rh, f=first, s=last: e.matmul(
                        o[:, 0:TCH], l, r, start=f, stop=s), sem="mm" if last else None)
            mm_hist[u] = p.v("mm")
            if u + 2 < len(units):
                u_load(u + 2)
            if mi == 1:
                pa = PS[4 + ((u - 1) % 4)]; pb = PS[4 + (u % 4)]
                p.wait("scalar", "mm")
                if u >= 4:
                    p.wait("scalar", "vc", vc_hist[u - 4])
                sl = Em[:, (hc % 2) * TCH:(hc % 2) * TCH + TCH]
                p.do("scalar", lambda e, o=sl, i=pb: e.activation(o, i[:, 0:TCH], ACT.Silu), sem="sc")
                p.wait("vector", "sc")
                uu = Em[:, 1024 + (hc % 2) * TCH:1024 + (hc % 2) * TCH + TCH] if TCH <= 512 else Em[:, TCH:2 * TCH]
                p.do("vector", lambda e, o=uu, s2=sl, i=pa: e.tensor_tensor(o, i[:, 0:TCH], s2, ALU.mult), sem="vc")
                split_hilo(uu, bc16(C, 0, 2 * UH)[:, hc * TCH:(hc + 1) * TCH],
                           bc16(C, 0, 2 * UH)[:, UH + hc * TCH:UH + (hc + 1) * TCH])
                vc_hist[u] = p.v("vc")
        p.wait("tensor", "vc")
        ymm = {}
        yld = {}

        def y_load(hc, tc=tc, ymm=ymm, yld=yld):
            s = hc % 2
            if hc >= 2:
                p.wait("sync", "mm", ymm[hc - 2])
            else:
                p.wait("sync", "mm")
                p.wait("sync", "vc")
            blk = Dm[:, s * 2 * D:s * 2 * D + D]
            ld(blk, wox_d[hc * 128:(hc + 1) * 128, :])
            yld[hc] = p.v("ld")

        y_load(0)
        y_load(1)
        for hc in range(HC):
            s = hc % 2
            blk = Dm[:, s * 2 * D:s * 2 * D + D]
            hi = bc16(Dm, s * 2 * D + D, D); lo = bc16(Dm, s * 2 * D + D + D // 2, D)
            p.wait("scalar", "ld", yld[hc])
            if hc >= 2:
                p.wait("scalar", "mm", ymm[hc - 2])
            split_hilo(blk, hi, lo)
            p.wait("tensor", "vc")
            for ts4 in range(TCH // 128):
                uh_ = bc16(C, 0, 2 * UH)[:, hc * TCH + ts4 * 128:hc * TCH + ts4 * 128 + 128]
                ul_ = bc16(C, 0, 2 * UH)[:, UH + hc * TCH + ts4 * 128:UH + hc * TCH + ts4 * 128 + 128]
                for dh2 in range(D // NQ):
                    pso = PS[ts4 * (D // NQ) + dh2]
                    for pp in range(3):
                        lh = uh_ if pp < 2 else ul_
                        rh = (hi if pp != 1 else lo)[:, dh2 * NQ:(dh2 + 1) * NQ]
                        first = (pp == 0 and hc == 0); last = (pp == 2 and hc == HC - 1)
                        p.do("tensor", lambda e, o=pso, l=lh, r=rh, f=first, s=last: e.matmul(
                            o[:, 0:NQ], l, r, start=f, stop=s),
                            sem="mm" if pp == 2 else None)
            ymm[hc] = p.v("mm")
            if hc + 2 < HC:
                y_load(hc + 2)
        p.wait("vector", "mm")
        for ts4 in range(TCH // 128):
            for dh2 in range(D // NQ):
                yv = Em[:, 0:NQ]
                p.do("vector", lambda e, o=yv, i=PS[ts4 * (D // NQ) + dh2]: e.tensor_copy(o, i[:, 0:NQ]), sem="vc")
                p.wait("scalar", "vc")
                p.dma("scalar", lambda e, s=yv, tc=tc, ts4=ts4, dh2=dh2: e.dma_start(
                    out=y_in.ap()[tc * TCH + ts4 * 128:tc * TCH + (ts4 + 1) * 128, dh2 * NQ:(dh2 + 1) * NQ],
                    in_=s), sem="vld")
                p.wait("vector", "vld")
    p.wait("gpsimd", "vld")
    p.do("gpsimd", lambda e: e.collective_compute(
        "AllGather", ALU.bypass, replica_groups=[list(range(NC))],
        ins=[y_in.ap()], outs=[y_all.ap()]), sem="ccy")

    p.wait("gpsimd", "ccy")
    if DBGV:
        p.dma("gpsimd", lambda e: e.dma_start(out=dbgY.ap(), in_=y_all.ap()), sem="gdma")
        p.dma("gpsimd", lambda e: e.dma_start(out=dbgS.ap(), in_=ssp.ap()), sem="gdma")
        p.dma("gpsimd", lambda e: e.dma_start(out=dbgB.ap(), in_=bcb.ap()), sem="gdma")
        p.dma("gpsimd", lambda e: e.dma_start(out=dbgVs.ap(), in_=vsp.ap()), sem="gdma")
        p.dma("gpsimd", lambda e: e.dma_start(out=dbgKh.ap(), in_=ksp_h.ap()), sem="gdma")
        p.dma("gpsimd", lambda e: e.dma_start(out=dbgKl.ap(), in_=ksp_l.ap()), sem="gdma")
        p.wait("gpsimd", "gdma")
    p.wait("vector", "mm")
    for j in range(TT):
        p.wait("sync", "vc")
        ld(Dm[:, 0:D], ssp.ap()[j * 128:(j + 1) * 128, :], sem="ld2")
        rdv = u32v(SMALL, 128, 2)
        rwv = SMALL[:, 132:136]
        p.wait("gpsimd", "vc", p.v("vc"))
        p.dma("gpsimd", lambda e, d=rdv, j=j: e.indirect_dma_start(
            out=d, out_offset=None, in_=rtd_d.ap(),
            in_offset=IndirectOffsetOnAxis(ap=OWN[:, j:j + 1], axis=0)), sem="gdma")
        p.dma("gpsimd", lambda e, d=rwv, j=j: e.indirect_dma_start(
            out=d, out_offset=None, in_=rtw_d.ap(),
            in_offset=IndirectOffsetOnAxis(ap=OWN[:, j:j + 1], axis=0)), sem="gdma")
        p.wait("gpsimd", "gdma")
        g0 = Em[:, 0:D]; g1v = Em2[:, 0:D]
        p.dma("gpsimd", lambda e, d=g0, i=rdv: e.indirect_dma_start(
            out=d, out_offset=None, in_=y_all.ap(),
            in_offset=IndirectOffsetOnAxis(ap=i[:, 0:1], axis=0)), sem="gdma")
        p.dma("gpsimd", lambda e, d=g1v, i=rdv: e.indirect_dma_start(
            out=d, out_offset=None, in_=y_all.ap(),
            in_offset=IndirectOffsetOnAxis(ap=i[:, 1:2], axis=0)), sem="gdma")
        xr = Em2[:, D:2 * D] if D <= 1024 else Em2[:, 0:D]
        p.wait("sync", "vc")
        ld(xr, xown[j * 128:(j + 1) * 128, :], sem="ld")
        p.wait("vector", "gdma")
        p.wait("vector", "ld")
        p.wait("vector", "ld2")
        mo = Em[:, D:2 * D] if D <= 1024 else Em[:, 0:D]
        p.do("vector", lambda e, o=mo, a=g0, w=rwv: e.tensor_scalar(o, a, w[:, 0:1], None, ALU.mult))
        p.do("vector", lambda e, o=mo, a=g1v, w=rwv: e.scalar_tensor_tensor(o, a, w[:, 1:2], o, ALU.mult, ALU.add), sem="vc")
        p.wait("scalar", "vc")
        gs = Em[:, 0:D]
        p.do("scalar", lambda e, o=gs, i=mo: e.activation(o, i, ACT.Silu), sem="sc")
        p.wait("vector", "sc")
        x1t = A[:, j * D:(j + 1) * D]
        p.do("vector", lambda e, o=x1t, s=Dm[:, 0:D], g=gs: e.tensor_tensor(o, s, g, ALU.mult))
        p.do("vector", lambda e, o=x1t, x=xr: e.tensor_tensor(o, o, x, ALU.add), sem="vc")
        p.wait("scalar", "vc")
        p.dma("scalar", lambda e, s=x1t, j=j: e.dma_start(out=x1sp.ap()[j * 128:(j + 1) * 128, :], in_=s), sem="vld")

    if DBGV:
        p.wait("gpsimd", "vld")
        p.dma("gpsimd", lambda e: e.dma_start(out=dbgX1.ap(), in_=x1sp.ap()), sem="gdma")
        p.wait("gpsimd", "gdma")
    for j in range(TT):
        x1t = A[:, j * D:(j + 1) * D]
        stats_pass(x1t, MU1, RS1, j)
        xnt = Em[:, 1024:1024 + D] if D <= 1024 else Em[:, 0:D]
        p.do("vector", lambda e, x=x1t, o=xnt, j=j: e.tensor_scalar(
            o, x, MU1[:, j:j + 1], RS1[:, j:j + 1], ALU.subtract, ALU.mult), sem="vc")
        split_transpose_to(B, j, xnt)
        for pp in range(3):
            for dc in range(DC):
                lh = (xnh(dc) if pp != 2 else xnl(dc))[:, j * 128:(j + 1) * 128]
                rh = (wgT_h if pp != 1 else wgT_l)[:, dc * E:(dc + 1) * E]
                first = (pp == 0 and dc == 0); last = (pp == 2 and dc == DC - 1)
                p.do("tensor", lambda e, l=lh, r=rh, f=first, s=last: e.matmul(
                    PS[6][:, 0:E], l, r, start=f, stop=s), sem="mm" if last else None)
        p.wait("vector", "mm")
        lgt = Em[:, 0:E]
        p.do("vector", lambda e, o=lgt: e.tensor_copy(o, PS[6][:, 0:E]))
        v0 = Em[:, 16:17]; v1 = Em[:, 17:18]; oh0 = Em[:, 32:32 + E]; oh1 = Em[:, 48:48 + E]; lm = Em[:, 64:64 + E]
        p.do(V, lambda e, l=lgt, o=v0: e.reduce_max(o, l, axis=AX))
        p.do(V, lambda e, l=lgt, m=v0, o=oh0: e.tensor_scalar(o, l, m[:, 0:1], None, ALU.is_equal))
        p.do(V, lambda e, l=lgt, o=lm, h=oh0: e.scalar_tensor_tensor(o, h, -1e30, l, ALU.mult, ALU.add))
        p.do(V, lambda e, l=lm, o=v1: e.reduce_max(o, l, axis=AX))
        p.do(V, lambda e, l=lm, m=v1, o=oh1: e.tensor_scalar(o, l, m[:, 0:1], None, ALU.is_equal))
        p.do(V, lambda e, a=v0, b=v1, o=Em[:, 18:19]: e.tensor_tensor(o, a, b, ALU.subtract), sem="vc")
        p.wait("scalar", "vc")
        p.do("scalar", lambda e, i=Em[:, 18:19], o=Em[:, 19:20]: e.activation(o, i, ACT.Sigmoid), sem="sc")
        p.wait(V, "sc")
        p.do(V, lambda e, w=Em[:, 19:20], o=Em[:, 20:21]: e.tensor_single_scalar(o, w, 1.0, ALU.subtract))
        p.do(V, lambda e, o=Em[:, 20:21]: e.tensor_single_scalar(o, o, -1.0, ALU.mult))
        p.do(V, lambda e, h=oh0, w=Em[:, 19:20], o=CW[:, j * E:(j + 1) * E]: e.tensor_scalar(o, h, w[:, 0:1], None, ALU.mult))
        p.do(V, lambda e, h=oh1, w=Em[:, 20:21], o=CW[:, j * E:(j + 1) * E]: e.scalar_tensor_tensor(
            o, h, w[:, 0:1], o, ALU.mult, ALU.add), sem="vc")

    p.wait("sync", "ccw")
    p.wait("vector", "vld")
    wmm = {}

    wld = {}

    def wet_load(e_, wmm=wmm, wld=wld):
        if e_ >= 1:
            p.wait("sync", "mm", wmm[e_ - 1])
        ld(bc16(C, 0, DC * 2 * D).rearrange("p (a c) -> p a c", a=DC),
           wet_all.ap()[e_ * D:(e_ + 1) * D, :].rearrange("(a p) c -> p a c", p=128), sem="ld2")
        wld[e_] = p.v("ld2")

    p.wait("sync", "mm")
    wet_load(0)
    zvch = []
    for e_ in range(E):
        wet_sb = bc16(C, 0, DC * 2 * D)
        beb = Em2[:, 1024:1024 + D] if D <= 1024 else Em2[:, D:2 * D]
        p.wait("sync", "vc")
        ld(beb, be_d[e_:e_ + 1, :].broadcast_to((128, D)), sem="ld")
        p.wait("tensor", "ld2", wld[e_])
        for j in range(TT):
            for dh2 in range(D // NQ):
                z = (e_ * TT + j) * (D // NQ) + dh2
                if z >= 2:
                    p.wait("tensor", "vc", zvch[z - 2])
                pso = PS[0] if z % 2 == 0 else PS[1]
                for pp in range(3):
                    for dc in range(DC):
                        lh = (xnh(dc) if pp != 2 else xnl(dc))[:, j * 128:(j + 1) * 128]
                        rh = wet_sb.rearrange("p (a c) -> p a c", a=DC)[:, dc, :][:, (0 if pp != 1 else D) + dh2 * NQ:(0 if pp != 1 else D) + (dh2 + 1) * NQ]
                        first = (pp == 0 and dc == 0); last = (pp == 2 and dc == DC - 1)
                        p.do("tensor", lambda e, o=pso, l=lh, r=rh, f=first, s=last: e.matmul(
                            o[:, 0:NQ], l, r, start=f, stop=s), sem="mm" if last else None)
                p.wait("vector", "mm")
                p.wait("vector", "ld")
                ta = Em[:, (z % 2) * NQ:(z % 2) * NQ + NQ]
                p.do("vector", lambda e, o=ta, i=pso, b=beb, dh2=dh2: e.tensor_tensor(
                    o, i[:, 0:NQ], b[:, dh2 * NQ:(dh2 + 1) * NQ], ALU.add))
                accs = A[:, j * D + dh2 * NQ:j * D + (dh2 + 1) * NQ]
                if e_ == 0:
                    p.do("vector", lambda e, o=accs, i=ta, j=j: e.tensor_scalar(
                        o, i, CW[:, j * E:j * E + 1], None, ALU.mult), sem="vc")
                else:
                    p.do("vector", lambda e, o=accs, i=ta, j=j, e_=e_: e.scalar_tensor_tensor(
                        o, i, CW[:, j * E + e_:j * E + e_ + 1], o, ALU.mult, ALU.add), sem="vc")
                zvch.append(p.v("vc"))
        wmm[e_] = p.v("mm")
        if e_ + 1 < E:
            wet_load(e_ + 1)
    p.wait("scalar", "vc")
    for j in range(TT):
        p.do("scalar", lambda e, j=j: e.activation(A[:, j * D:(j + 1) * D], A[:, j * D:(j + 1) * D], ACT.Silu), sem="sc")

    lnw_bc = Dm[:, 0:D]; lnb_bc = Dm[:, D:2 * D]
    p.wait("sync", "mm")
    p.wait("sync", "vc")
    ld(lnw_bc, lnw_d.broadcast_to((128, D)), sem="ld2")
    ld(lnb_bc, lnb_d.broadcast_to((128, D)), sem="ld2")
    p.wait("vector", "ld2")
    for j in range(TT):
        x1t = Em[:, 0:D] if D <= 1024 else Em[:, 0:D]
        p.wait("sync", "vc")
        p.wait("sync", "sc")
        ld(x1t, x1sp.ap()[j * 128:(j + 1) * 128, :], sem="ld")
        p.wait("vector", "ld")
        xn1t = Em[:, 1024:1024 + D] if D <= 1024 else Em[:, D:2 * D]
        p.do("vector", lambda e, x=x1t, o=xn1t, j=j: e.tensor_scalar(
            o, x, MU1[:, j:j + 1], RS1[:, j:j + 1], ALU.subtract, ALU.mult), sem="vc")
        bst = Em2[:, 0:6 * NCH]; bag = Em2[:, 8 * NCH:8 * NCH + 2]
        for ch in range(NCH):
            p.do("vector", lambda e, x=xn1t, ch=ch: e.bn_stats(
                Em2[:, 6 * ch:6 * ch + 6], x[:, 512 * ch:min(512 * (ch + 1), D)]))
        p.do("vector", lambda e, b=bst, a=bag: e.bn_aggr(a, b))
        p.do("vector", lambda e, a=bag, j=j: e.tensor_copy(MU2[:, j:j + 1], a[:, 0:1]))
        p.do("vector", lambda e, a=bag: e.tensor_single_scalar(Em2[:, 8 * NCH + 4:8 * NCH + 5], a[:, 1:2], 1e-5, ALU.add), sem="vc")
        p.wait("scalar", "vc")
        p.do("scalar", lambda e: e.activation(Em2[:, 8 * NCH + 5:8 * NCH + 6], Em2[:, 8 * NCH + 4:8 * NCH + 5], ACT.Sqrt), sem="sc")
        p.wait("vector", "sc")
        p.do("vector", lambda e, j=j: e.reciprocal(RS2[:, j:j + 1], Em2[:, 8 * NCH + 5:8 * NCH + 6]), sem="vc")
        znt = Em2[:, 1024:1024 + D] if D <= 1024 else Em2[:, D:2 * D]
        p.do("vector", lambda e, x=xn1t, o=znt, j=j: e.tensor_scalar(
            o, x, MU2[:, j:j + 1], RS2[:, j:j + 1], ALU.subtract, ALU.mult))
        p.do("vector", lambda e, o=znt: e.tensor_tensor(o, o, lnw_bc, ALU.mult))
        p.do("vector", lambda e, o=znt: e.tensor_tensor(o, o, lnb_bc, ALU.add), sem="vc")
        split_transpose_to(B, j, znt)
    znh = xnh; znl = xnl

    hp2 = 2 * DH
    NPH = NH // 2
    Qh = bc16(Dm, 0, T); Ql = bc16(Dm, T // 2, T)
    Kh = bc16(Dm, T, T); Kl = bc16(Dm, T + T // 2, T)
    VA0c = 0
    VAb = T

    for hp in range(NPH):
        for part in range(3):
            colbase = part * D + hp * hp2
            wful = Em2[:, 0:DC * hp2]
            p.wait("sync", "mm")
            p.wait("sync", "vc")
            blkid = part * (D // hp2) * 0 + part * NPH + hp if False else part * (NH // 2) + hp
            ld(wful, wqkv_d[blkid, :, :, :].rearrange("p a c -> p (a c)"))
            p.wait("scalar", "ld")
            wh = bc16(Em2, DC * hp2 // 2 * 0 + 512, DC * hp2) if False else bc16(Em2, 512, DC * hp2)
            wl = bc16(Em2, 512 + DC * hp2 // 2, DC * hp2)
            split_hilo(wful, wh, wl)
            p.wait("tensor", "vc")
            for ts in range(T // NQ):
                p.wait("tensor", "vc")
                pso = PS[0] if ts % 2 == 0 else PS[1]
                for pp in range(3):
                    for dc in range(DC):
                        lh = (wh if pp < 2 else wl).rearrange("p (a c) -> p a c", a=DC)[:, dc, :]
                        rh = (znh(dc) if pp != 1 else znl(dc))[:, ts * NQ:(ts + 1) * NQ]
                        first = (pp == 0 and dc == 0); last = (pp == 2 and dc == DC - 1)
                        p.do("tensor", lambda e, o=pso, l=lh, r=rh, f=first, s=last: e.matmul(
                            o[0:hp2, 0:NQ], l, r, start=f, stop=s), sem="mm" if last else None)
                p.wait("vector", "mm")
                ev = Em[:, 0:NQ]
                bcol = part * (D // hp2) + hp
                p.do("vector", lambda e, o=ev, i=pso, bcol=bcol: e.tensor_scalar(
                    o[0:hp2, :], i[0:hp2, 0:NQ], bqkv_s[0:hp2, bcol:bcol + 1], None, ALU.add), sem="vc")
                if part < 2:
                    hi_t = bc16(Em, 512, NQ); lo_t = bc16(Em, 512 + NQ // 2, NQ)
                    split_hilo(ev, hi_t, lo_t)
                    dsth = Qh if part == 0 else Kh
                    dstl = Ql if part == 0 else Kl
                    p.do("vector", lambda e, d=dsth, s=hi_t, ts=ts: e.tensor_copy(
                        d[0:hp2, ts * NQ:(ts + 1) * NQ], s[0:hp2, :]))
                    p.do("vector", lambda e, d=dstl, s=lo_t, ts=ts: e.tensor_copy(
                        d[0:hp2, ts * NQ:(ts + 1) * NQ], s[0:hp2, :]), sem="vc")
                    sqb = Em[:, 1024:1024 + NQ]
                    p.do("scalar", lambda e, o=sqb, i=ev: e.activation(o[0:hp2, :], i[0:hp2, :], ACT.Square), sem="sc")
                    p.wait("tensor", "sc")
                    p.do("tensor", lambda e, s=sqb: e.matmul(PS[6][0:2, 0:NQ], op2_s[0:hp2, :], s[0:hp2, :],
                                                             start=True, stop=True), sem="mm")
                    p.wait("vector", "mm")
                    ncol = 70 if part == 0 else 71
                    p.do("vector", lambda e, c=ncol: e.reduce_max(SMALL[0:2, 76 + c - 70:77 + c - 70], PS[6][0:2, 0:NQ], axis=AX))
                    if ts == 0:
                        p.do("vector", lambda e, c=ncol: e.tensor_copy(SMALL[0:2, c:c + 1], SMALL[0:2, 76 + c - 70:77 + c - 70]), sem="vc")
                    else:
                        p.do("vector", lambda e, c=ncol: e.tensor_tensor(SMALL[0:2, c:c + 1], SMALL[0:2, c:c + 1], SMALL[0:2, 76 + c - 70:77 + c - 70], ALU.max), sem="vc")
                else:
                    p.do("vector", lambda e, s=ev, ts=ts: e.tensor_copy(
                        C[0:hp2, ts * NQ:(ts + 1) * NQ], s[0:hp2, :]), sem="vc")
        p.wait("tensor", "vc")
        for tt2 in range(TT):
            p.wait("tensor", "vc")
            p.do("tensor", lambda e, tt2=tt2: e.transpose(PS[4][:, 0:2 * DH], C[0:2 * DH, tt2 * 128:(tt2 + 1) * 128], idf_s[0:hp2, 0:hp2]), sem="mm")
            p.wait("vector", "mm")
            p.do("vector", lambda e, tt2=tt2: e.tensor_copy(C[:, VAb + tt2 * 132:VAb + tt2 * 132 + DH], PS[4][:, 0:DH]))
            p.do("vector", lambda e, tt2=tt2: e.tensor_copy(C[:, VAb + tt2 * 132 + 66:VAb + tt2 * 132 + 66 + DH], PS[4][:, DH:2 * DH]))
            p.do("vector", lambda e, tt2=tt2: e.memset(C[:, VAb + tt2 * 132 + DH:VAb + tt2 * 132 + DH + 1], 1.0))
            p.do("vector", lambda e, tt2=tt2: e.memset(C[:, VAb + tt2 * 132 + 66 + DH:VAb + tt2 * 132 + 66 + DH + 1], 1.0), sem="vc")
        p.wait("vector", "vld")
        p.wait("scalar", "vc")
        p.do("scalar", lambda e: e.activation(SMALL[0:2, 72:73], SMALL[0:2, 70:71], ACT.Sqrt))
        p.do("scalar", lambda e: e.activation(SMALL[0:2, 73:74], SMALL[0:2, 71:72], ACT.Sqrt), sem="sc")
        p.wait("vector", "sc")
        p.do("vector", lambda e: e.tensor_tensor(SMALL[0:2, 74:75], SMALL[0:2, 72:73], SMALL[0:2, 73:74], ALU.mult))
        p.do("vector", lambda e: e.tensor_single_scalar(SMALL[0:2, 75:76], SMALL[0:2, 74:75], -L1SC, ALU.mult), sem="vc")
        p.wait("scalar", "vc")
        p.dma("scalar", lambda e, hp=hp: e.dma_start(out=bcb.ap()[0:1, 8 + 2 * hp:8 + 2 * hp + 2].transpose([1, 0]),
                                                    in_=SMALL[0:2, 75:76]), sem="vld")
        p.wait("sync", "vld")
        ld(negcH[:, 2 * hp:2 * hp + 2], bcb.ap()[0:1, 8 + 2 * hp:8 + 2 * hp + 2].broadcast_to((128, 2)), sem="ld2")

        p.do("vector", lambda e: e.memset(Mm[0:1, 300:300 + DH], 1.0), sem="vc")
        for qq in range(T // NQ):
            for h2 in range(2):
                p.wait("tensor", "vc")
                p.wait("scalar", "ld2", p.v("ld2"))
                po = PS[2] if h2 == 0 else PS[3]
                sc_hist = []

                def emit_pv(kt, hp=hp, h2=h2, po=po):
                    Ehb = Em[:, (kt % 2) * NQ:(kt % 2) * NQ + NQ]
                    p.wait("tensor", "sc", sc_hist[kt])
                    p.do("tensor", lambda e, o=po, kt=kt, r=Ehb: e.matmul(
                        o[0:DH + 1, 0:NQ], C[:, VAb + kt * 132 + 66 * h2:VAb + kt * 132 + 66 * h2 + DH + 1], r,
                        start=(kt == 0), stop=(kt == TT - 1)), sem="mm")

                for kt in range(TT):
                    pss = PS[0] if kt % 2 == 0 else PS[1]
                    if kt >= 2:
                        p.wait("tensor", "sc", sc_hist[kt - 2])
                    for pp in range(3):
                        lh = (Kh if pp < 2 else Kl)[h2 * DH:(h2 + 1) * DH, kt * 128:(kt + 1) * 128]
                        rh = (Qh if pp != 1 else Ql)[h2 * DH:(h2 + 1) * DH, qq * NQ:(qq + 1) * NQ]
                        first = (pp == 0); last = (pp == 2)
                        p.do("tensor", lambda e, o=pss, l=lh, r=rh, f=first, s=last: e.matmul(
                            o[:, 0:NQ], l, r, start=f, stop=s), sem="mm" if last else None)
                    p.wait("scalar", "mm")
                    Ehb = Em[:, (kt % 2) * NQ:(kt % 2) * NQ + NQ]
                    p.do("scalar", lambda e, d=Ehb, i=pss, hp=hp, h2=h2: e.activation(
                        d, i[:, 0:NQ], ACT.Exp, bias=negcH[:, 2 * hp + h2:2 * hp + h2 + 1], scale=L1SC), sem="sc")
                    sc_hist.append(p.v("sc"))
                    if kt >= 1:
                        emit_pv(kt - 1)
                emit_pv(TT - 1)
                p.wait("vector", "mm")
                p.wait("vector", "vld")
                p.do("vector", lambda e, i=po: e.reciprocal(Em2[0:1, 0:NQ], i[DH:DH + 1, 0:NQ]), sem="vc")
                p.wait("tensor", "vc")
                p.do("tensor", lambda e: e.matmul(PS[5][0:DH, 0:NQ], Mm[0:1, 300:300 + DH],
                                                  Em2[0:1, 0:NQ], start=True, stop=True), sem="mm")
                p.wait("vector", "mm")
                p.do("vector", lambda e: e.tensor_copy(Em2[0:DH, 0:NQ], PS[5][0:DH, 0:NQ]))
                mslice = Em[:, 1024:1024 + NQ]
                p.do("vector", lambda e, i=po: e.tensor_tensor(
                    mslice[0:DH, :] if DH <= 128 else None, i[0:DH, 0:NQ], Em2[0:DH, 0:NQ], ALU.mult), sem="vc")
                p.wait("scalar", "vc")
                p.dma("scalar", lambda e, hp=hp, h2=h2, qq=qq: e.dma_start(
                    out=msp.ap()[hp * hp2 + h2 * DH:hp * hp2 + (h2 + 1) * DH, qq * NQ:(qq + 1) * NQ],
                    in_=mslice[0:DH, :]), sem="vld")
                p.wait("vector", "vld")

    wph = bc16(C, 0, D * DC)
    wpl = bc16(C, (D * DC) // 2, D * DC)
    for dc in range(DC):
        blk = Em2[:, 0:D]
        p.wait("sync", "vc")
        p.wait("sync", "sc")
        ld(blk, wproj_d[dc * 128:(dc + 1) * 128, :])
        p.wait("scalar", "ld")
        split_hilo(blk, wph[:, dc * D:(dc + 1) * D], wpl[:, dc * D:(dc + 1) * D])
    bpb = Dm[:, 0:D]
    p.wait("sync", "sc")
    ld(bpb, bproj_d.broadcast_to((128, D)), sem="ld")
    p.wait("tensor", "vc")
    p.wait("vector", "vld")
    for qq2 in range(T // NQ):
        mchq = B[:, 0:DC * NQ]
        p.wait("sync", "mm")
        p.wait("sync", "vc")
        if qq2 == 0:
            p.wait("sync", "vld")
        ld(mchq.rearrange("p (a c) -> p a c", a=DC),
           msp.ap()[:, qq2 * NQ:(qq2 + 1) * NQ].rearrange("(a p) c -> p a c", p=128), sem="ld2")
        p.wait("scalar", "ld2")
        mh = bc16(B, DC * NQ, DC * NQ); ml_ = bc16(B, DC * NQ + DC * NQ // 2, DC * NQ)
        split_hilo(mchq, mh, ml_)
        p.wait("tensor", "vc")
        for qt in [qq2 * (NQ // 128) + i for i in range(NQ // 128)]:
            qo = (qt - qq2 * (NQ // 128)) * 128
            for dh2 in range(D // NQ):
                p.wait("tensor", "vc")
                pso = PS[4] if dh2 % 2 == 0 else PS[5]
                for pp in range(3):
                    for dc in range(DC):
                        lh = (mh if pp < 2 else ml_).rearrange("p (a c) -> p a c", a=DC)[:, dc, :][:, qo:qo + 128]
                        rh = (wph if pp != 1 else wpl)[:, dc * D + dh2 * NQ:dc * D + (dh2 + 1) * NQ]
                        first = (pp == 0 and dc == 0); last = (pp == 2 and dc == DC - 1)
                        p.do("tensor", lambda e, o=pso, l=lh, r=rh, f=first, s=last: e.matmul(
                            o[:, 0:NQ], l, r, start=f, stop=s), sem="mm" if last else None)
            x1t = Em[:, 0:D]
            p.wait("sync", "vc")
            ld(x1t, x1sp.ap()[qt * 128:(qt + 1) * 128, :], sem="ld")
            p.wait("vector", "mm")
            p.wait("vector", "ld")
            xn1t = Em[:, 1024:1024 + D] if D <= 1024 else Em[:, D:2 * D]
            p.do("vector", lambda e, x=x1t, o=xn1t, qt=qt: e.tensor_scalar(
                o, x, MU1[:, qt:qt + 1], RS1[:, qt:qt + 1], ALU.subtract, ALU.mult), sem="vc")
            for dh2 in range(D // NQ):
                pso = PS[4] if dh2 % 2 == 0 else PS[5]
                st_ = Em2[:, 0:NQ]
                p.do("vector", lambda e, o=st_, i=pso, dh2=dh2: e.tensor_tensor(
                    o, i[:, 0:NQ], bpb[:, dh2 * NQ:(dh2 + 1) * NQ], ALU.add))
                p.do("vector", lambda e, o=st_, z=xn1t, dh2=dh2: e.tensor_tensor(
                    o, o, z[:, dh2 * NQ:(dh2 + 1) * NQ], ALU.add))
                p.do("vector", lambda e, o=st_, g=A, qt=qt, dh2=dh2: e.tensor_tensor(
                    o, o, g[:, qt * D + dh2 * NQ:qt * D + (dh2 + 1) * NQ], ALU.mult))
                p.do("vector", lambda e, o=st_, x=x1t, dh2=dh2: e.tensor_tensor(
                    o, o, x[:, dh2 * NQ:(dh2 + 1) * NQ], ALU.add), sem="vc")
                p.wait("scalar", "vc")
                p.dma("scalar", lambda e, s=st_, qt=qt, dh2=dh2: e.dma_start(
                    out=out_d[qt * 128:(qt + 1) * 128, dh2 * NQ:(dh2 + 1) * NQ], in_=s), sem="vld")
                p.wait("vector", "vld")

    p.wait("gpsimd", "vld")
    p.wait("gpsimd", "st")
    p.wait("gpsimd", "gdma")
    p.wait("sync", "vld")

    with nc.Block() as block:
        for name in ("sync", "tensor", "vector", "scalar", "gpsimd"):
            def make_body(nm):
                def body(eng):
                    for fn in p.items[nm]:
                        fn(eng)
                return body
            getattr(block, name)(make_body(name))

    es.close()
    return nc


_NC_CACHE = {}


def _get_nc(key, cfg):
    if key not in _NC_CACHE:
        _NC_CACHE[key] = build_nc(cfg)
    return _NC_CACHE[key]


def make_in_maps(I, cfg):
    T, D, HF, NH, E, NC = cfg["T"], cfg["D"], cfg["HF"], cfg["NH"], cfg["E"], cfg["NC"]
    N = NC * T
    x = np.ascontiguousarray(I["L0_attn_x"], dtype=np.float32)
    xf = np.ascontiguousarray(x.reshape(N, D))
    c = np.ascontiguousarray
    maps = []
    for ci in range(NC):
        m = dict(
            xfull=xf,
            xown=c(x[ci]),
            coff=np.full((128, 1), float(ci * T), dtype=np.float32),
            wq=c(I["L0_attn_Wq"]), wk=c(I["L0_attn_Wk"]), wv=c(I["L0_attn_Wv"]),
            bq=c(I["L0_attn_bq"]), bk=c(I["L0_attn_bk"]), bv=c(I["L0_attn_bv"]),
            bvr=c(I["L0_attn_bv"].reshape(1, D)),
            rw=c(I["L0_ffn_router_w"]), rb=c(I["L0_ffn_router_b"].reshape(E, 1)),
            wx=c(I["L0_ffn_W"][ci].reshape(D // 128, 128, HF // 128, 128).transpose(2, 1, 0, 3)),
            vx=c(I["L0_ffn_V"][ci].reshape(D // 128, 128, HF // 128, 128).transpose(2, 1, 0, 3)),
            wox=c(I["L0_ffn_W_out"][ci]),
            lnw=c(I["L1_attn_ln_w"].reshape(1, D)), lnb=c(I["L1_attn_ln_b"].reshape(1, D)),
            wqkv=c(I["L1_attn_W_qkv"].reshape(D // 128, 128, 3, NH // 2, 2 * D // NH).transpose(2, 3, 1, 0, 4).reshape(3 * NH // 2, 128, D // 128, 2 * D // NH)),
            bqkv=c(I["L1_attn_b_qkv"]),
            wproj=c(I["L1_attn_W_proj"]), bproj=c(I["L1_attn_b_proj"].reshape(1, D)),
            wg=c(I["L1_ffn_Wg"]), weo=c(I["L1_ffn_We"][ci]), be=c(I["L1_ffn_be"]),
        )
        maps.append(m)
    return maps


def kernel(**inputs):
    cfg = FULL_CFG
    nc = _get_nc("full", cfg)
    in_maps = make_in_maps(inputs, cfg)
    res = run_bass_kernel_spmd(nc, in_maps, core_ids=list(range(cfg["NC"])))
    out = np.stack([np.asarray(res.results[ci]["out"]) for ci in range(cfg["NC"])], axis=0)
    return out.astype(np.float32)
